# version 04
# baseline (speedup 1.0000x reference)
import contextlib
import math

import numpy as np
import ml_dtypes

import concourse.bass as bass
import concourse.mybir as mybir
from concourse.bass_utils import run_bass_kernel_spmd

F32 = mybir.dt.float32
BF16 = mybir.dt.bfloat16
AF = mybir.ActivationFunctionType
ALU = mybir.AluOpType
AXX = mybir.AxisListType.X

B, S, D = 4, 4096, 2048
NCH = 16
TCH = 512
NT = S // TCH
G = 2
NGRP = 4
QM = 512
NQM = S // QM
E = 8
OWN = 2048
EBIAS = -12.0
ISQD = 1.0 / math.sqrt(128.0)
ROPE_BASE = 10000.0

BFD = np.dtype(ml_dtypes.bfloat16)


class Ctr:
    def __init__(self, es, nc, name):
        self.h = es.enter_context(nc.semaphore(name))
        self.n = 0

    def inc(self, instr, k=1):
        instr.then_inc(self.h, k)
        self.n += k
        return self.n


class Arena:
    def __init__(self, t, nelem):
        self.t, self.nelem, self.off, self.marks = t, nelem, 0, []

    def push(self):
        self.marks.append(self.off)

    def pop(self):
        self.off = self.marks.pop()

    def bf16(self, n):
        n = (n + 511) & ~511
        ap = self.t[:, self.off:self.off + n]
        self.off += n
        assert self.off <= self.nelem, f"arena overflow {self.off} > {self.nelem}"
        return ap

    def f32(self, n):
        return self.bf16(2 * n).bitcast(F32)


def build_nc():
    nc = bass.Bass("TRN2", target_bir_lowering=False)

    x_fm = nc.dram_tensor("x_fm", [D, S], F32, kind="ExternalInput").ap()
    x_own = nc.dram_tensor("x_own", [D, OWN], F32, kind="ExternalInput").ap()
    wqT = nc.dram_tensor("wqT", [D, 1024], F32, kind="ExternalInput").ap()
    wkT = nc.dram_tensor("wkT", [D, 1024], F32, kind="ExternalInput").ap()
    wvT = nc.dram_tensor("wvT", [D, 1024], F32, kind="ExternalInput").ap()
    woT = nc.dram_tensor("woT", [D, D], F32, kind="ExternalInput").ap()
    rw_d = nc.dram_tensor("rw_d", [D, E], F32, kind="ExternalInput").ap()
    rb_d = nc.dram_tensor("rb_d", [1, E], F32, kind="ExternalInput").ap()
    ew_d = nc.dram_tensor("ew_d", [E, D, D], F32, kind="ExternalInput").ap()
    eb_d = nc.dram_tensor("eb_d", [128, E * NCH], F32, kind="ExternalInput").ap()
    pw_d = nc.dram_tensor("pw_d", [D, D], F32, kind="ExternalInput").ap()
    pb_d = nc.dram_tensor("pb_d", [128, NCH], F32, kind="ExternalInput").ap()
    nw_d = nc.dram_tensor("nw_d", [128, NCH], F32, kind="ExternalInput").ap()
    cos_d = nc.dram_tensor("cos_d", [128, S], BF16, kind="ExternalInput").ap()
    sin_d = nc.dram_tensor("sin_d", [128, S], BF16, kind="ExternalInput").ap()
    msk_d = nc.dram_tensor("msk_d", [128, 2 * 1024], BF16, kind="ExternalInput").ap()
    ident_d = nc.dram_tensor("ident_d", [128, 128], F32, kind="ExternalInput").ap()
    sel_d = nc.dram_tensor("sel_d", [8, E * 128], BF16, kind="ExternalInput").ap()
    out_d = nc.dram_tensor("out_d", [D, OWN], F32, kind="ExternalOutput").ap()

    xn_d = nc.dram_tensor("xn_d", [D, S], BF16).ap()
    xno_d = nc.dram_tensor("xno_d", [D, OWN], BF16).ap()
    rows_d = nc.dram_tensor("rows_d", [2, S], F32).ap()
    rowso_d = nc.dram_tensor("rowso_d", [2, OWN], F32).ap()
    ctxo_d = nc.dram_tensor("ctxo_d", [D, OWN], BF16).ap()
    comb_d = nc.dram_tensor("comb_d", [D, OWN], BF16).ap()
    y1_d = nc.dram_tensor("y1_d", [D, OWN], BF16).ap()
    attn_d = nc.dram_tensor("attn_d", [D, OWN], BF16).ap()
    cin_t = nc.dram_tensor("cin", [8 * 128, S], BF16)
    cout_t = nc.dram_tensor("cout", [16 * 128, S], BF16)
    cin = cin_t.ap()
    cout = cout_t.ap()
    RG = [[0, 1], [2, 3], [4, 5], [6, 7]]

    es = contextlib.ExitStack()
    AR_N = 90 * 1024
    art = es.enter_context(nc.sbuf_tensor("arena", [128, AR_N], BF16))
    ar = Arena(art, AR_N)

    ones_cb = es.enter_context(nc.sbuf_tensor("ones_cb", [128, 1], BF16))
    ones_rf = es.enter_context(nc.sbuf_tensor("ones_rf", [1, 128], F32))
    ident = es.enter_context(nc.sbuf_tensor("ident", [128, 128], F32))
    ebias_c = es.enter_context(nc.sbuf_tensor("ebias_c", [128, 1], F32))
    epsln_c = es.enter_context(nc.sbuf_tensor("epsln_c", [1, 1], F32))
    epsrm_c = es.enter_context(nc.sbuf_tensor("epsrm_c", [1, 1], F32))
    masks = es.enter_context(nc.sbuf_tensor("masks", [128, 2 * 1024], BF16))
    eb_sb = es.enter_context(nc.sbuf_tensor("eb_sb", [128, E * NCH], F32))
    pb_sb = es.enter_context(nc.sbuf_tensor("pb_sb", [128, NCH], F32))
    nw_sb = es.enter_context(nc.sbuf_tensor("nw_sb", [128, NCH], F32))
    rb_sb = es.enter_context(nc.sbuf_tensor("rb_sb", [1, E], F32))
    rw_sb = es.enter_context(nc.sbuf_tensor("rw_sb", [128, NCH * E], BF16))
    wrow_sb = es.enter_context(nc.sbuf_tensor("wrow_sb", [8, OWN], BF16))
    sel_sb = es.enter_context(nc.sbuf_tensor("sel_sb", [8, E * 128], BF16))
    mrow = es.enter_context(nc.sbuf_tensor("mrow", [1, TCH], F32))
    srow = es.enter_context(nc.sbuf_tensor("srow", [1, TCH], F32))
    vrow = es.enter_context(nc.sbuf_tensor("vrow", [1, TCH], F32))
    rrow = es.enter_context(nc.sbuf_tensor("rrow", [1, TCH], F32))
    linv = es.enter_context(nc.sbuf_tensor("linv", [1, QM], F32))
    lbc = es.enter_context(nc.sbuf_tensor("lbc", [128, QM], F32))
    lgt = es.enter_context(nc.sbuf_tensor("lgt", [128, 8], F32))
    rm1 = es.enter_context(nc.sbuf_tensor("rm1", [128, 1], F32))
    rm2 = es.enter_context(nc.sbuf_tensor("rm2", [128, 1], F32))
    req1 = es.enter_context(nc.sbuf_tensor("req1", [128, 8], F32))
    req2 = es.enter_context(nc.sbuf_tensor("req2", [128, 8], F32))
    rmsk = es.enter_context(nc.sbuf_tensor("rmsk", [128, 8], F32))
    rw1 = es.enter_context(nc.sbuf_tensor("rw1", [128, 1], F32))
    rw2 = es.enter_context(nc.sbuf_tensor("rw2", [128, 1], F32))
    rd12 = es.enter_context(nc.sbuf_tensor("rd12", [128, 1], F32))
    rfw = es.enter_context(nc.sbuf_tensor("rfw", [128, 8], F32))
    rfw1 = es.enter_context(nc.sbuf_tensor("rfw1", [128, 8], F32))

    PA = es.enter_context(nc.psum_tensor("PA", [128, 1024], F32))
    PB = es.enter_context(nc.psum_tensor("PB", [128, 1024], F32))
    PC = es.enter_context(nc.psum_tensor("PC", [128, 512], F32))
    PD = es.enter_context(nc.psum_tensor("PD", [128, 512], F32))
    PE_ = es.enter_context(nc.psum_tensor("PE", [128, 512], F32))
    PF = es.enter_context(nc.psum_tensor("PF", [128, 512], F32))

    with es:
        nc.cache_partition_id()
        block = es.enter_context(nc.Block())

        ld = Ctr(es, nc, "ld")
        gd = Ctr(es, nc, "gd")
        tc = Ctr(es, nc, "tc")
        sc = Ctr(es, nc, "sc")
        vc = Ctr(es, nc, "vc")
        cc = Ctr(es, nc, "cc")
        oc = Ctr(es, nc, "oc")

        def toko(eng, key=None):
            return (eng.partition_id() % 2) * OWN

        @block.vector
        def _(v):
            v.memset(ones_cb[:], 1.0)
            v.memset(ones_rf[:], 1.0)
            v.memset(ebias_c[:], EBIAS)
            v.memset(epsln_c[:], 1e-5)
            i = v.memset(epsrm_c[:], 1e-6)
            vc.inc(i)
        v_init = vc.n

        @block.sync
        def _(s):
            ld.inc(s.dma_start(out=masks[:], in_=msk_d[:, :]), 16)
            ld.inc(s.dma_start(out=rb_sb[:], in_=rb_d[:, :]), 16)
            ld.inc(s.dma_start(out=eb_sb[:], in_=eb_d[:, :]), 16)
            ld.inc(s.dma_start(out=ident[:], in_=ident_d[:, :]), 16)
            ld.inc(s.dma_start(out=pb_sb[:], in_=pb_d[:, :]), 16)
            ld.inc(s.dma_start(out=nw_sb[:], in_=nw_d[:, :]), 16)
            ld.inc(s.dma_start(out=sel_sb[:], in_=sel_d[:, :]), 16)
            s.wait_ge(ld.h, ld.n)
        ld_init = ld.n

        @block.gpsimd
        def _(g):
            gd.inc(g.dma_start(out=rw_sb[:].rearrange("p (c e) -> p c e", c=NCH),
                               in_=rw_d.rearrange("(c p) e -> p c e", p=128)), 16)
        gd_init = gd.n

        ar.push()
        cosb = ar.bf16(S)
        sinb = ar.bf16(S)
        ar.push()
        xf32 = [ar.f32(NCH * TCH) for _ in range(2)]
        xbf = [ar.bf16(NCH * TCH) for _ in range(2)]
        xsq = ar.bf16(NCH * TCH)
        xnb = ar.bf16(NCH * TCH)
        mbc = ar.f32(TCH)
        rbc = ar.f32(TCH)

        @block.sync
        def _(s):
            ld.inc(s.dma_start(out=cosb, in_=cos_d[:, :]), 16)
            ld.inc(s.dma_start(out=sinb, in_=sin_d[:, :]), 16)

        ld_x, gd_x, sc_sq, tc_st, vc_rows, sc_rs, tc_bc, vc_cp, vc_xn, ld_sto = \
            {}, {}, {}, {}, {}, {}, {}, {}, {}, {}

        for t in range(NT):
            bi = t % 2
            t0 = t * TCH

            @block.sync
            def _(s, t=t, bi=bi, t0=t0):
                if t >= 2:
                    s.wait_ge(vc.h, vc_xn[t - 2])
                ld.inc(s.dma_start(
                    out=xf32[bi].rearrange("p (c t) -> p c t", c=NCH),
                    in_=x_fm.rearrange("(c p) t -> p c t", p=128)[:, :, t0:t0 + TCH]), 16)
            ld_x[t] = ld.n

            @block.gpsimd
            def _(g, t=t, bi=bi, t0=t0):
                if t >= 2:
                    g.wait_ge(tc.h, tc_st[t - 2])
                gd.inc(g.dma_start(
                    out=xbf[bi].rearrange("p (c t) -> p c t", c=NCH),
                    in_=x_fm.rearrange("(c p) t -> p c t", p=128)[:, :, t0:t0 + TCH]), 16)
            gd_x[t] = gd.n

            @block.scalar
            def _(s, t=t, bi=bi):
                s.wait_ge(gd.h, gd_x[t])
                if t >= 1:
                    s.wait_ge(tc.h, tc_st[t - 1])
                sc.inc(s.activation(out=xsq, in_=xbf[bi], func=AF.Square))
            sc_sq[t] = sc.n

            @block.tensor
            def _(te, t=t, bi=bi):
                te.wait_ge(sc.h, sc_sq[t])
                if t >= 1:
                    te.wait_ge(vc.h, vc_rows[t - 1])
                for c in range(NCH):
                    te.matmul(PE_[0:1, 0:TCH], ones_cb[:, 0:1],
                              xbf[bi][:, c * TCH:(c + 1) * TCH],
                              start=(c == 0), stop=(c == NCH - 1))
                for c in range(NCH):
                    i = te.matmul(PF[0:1, 0:TCH], ones_cb[:, 0:1],
                                  xsq[:, c * TCH:(c + 1) * TCH],
                                  start=(c == 0), stop=(c == NCH - 1))
                tc.inc(i)
            tc_st[t] = tc.n

            @block.vector
            def _(v, t=t):
                v.wait_ge(tc.h, tc_st[t])
                if t >= 1:
                    v.wait_ge(ld.h, ld_sto[t - 1])
                v.tensor_scalar(out=mrow[:], in0=PE_[0:1, 0:TCH], scalar1=1.0 / D,
                                scalar2=None, op0=ALU.mult)
                v.tensor_scalar(out=srow[:], in0=PF[0:1, 0:TCH], scalar1=1.0 / D,
                                scalar2=None, op0=ALU.mult)
                v.drain()
                v.tensor_tensor(out=vrow[:], in0=mrow[:], in1=mrow[:], op=ALU.mult)
                v.drain()
                i = v.tensor_tensor(out=vrow[:], in0=srow[:], in1=vrow[:], op=ALU.subtract)
                vc.inc(i)
            vc_rows[t] = vc.n

            @block.scalar
            def _(s, t=t):
                s.wait_ge(vc.h, vc_rows[t])
                s.activation(out=rrow[:], in_=vrow[:], func=AF.Ln, bias=epsln_c[:])
                s.drain()
                sc.inc(s.activation(out=rrow[:], in_=rrow[:], func=AF.Exp, scale=-0.5))
            sc_rs[t] = sc.n

            @block.tensor
            def _(te, t=t):
                te.wait_ge(sc.h, sc_rs[t])
                if t >= 1:
                    te.wait_ge(vc.h, vc_cp[t - 1])
                te.matmul(PC[:, 0:TCH], ones_rf[0:1, :], mrow[0:1, :], start=True, stop=True)
                i = te.matmul(PD[:, 0:TCH], ones_rf[0:1, :], rrow[0:1, :], start=True, stop=True)
                tc.inc(i)
            tc_bc[t] = tc.n

            @block.vector
            def _(v, t=t, bi=bi):
                v.wait_ge(tc.h, tc_bc[t])
                v.wait_ge(ld.h, ld_x[t])
                if t >= 1:
                    v.wait_ge(ld.h, ld_sto[t - 1])
                v.tensor_copy(out=mbc, in_=PC[:, 0:TCH])
                i = v.tensor_copy(out=rbc, in_=PD[:, 0:TCH])
                vc.inc(i)
                v.drain()
                for c in range(NCH):
                    v.tensor_tensor(out=xf32[bi][:, c * TCH:(c + 1) * TCH],
                                    in0=xf32[bi][:, c * TCH:(c + 1) * TCH], in1=mbc,
                                    op=ALU.subtract)
                v.drain()
                for c in range(NCH):
                    i = v.tensor_tensor(out=xnb[:, c * TCH:(c + 1) * TCH],
                                        in0=xf32[bi][:, c * TCH:(c + 1) * TCH], in1=rbc,
                                        op=ALU.mult)
                vc.inc(i)
            vc_cp[t] = vc.n - 1
            vc_xn[t] = vc.n

            @block.sync
            def _(s, t=t, t0=t0):
                s.wait_ge(vc.h, vc_xn[t])
                ld.inc(s.dma_start(
                    out=xn_d.rearrange("(c p) t -> p c t", p=128)[:, :, t0:t0 + TCH],
                    in_=xnb.rearrange("p (c t) -> p c t", c=NCH)), 16)
                ld.inc(s.dma_start(out=rows_d[0:1, t0:t0 + TCH], in_=mrow[:]), 16)
                ld.inc(s.dma_start(out=rows_d[1:2, t0:t0 + TCH], in_=rrow[:]), 16)
                s.wait_ge(ld.h, ld.n)
            ld_sto[t] = ld.n

        p0_ld = ld.n
        ar.pop()

        ar.push()
        xnp = [ar.bf16(NCH * TCH) for _ in range(2)]
        qraw = ar.bf16(G * S)
        kraw = ar.bf16(G * S)
        qfm = ar.bf16(G * S)
        kfm = ar.bf16(G * S)
        vtm = ar.bf16(G * S)
        xsw = ar.bf16(S)
        t2s = ar.bf16(S)
        wq_sb = ar.bf16(NCH * G * 128)
        wk_sb = ar.bf16(NCH * G * 128)
        wv_sb = ar.bf16(NCH * G * 128)
        pt = [ar.bf16(1024) for _ in range(2)]
        ctxo = [ar.bf16(QM) for _ in range(2)]

        @block.gpsimd
        def _(gg):
            tko = toko(gg)
            gg.wait_ge(ld.h, p0_ld)
            gd.inc(gg.dma_start(out=xno_d[:, :], in_=xn_d[:, bass.ds(tko, OWN)]), 16)
            gd.inc(gg.dma_start(out=rowso_d[:, :], in_=rows_d[:, bass.ds(tko, OWN)]), 16)
            gg.wait_ge(gd.h, gd.n)
        gd_own = gd.n

        gd_w, ld_xn1, tc_proj, vc_evac, vc_rope, ld_sw = {}, {}, {}, {}, {}, {}
        ld_cst = []
        v_ctx_list = []
        grp_end_vc = {}
        fin_idx = 0

        for g in range(NGRP):
            o0 = g * G * 128

            @block.gpsimd
            def _(gg, g=g, o0=o0):
                if g >= 1:
                    gg.wait_ge(tc.h, tc_proj[(g - 1, NT - 1)])
                for w_sb, w_dd in ((wq_sb, wqT), (wk_sb, wkT), (wv_sb, wvT)):
                    gd.inc(gg.dma_start(
                        out=w_sb.rearrange("p (c o) -> p c o", c=NCH),
                        in_=w_dd.rearrange("(c p) o -> p c o", p=128)[:, :, o0:o0 + G * 128]), 16)
            gd_w[g] = gd.n

            for t in range(NT):
                bi = t % 2
                t0 = t * TCH

                @block.sync
                def _(s, g=g, t=t, bi=bi, t0=t0):
                    if g == 0 and t == 0:
                        s.wait_ge(ld.h, p0_ld)
                    gi = g * NT + t
                    if gi >= 2:
                        pg, pt_ = divmod(gi - 2, NT)
                        s.wait_ge(tc.h, tc_proj[(pg, pt_)])
                    ld.inc(s.dma_start(
                        out=xnp[bi].rearrange("p (c t) -> p c t", c=NCH),
                        in_=xn_d.rearrange("(c p) t -> p c t", p=128)[:, :, t0:t0 + TCH]), 16)
                ld_xn1[(g, t)] = ld.n

                @block.tensor
                def _(te, g=g, t=t, bi=bi):
                    te.wait_ge(ld.h, ld_xn1[(g, t)])
                    te.wait_ge(gd.h, gd_w[g])
                    if t > 0:
                        te.wait_ge(vc.h, vc_evac[(g, t - 1)])
                    elif g > 0:
                        te.wait_ge(vc.h, grp_end_vc[g - 1])
                    for oc in range(G):
                        for c in range(NCH):
                            te.matmul(PA[:, oc * 512:oc * 512 + TCH],
                                      wq_sb[:, c * G * 128 + oc * 128:c * G * 128 + (oc + 1) * 128],
                                      xnp[bi][:, c * TCH:(c + 1) * TCH],
                                      start=(c == 0), stop=(c == NCH - 1))
                    for oc in range(G):
                        for c in range(NCH):
                            te.matmul(PB[:, oc * 512:oc * 512 + TCH],
                                      wk_sb[:, c * G * 128 + oc * 128:c * G * 128 + (oc + 1) * 128],
                                      xnp[bi][:, c * TCH:(c + 1) * TCH],
                                      start=(c == 0), stop=(c == NCH - 1))
                    for ts in range(4):
                        pp = (PC, PD, PE_, PF)[ts]
                        for c in range(NCH):
                            i = te.matmul(pp[:, 0:G * 128],
                                          xnp[bi][:, c * TCH + ts * 128:c * TCH + (ts + 1) * 128],
                                          wv_sb[:, c * G * 128:(c + 1) * G * 128],
                                          start=(c == 0), stop=(c == NCH - 1))
                    tc.inc(i)
                tc_proj[(g, t)] = tc.n

                @block.vector
                def _(v, g=g, t=t, t0=t0):
                    v.wait_ge(tc.h, tc_proj[(g, t)])
                    for ocp in range(G):
                        v.tensor_copy(out=qraw[:, ocp * S + t0:ocp * S + t0 + TCH],
                                      in_=PA[:, ocp * 512:ocp * 512 + TCH])
                        v.tensor_copy(out=kraw[:, ocp * S + t0:ocp * S + t0 + TCH],
                                      in_=PB[:, ocp * 512:ocp * 512 + TCH])
                    for ts in range(4):
                        pp = (PC, PD, PE_, PF)[ts]
                        cg = t * 4 + ts
                        for h in range(G):
                            i = v.tensor_copy(
                                out=vtm[:, h * S + cg * 128:h * S + (cg + 1) * 128],
                                in_=pp[:, h * 128:(h + 1) * 128])
                    vc.inc(i)
                vc_evac[(g, t)] = vc.n

            for pi, (raw, fm) in enumerate(((qraw, qfm), (kraw, kfm))):
                for h in range(G):
                    @block.sync
                    def _(s, raw=raw, h=h, g=g, pi=pi):
                        s.wait_ge(vc.h, vc_evac[(g, NT - 1)])
                        if (g, pi, h) != (0, 0, 0):
                            s.wait_ge(vc.h, vc.n)
                        ld.inc(s.dma_start(out=xsw[0:64, :],
                                           in_=raw[64:128, h * S:(h + 1) * S]), 16)
                        ld.inc(s.dma_start(out=xsw[64:128, :],
                                           in_=raw[0:64, h * S:(h + 1) * S]), 16)
                        s.wait_ge(ld.h, ld.n)
                    ld_sw[(g, pi, h)] = ld.n

                    @block.vector
                    def _(v, raw=raw, fm=fm, h=h, g=g, pi=pi):
                        v.wait_ge(ld.h, ld_sw[(g, pi, h)])
                        v.tensor_tensor(out=xsw, in0=xsw, in1=sinb, op=ALU.mult)
                        v.tensor_tensor(out=t2s, in0=raw[:, h * S:(h + 1) * S], in1=cosb,
                                        op=ALU.mult)
                        v.drain()
                        i = v.tensor_tensor(out=fm[:, h * S:(h + 1) * S], in0=xsw, in1=t2s,
                                            op=ALU.add)
                        vc.inc(i)
                    vc_rope[(g, pi, h)] = vc.n

            for hg in range(G):
                for qm in range(NQM):
                    q0 = qm * QM
                    nsup = 2 * (qm + 1)
                    diag0 = 2 * qm

                    sup_sc, sup_pe, sup_pm, sup_pv = {}, {}, {}, {}
                    tq, sq_, vq = tc.n, sc.n, vc.n
                    for ks in range(nsup):
                        tq += 1
                        sup_sc[ks] = tq
                        sq_ += 1
                        sup_pe[ks] = sq_
                        if ks >= diag0:
                            vq += 1
                            sup_pm[ks] = vq
                        tq += 1
                        sup_pv[ks] = tq

                    @block.tensor
                    def _(te, g=g, hg=hg, qm=qm, nsup=nsup, q0=q0, diag0=diag0,
                          sup_pe=sup_pe, sup_pm=sup_pm, fin_idx=fin_idx):
                        te.wait_ge(vc.h, vc_rope[(g, 0, hg)])
                        te.wait_ge(vc.h, vc_rope[(g, 1, hg)])
                        if fin_idx > 0:
                            te.wait_ge(vc.h, v_ctx_list[fin_idx - 1])
                        for ks in range(nsup):
                            b = ks % 2
                            sst = (PA, PB)[b]
                            if ks >= 2:
                                te.wait_ge(sc.h, sup_pe[ks - 2])
                            for c2 in range(2):
                                kc = ks * 2 + c2
                                i = te.matmul(sst[:, c2 * 512:c2 * 512 + QM],
                                              kfm[:, hg * S + kc * 128:hg * S + (kc + 1) * 128],
                                              qfm[:, hg * S + q0:hg * S + q0 + QM],
                                              start=True, stop=True)
                            tc.inc(i)
                            if ks >= diag0:
                                te.wait_ge(vc.h, sup_pm[ks])
                            else:
                                te.wait_ge(sc.h, sup_pe[ks])
                            for c2 in range(2):
                                kc = ks * 2 + c2
                                te.matmul(PC[:, 0:QM],
                                          vtm[:, hg * S + kc * 128:hg * S + (kc + 1) * 128],
                                          pt[b][:, c2 * 512:c2 * 512 + QM],
                                          start=(ks == 0 and c2 == 0),
                                          stop=(ks == nsup - 1 and c2 == 1))
                                i = te.matmul(PD[0:1, 0:QM], ones_cb[:, 0:1],
                                              pt[b][:, c2 * 512:c2 * 512 + QM],
                                              start=(ks == 0 and c2 == 0),
                                              stop=(ks == nsup - 1 and c2 == 1))
                            tc.inc(i)
                    assert tc.n == sup_pv[nsup - 1], (tc.n, sup_pv[nsup - 1])

                    @block.scalar
                    def _(s, nsup=nsup, sup_sc=sup_sc, sup_pv=sup_pv):
                        for ks in range(nsup):
                            b = ks % 2
                            sst = (PA, PB)[b]
                            s.wait_ge(tc.h, sup_sc[ks])
                            if ks >= 2:
                                s.wait_ge(tc.h, sup_pv[ks - 2])
                            sc.inc(s.activation(out=pt[b][:], in_=sst[:, 0:1024],
                                                func=AF.Exp, scale=ISQD, bias=ebias_c[:]))
                    assert sc.n == sup_pe[nsup - 1]

                    if nsup > diag0:
                        @block.vector
                        def _(v, nsup=nsup, diag0=diag0, sup_pe=sup_pe):
                            for ks in range(diag0, nsup):
                                b = ks % 2
                                js = ks - diag0
                                v.wait_ge(sc.h, sup_pe[ks])
                                vc.inc(v.tensor_tensor(
                                    out=pt[b][:], in0=pt[b][:],
                                    in1=masks[:, js * 1024:(js + 1) * 1024], op=ALU.mult))
                        assert vc.n == sup_pm[nsup - 1]

                    @block.vector
                    def _(v, sup_pv=sup_pv, nsup=nsup):
                        v.wait_ge(tc.h, sup_pv[nsup - 1])
                        vc.inc(v.reciprocal(out=linv[:], in_=PD[0:1, 0:QM]))
                    v_linv = vc.n

                    @block.tensor
                    def _(te, v_linv=v_linv):
                        te.wait_ge(vc.h, v_linv)
                        tc.inc(te.matmul(PE_[:, 0:QM], ones_rf[0:1, :], linv[0:1, :],
                                         start=True, stop=True))
                    t_lb = tc.n

                    @block.vector
                    def _(v, t_lb=t_lb, fin_idx=fin_idx):
                        bo = fin_idx % 2
                        v.wait_ge(tc.h, t_lb)
                        if fin_idx >= 2:
                            v.wait_ge(ld.h, ld_cst[fin_idx - 2])
                        v.tensor_copy(out=lbc[:], in_=PE_[:, 0:QM])
                        v.drain()
                        vc.inc(v.tensor_tensor(out=ctxo[bo][:], in0=PC[:, 0:QM],
                                               in1=lbc[:], op=ALU.mult))
                    v_ctx_list.append(vc.n)

                    @block.sync
                    def _(s, g=g, hg=hg, qm=qm, q0=q0, fin_idx=fin_idx):
                        bo = fin_idx % 2
                        s.wait_ge(vc.h, v_ctx_list[fin_idx])
                        hidx = g * G + hg
                        ld.inc(s.dma_start(out=cin[hidx * 128:(hidx + 1) * 128, q0:q0 + QM],
                                           in_=ctxo[bo][:]), 16)
                    ld_cst.append(ld.n)
                    fin_idx += 1

            grp_end_vc[g] = vc.n

        p1_ld = ld.n
        p1_tc = tc.n
        p1_vc = vc.n
        ar.pop()
        ar.pop()

        ar.push()
        xnp3 = [ar.bf16(NCH * TCH) for _ in range(2)]
        comb = ar.bf16(NCH * OWN)
        wep = [ar.bf16(NCH * 1024) for _ in range(2)]
        silu_t = [ar.bf16(TCH) for _ in range(4)]
        tmp_t = [ar.bf16(TCH) for _ in range(4)]


        ld_xn3, tc_lg, tc_tr, vc_fw = {}, {}, {}, {}
        vc_r1, sc_sg = {}, {}
        for t3 in range(4):
            bi = t3 % 2

            @block.sync
            def _(s, t3=t3, bi=bi):
                s.wait_ge(ld.h, p1_ld)
                s.wait_ge(vc.h, p1_vc)
                s.wait_ge(tc.h, p1_tc)
                s.wait_ge(gd.h, gd_own)
                if t3 == 0:
                    s.wait_ge(cc.h, 1)
                if t3 >= 2:
                    s.wait_ge(tc.h, tc_lg[(t3 - 2, 3)])
                ld.inc(s.dma_start(
                    out=xnp3[bi].rearrange("p (c t) -> p c t", c=NCH),
                    in_=xno_d.rearrange("(c p) t -> p c t", p=128)[:, :, t3 * TCH:(t3 + 1) * TCH]), 16)
            ld_xn3[t3] = ld.n

            for ts in range(4):
                ti = t3 * 4 + ts

                @block.tensor
                def _(te, t3=t3, bi=bi, ts=ts, ti=ti):
                    te.wait_ge(ld.h, ld_xn3[t3])
                    te.wait_ge(gd.h, gd_init)
                    if ti == 0:
                        te.wait_ge(ld.h, ld_init)
                    else:
                        te.wait_ge(vc.h, vc_r1[ti - 1])
                    if ti >= 1:
                        te.wait_ge(vc.h, vc_fw[ti - 1])
                    for c in range(NCH):
                        te.matmul(PE_[:, 0:8],
                                  xnp3[bi][:, c * TCH + ts * 128:c * TCH + (ts + 1) * 128],
                                  rw_sb[:, c * E:(c + 1) * E],
                                  start=(c == 0), stop=False)
                    tc.inc(te.matmul(PE_[:, 0:8], ones_rf[0:1, 0:128], rb_sb[0:1, :],
                                     start=False, stop=True))
                tc_lg[(t3, ts)] = tc.n

                @block.vector
                def _(v, t3=t3, ts=ts, ti=ti):
                    v.wait_ge(tc.h, tc_lg[(t3, ts)])
                    v.tensor_copy(out=lgt[:], in_=PE_[:, 0:8])
                    v.drain()
                    v.reduce_max(out=rm1[:], in_=lgt[:], axis=AXX)
                    v.drain()
                    v.tensor_scalar(out=req1[:], in0=lgt[:], scalar1=rm1[:], scalar2=None,
                                    op0=ALU.is_equal)
                    v.drain()
                    v.scalar_tensor_tensor(out=rmsk[:], in0=req1[:], scalar=-1e30, in1=lgt[:],
                                           op0=ALU.mult, op1=ALU.add)
                    v.drain()
                    v.reduce_max(out=rm2[:], in_=rmsk[:], axis=AXX)
                    v.drain()
                    v.tensor_tensor(out=rd12[:], in0=rm1[:], in1=rm2[:], op=ALU.subtract)
                    vc.inc(v.tensor_scalar(out=req2[:], in0=lgt[:], scalar1=rm2[:],
                                           scalar2=None, op0=ALU.is_equal))
                vc_r1[ti] = vc.n

                @block.scalar
                def _(s, ti=ti):
                    s.wait_ge(vc.h, vc_r1[ti])
                    sc.inc(s.activation(out=rw1[:], in_=rd12[:], func=AF.Sigmoid))
                sc_sg[ti] = sc.n

                @block.vector
                def _(v, ti=ti):
                    v.wait_ge(sc.h, sc_sg[ti])
                    v.tensor_scalar(out=rw2[:], in0=rw1[:], scalar1=-1.0, scalar2=1.0,
                                    op0=ALU.mult, op1=ALU.add)
                    v.tensor_scalar(out=rfw1[:], in0=req1[:], scalar1=rw1[:], scalar2=None,
                                    op0=ALU.mult)
                    v.drain()
                    vc.inc(v.scalar_tensor_tensor(out=rfw[:], in0=req2[:], scalar=rw2[:],
                                                  in1=rfw1[:], op0=ALU.mult, op1=ALU.add))
                v_fw2 = vc.n

                @block.tensor
                def _(te, v_fw2=v_fw2):
                    te.wait_ge(vc.h, v_fw2)
                    tc.inc(te.transpose(PF[0:8, 0:128], rfw[:, 0:8], ident[:]))
                tc_tr[ti] = tc.n

                @block.vector
                def _(v, ti=ti):
                    v.wait_ge(tc.h, tc_tr[ti])
                    vc.inc(v.tensor_copy(out=wrow_sb[:, ti * 128:(ti + 1) * 128],
                                         in_=PF[0:8, 0:128]))
                vc_fw[ti] = vc.n

        NEP = 2
        gd_wep = {}
        tc_hh, sc_hh, vc_cmb, vc_cmb_idx = {}, {}, {}, {}

        @block.gpsimd
        def _(gg):
            gg.wait_ge(tc.h, p1_tc)
            gg.wait_ge(vc.h, p1_vc)
            gd.inc(gg.dma_start(
                out=wep[0].rearrange("p (c f) -> p c f", c=NCH),
                in_=ew_d[0].rearrange("(c p) f -> p c f", p=128)[:, :, 0:1024]), 16)
        gd_wep[(0, 0)] = gd.n

        @block.gpsimd
        def _(gg):
            tko = toko(gg)
            gg.wait_ge(ld.h, p1_ld)
            gg.wait_ge(tc.h, p1_tc)
            gg.wait_ge(vc.h, p1_vc)
            gg.wait_ge(gd.h, gd_wep[(0, 0)])
            cc.inc(gg.collective_compute("AllGather", ALU.bypass, replica_groups=RG,
                                         ins=[cin_t.ap().opt()], outs=[cout_t.ap().opt()]))
            gg.wait_ge(cc.h, cc.n)
            gd.inc(gg.dma_start(out=ctxo_d[:, :], in_=cout[:, bass.ds(tko, OWN)]), 16)
            gg.wait_ge(gd.h, gd.n)
        gd_ctxo = gd.n

        tc_bc3 = {}
        for e in range(E):
            @block.tensor
            def _(te, e=e):
                for t3 in range(4):
                    slot = (PC, PD, PE_, PF)[t3]
                    if e == 0:
                        te.wait_ge(vc.h, vc_fw[15])
                    else:
                        te.wait_ge(vc.h, vc_cmb[(e - 1, NEP - 1, t3, 7)])
                    tc.inc(te.matmul(slot[:, 0:TCH], sel_sb[0:8, e * 128:(e + 1) * 128],
                                     wrow_sb[0:8, t3 * TCH:(t3 + 1) * TCH],
                                     start=True, stop=True))
                    tc_bc3[(e, t3)] = tc.n
            for p in range(NEP):
                ep = e * NEP + p
                if ep > 0:
                    @block.gpsimd
                    def _(gg, e=e, p=p, ep=ep):
                        if ep >= 2:
                            e2, p2 = divmod(ep - 2, NEP)
                            gg.wait_ge(tc.h, tc_hh[(e2, p2, 3, 7)])
                        gd.inc(gg.dma_start(
                            out=wep[ep % 2].rearrange("p (c f) -> p c f", c=NCH),
                            in_=ew_d[e].rearrange("(c p) f -> p c f", p=128)[:, :, p * 1024:(p + 1) * 1024]), 16)
                    gd_wep[(e, p)] = gd.n
                else:
                    pass

                for t3 in range(4):
                    bi3 = (ep * 4 + t3) % 2

                    @block.sync
                    def _(s, e=e, p=p, t3=t3, bi3=bi3, ep=ep):
                        gi = ep * 4 + t3
                        if gi == 0:
                            s.wait_ge(tc.h, tc_lg[(3, 3)])
                        if gi >= 2:
                            g2 = gi - 2
                            e2, r2 = divmod(g2, NEP * 4)
                            p2, t2 = divmod(r2, 4)
                            s.wait_ge(tc.h, tc_hh[(e2, p2, t2, 7)])
                        ld.inc(s.dma_start(
                            out=xnp3[bi3].rearrange("p (c t) -> p c t", c=NCH),
                            in_=xno_d.rearrange("(c p) t -> p c t", p=128)[:, :, t3 * TCH:(t3 + 1) * TCH]), 16)
                    ld_xn3[(e, p, t3)] = ld.n

                    for fj in range(8):
                        idx = (ep * 4 + t3) * 8 + fj

                        @block.tensor
                        def _(te, e=e, p=p, t3=t3, fj=fj, bi3=bi3, idx=idx):
                            te.wait_ge(ld.h, ld_xn3[(e, p, t3)])
                            te.wait_ge(gd.h, gd_wep[(e, p)])
                            if idx >= 4:
                                te.wait_ge(sc.h, sc_hh[idx - 4])
                            pp = (PA[:, 0:512], PA[:, 512:1024],
                                  PB[:, 0:512], PB[:, 512:1024])[fj % 4]
                            for c in range(NCH):
                                i = te.matmul(pp,
                                              wep[p % 2][:, c * 1024 + fj * 128:c * 1024 + (fj + 1) * 128],
                                              xnp3[bi3][:, c * TCH:(c + 1) * TCH],
                                              start=(c == 0), stop=(c == NCH - 1))
                            tc.inc(i)
                        tc_hh[(e, p, t3, fj)] = tc.n

                        @block.scalar
                        def _(s, e=e, p=p, t3=t3, fj=fj, idx=idx):
                            s.wait_ge(tc.h, tc_hh[(e, p, t3, fj)])
                            if idx >= 4:
                                s.wait_ge(vc.h, vc_cmb_idx[idx - 4])
                            pp = (PA[:, 0:512], PA[:, 512:1024],
                                  PB[:, 0:512], PB[:, 512:1024])[fj % 4]
                            cf = p * 8 + fj
                            sc.inc(s.activation(out=silu_t[fj % 4][:], in_=pp, func=AF.Silu,
                                                bias=eb_sb[:, e * NCH + cf:e * NCH + cf + 1]))
                        sc_hh[idx] = sc.n

                        if fj % 4 == 3:
                            @block.vector
                            def _(v, e=e, p=p, t3=t3, fj=fj, idx=idx):
                                v.wait_ge(sc.h, sc_hh[idx])
                                v.wait_ge(tc.h, tc_bc3[(e, t3)])
                                wslice = (PC, PD, PE_, PF)[t3][:, 0:TCH]
                                for q in range(4):
                                    fjq = fj - 3 + q
                                    cf = p * 8 + fjq
                                    dst = comb[:, cf * OWN + t3 * TCH:cf * OWN + (t3 + 1) * TCH]
                                    if e == 0:
                                        i = v.tensor_tensor(out=dst, in0=silu_t[q][:],
                                                            in1=wslice, op=ALU.mult)
                                    else:
                                        v.tensor_tensor(out=tmp_t[q][:], in0=silu_t[q][:],
                                                        in1=wslice, op=ALU.mult)
                                if e > 0:
                                    v.drain()
                                    for q in range(4):
                                        fjq = fj - 3 + q
                                        cf = p * 8 + fjq
                                        dst = comb[:, cf * OWN + t3 * TCH:cf * OWN + (t3 + 1) * TCH]
                                        i = v.tensor_tensor(out=dst, in0=dst, in1=tmp_t[q][:],
                                                            op=ALU.add)
                                vc.inc(i)
                            for q in range(4):
                                vc_cmb_idx[idx - 3 + q] = vc.n
                            vc_cmb[(e, p, t3, fj)] = vc.n

        @block.sync
        def _(s):
            s.wait_ge(vc.h, vc_cmb[(E - 1, NEP - 1, 3, 7)])
            for q in range(4):
                ld.inc(s.dma_start(
                    out=comb_d.rearrange("(c p) t -> p c t", p=128)[:, :, q * TCH:(q + 1) * TCH],
                    in_=comb.rearrange("p (c t) -> p c t", c=NCH)[:, :, q * TCH:(q + 1) * TCH]), 16)
        p3_ld = ld.n
        p3_tc = tc.n
        p3_vc = vc.n
        p3_sc = sc.n
        ar.pop()

        ar.push()
        T4 = 256
        wob = [ar.bf16(NCH * 512) for _ in range(2)]
        pwb = [ar.bf16(NCH * 512) for _ in range(2)]
        cmb4 = [ar.bf16(NCH * T4) for _ in range(2)]
        ctx4 = [ar.bf16(NCH * T4) for _ in range(2)]
        y1s = [ar.bf16(4 * T4) for _ in range(2)]
        ats = [ar.bf16(4 * T4) for _ in range(2)]

        gd_pan, ld_c4, sc_y1, vc_at, ld_s4, tc_w4 = {}, {}, {}, {}, {}, {}

        for op in range(4):
            @block.gpsimd
            def _(gg, op=op):
                if op == 0:
                    gg.wait_ge(tc.h, p3_tc)
                    gg.wait_ge(sc.h, p3_sc)
                    gg.wait_ge(vc.h, p3_vc)
                    gg.wait_ge(ld.h, p3_ld)
                if op >= 2:
                    gg.wait_ge(tc.h, tc_w4[(op - 2, 7, 3)])
                gd.inc(gg.dma_start(
                    out=wob[op % 2].rearrange("p (c o) -> p c o", c=NCH),
                    in_=woT.rearrange("(c p) o -> p c o", p=128)[:, :, op * 512:(op + 1) * 512]), 16)
                gd.inc(gg.dma_start(
                    out=pwb[op % 2].rearrange("p (c o) -> p c o", c=NCH),
                    in_=pw_d.rearrange("(c p) o -> p c o", p=128)[:, :, op * 512:(op + 1) * 512]), 16)
            gd_pan[op] = gd.n

            for tq in range(8):
                bi4 = tq % 2
                gi4 = op * 8 + tq

                @block.sync
                def _(s, op=op, tq=tq, bi4=bi4, gi4=gi4):
                    if op == 0 and tq == 0:
                        s.wait_ge(ld.h, p3_ld)
                        s.wait_ge(gd.h, gd_ctxo)
                    if gi4 >= 2:
                        o2, t2 = divmod(gi4 - 2, 8)
                        s.wait_ge(tc.h, tc_w4[(o2, t2, 3)])
                    ld.inc(s.dma_start(
                        out=cmb4[bi4].rearrange("p (c t) -> p c t", c=NCH),
                        in_=comb_d.rearrange("(c p) t -> p c t", p=128)[:, :, tq * T4:(tq + 1) * T4]), 16)
                    ld.inc(s.dma_start(
                        out=ctx4[bi4].rearrange("p (c t) -> p c t", c=NCH),
                        in_=ctxo_d.rearrange("(c p) t -> p c t", p=128)[:, :, tq * T4:(tq + 1) * T4]), 16)
                ld_c4[(op, tq)] = ld.n

                for ocs in range(4):
                    @block.tensor
                    def _(te, op=op, tq=tq, ocs=ocs, bi4=bi4, gi4=gi4):
                        te.wait_ge(ld.h, ld_c4[(op, tq)])
                        te.wait_ge(gd.h, gd_pan[op])
                        idx4 = gi4 * 4 + ocs
                        if idx4 >= 2:
                            te.wait_ge(sc.h, sc_y1[idx4 - 2])
                            te.wait_ge(vc.h, vc_at[idx4 - 2])
                        pj = (PC, PE_)[ocs % 2]
                        pw_ = (PD, PF)[ocs % 2]
                        for c in range(NCH):
                            te.matmul(pj[:, 0:T4],
                                      pwb[op % 2][:, c * 512 + ocs * 128:c * 512 + (ocs + 1) * 128],
                                      cmb4[bi4][:, c * T4:(c + 1) * T4],
                                      start=(c == 0), stop=(c == NCH - 1))
                        for c in range(NCH):
                            i = te.matmul(pw_[:, 0:T4],
                                          wob[op % 2][:, c * 512 + ocs * 128:c * 512 + (ocs + 1) * 128],
                                          ctx4[bi4][:, c * T4:(c + 1) * T4],
                                          start=(c == 0), stop=(c == NCH - 1))
                        tc.inc(i)
                    tc_w4[(op, tq, ocs)] = tc.n

                    @block.scalar
                    def _(s, op=op, tq=tq, ocs=ocs, bi4=bi4):
                        s.wait_ge(tc.h, tc_w4[(op, tq, ocs)])
                        gi4 = op * 8 + tq
                        idx4 = gi4 * 4 + ocs
                        if idx4 >= 8:
                            s.wait_ge(ld.h, ld_s4[idx4 // 4 - 2])
                        pj = (PC, PE_)[ocs % 2]
                        oc_g = op * 4 + ocs
                        sc.inc(s.add(out=y1s[bi4][:, ocs * T4:(ocs + 1) * T4], in_=pj[:, 0:T4],
                                     add=pb_sb[:, oc_g:oc_g + 1]))
                    sc_y1[(op * 8 + tq) * 4 + ocs] = sc.n

                    @block.vector
                    def _(v, op=op, tq=tq, ocs=ocs, bi4=bi4):
                        v.wait_ge(tc.h, tc_w4[(op, tq, ocs)])
                        idx4 = (op * 8 + tq) * 4 + ocs
                        if idx4 >= 8:
                            v.wait_ge(ld.h, ld_s4[idx4 // 4 - 2])
                        pw_ = (PD, PF)[ocs % 2]
                        vc.inc(v.tensor_copy(out=ats[bi4][:, ocs * T4:(ocs + 1) * T4],
                                             in_=pw_[:, 0:T4]))
                    vc_at[(op * 8 + tq) * 4 + ocs] = vc.n

                @block.sync
                def _(s, op=op, tq=tq, bi4=bi4, gi4=gi4):
                    s.wait_ge(sc.h, sc_y1[gi4 * 4 + 3])
                    s.wait_ge(vc.h, vc_at[gi4 * 4 + 3])
                    ld.inc(s.dma_start(
                        out=y1_d.rearrange("(o p) t -> p o t", p=128)[:, op * 4:(op + 1) * 4, tq * T4:(tq + 1) * T4],
                        in_=y1s[bi4].rearrange("p (o t) -> p o t", o=4)), 16)
                    ld.inc(s.dma_start(
                        out=attn_d.rearrange("(o p) t -> p o t", p=128)[:, op * 4:(op + 1) * 4, tq * T4:(tq + 1) * T4],
                        in_=ats[bi4].rearrange("p (o t) -> p o t", o=4)), 16)
                ld_s4[gi4] = ld.n

        p4a_ld = ld.n
        ar.pop()

        ar.push()
        xo4 = ar.f32(NCH * TCH)
        y14 = ar.bf16(NCH * TCH)
        at4 = ar.bf16(NCH * TCH)
        yv = ar.f32(NCH * TCH)
        fin = ar.f32(NCH * TCH)
        sq4 = ar.bf16(TCH)
        m4 = ar.f32(TCH)
        r4 = ar.f32(TCH)
        mr4 = ar.f32(TCH)[0:1, :]
        rr4 = ar.f32(TCH)[0:1, :]
        ss4 = ar.f32(TCH)[0:1, :]
        rb4 = ar.f32(TCH)

        ld_b4, tc_b4, vc_b4, sc_b4, tc_ss, sc_rr, tc_rb, vc_fin = {}, {}, {}, {}, {}, {}, {}, {}

        for tq in range(4):
            @block.sync
            def _(s, tq=tq):
                if tq == 0:
                    s.wait_ge(ld.h, p4a_ld)
                if tq >= 1:
                    s.wait_ge(oc.h, 16 * tq)
                ld.inc(s.dma_start(
                    out=xo4.rearrange("p (c t) -> p c t", c=NCH),
                    in_=x_own.rearrange("(c p) t -> p c t", p=128)[:, :, tq * TCH:(tq + 1) * TCH]), 16)
                ld.inc(s.dma_start(
                    out=y14.rearrange("p (c t) -> p c t", c=NCH),
                    in_=y1_d.rearrange("(c p) t -> p c t", p=128)[:, :, tq * TCH:(tq + 1) * TCH]), 16)
                ld.inc(s.dma_start(
                    out=at4.rearrange("p (c t) -> p c t", c=NCH),
                    in_=attn_d.rearrange("(c p) t -> p c t", p=128)[:, :, tq * TCH:(tq + 1) * TCH]), 16)
                ld.inc(s.dma_start(out=mr4[:], in_=rowso_d[0:1, tq * TCH:(tq + 1) * TCH]), 16)
                ld.inc(s.dma_start(out=rr4[:], in_=rowso_d[1:2, tq * TCH:(tq + 1) * TCH]), 16)
                s.wait_ge(ld.h, ld.n)
            ld_b4[tq] = ld.n

            @block.tensor
            def _(te, tq=tq):
                te.wait_ge(ld.h, ld_b4[tq])
                if tq >= 1:
                    te.wait_ge(vc.h, vc_b4[tq - 1])
                te.matmul(PC[:, 0:TCH], ones_rf[0:1, :], mr4[0:1, :], start=True, stop=True)
                tc.inc(te.matmul(PD[:, 0:TCH], ones_rf[0:1, :], rr4[0:1, :], start=True, stop=True))
            tc_b4[tq] = tc.n

            @block.vector
            def _(v, tq=tq):
                v.wait_ge(tc.h, tc_b4[tq])
                v.tensor_copy(out=m4, in_=PC[:, 0:TCH])
                v.tensor_copy(out=r4, in_=PD[:, 0:TCH])
                v.drain()
                for c in range(NCH):
                    v.tensor_tensor(out=yv[:, c * TCH:(c + 1) * TCH],
                                    in0=xo4[:, c * TCH:(c + 1) * TCH], in1=m4, op=ALU.subtract)
                v.drain()
                for c in range(NCH):
                    v.tensor_tensor(out=yv[:, c * TCH:(c + 1) * TCH],
                                    in0=yv[:, c * TCH:(c + 1) * TCH], in1=r4, op=ALU.mult)
                v.drain()
                for c in range(NCH):
                    i = v.tensor_tensor(out=yv[:, c * TCH:(c + 1) * TCH],
                                        in0=yv[:, c * TCH:(c + 1) * TCH],
                                        in1=y14[:, c * TCH:(c + 1) * TCH], op=ALU.add)
                vc.inc(i)
            vc_b4[tq] = vc.n

            for c in range(NCH):
                @block.scalar
                def _(s, tq=tq, c=c):
                    if c == 0:
                        s.wait_ge(vc.h, vc_b4[tq])
                    if c >= 1:
                        s.wait_ge(tc.h, tc_ss.get((tq, c - 1), 0))
                    sc.inc(s.activation(out=sq4, in_=yv[:, c * TCH:(c + 1) * TCH],
                                        func=AF.Square))
                sc_b4[(tq, c)] = sc.n

                @block.tensor
                def _(te, tq=tq, c=c):
                    te.wait_ge(sc.h, sc_b4[(tq, c)])
                    if tq >= 1 and c == 0:
                        te.wait_ge(sc.h, sc_rr[tq - 1])
                    tc.inc(te.matmul(PE_[0:1, 0:TCH], ones_cb[:, 0:1], sq4,
                                     start=(c == 0), stop=(c == NCH - 1)))
                tc_ss[(tq, c)] = tc.n

            @block.scalar
            def _(s, tq=tq):
                s.wait_ge(tc.h, tc_ss[(tq, NCH - 1)])
                s.activation(out=ss4[:], in_=PE_[0:1, 0:TCH], func=AF.Ln, scale=1.0 / D,
                             bias=epsrm_c[:])
                s.drain()
                sc.inc(s.activation(out=ss4[:], in_=ss4[:], func=AF.Exp, scale=-0.5))
            sc_rr[tq] = sc.n

            @block.tensor
            def _(te, tq=tq):
                te.wait_ge(sc.h, sc_rr[tq])
                if tq >= 1:
                    te.wait_ge(vc.h, vc_fin[tq - 1])
                tc.inc(te.matmul(PF[:, 0:TCH], ones_rf[0:1, :], ss4[0:1, :],
                                 start=True, stop=True))
            tc_rb[tq] = tc.n

            @block.vector
            def _(v, tq=tq):
                v.wait_ge(tc.h, tc_rb[tq])
                v.tensor_copy(out=rb4, in_=PF[:, 0:TCH])
                v.drain()
                for c in range(NCH):
                    v.scalar_tensor_tensor(out=yv[:, c * TCH:(c + 1) * TCH],
                                           in0=yv[:, c * TCH:(c + 1) * TCH],
                                           scalar=nw_sb[:, c:c + 1], in1=rb4,
                                           op0=ALU.mult, op1=ALU.mult)
                for c in range(NCH):
                    v.tensor_tensor(out=fin[:, c * TCH:(c + 1) * TCH],
                                    in0=xo4[:, c * TCH:(c + 1) * TCH],
                                    in1=at4[:, c * TCH:(c + 1) * TCH], op=ALU.add)
                v.drain()
                for c in range(NCH):
                    i = v.tensor_tensor(out=fin[:, c * TCH:(c + 1) * TCH],
                                        in0=fin[:, c * TCH:(c + 1) * TCH],
                                        in1=yv[:, c * TCH:(c + 1) * TCH], op=ALU.add)
                vc.inc(i)
            vc_fin[tq] = vc.n

            @block.sync
            def _(s, tq=tq):
                s.wait_ge(vc.h, vc_fin[tq])
                oc.inc(s.dma_start(
                    out=out_d.rearrange("(c p) t -> p c t", p=128)[:, :, tq * TCH:(tq + 1) * TCH],
                    in_=fin.rearrange("p (c t) -> p c t", c=NCH)), 16)

        @block.sync
        def _(s):
            s.wait_ge(oc.h, oc.n)
        ar.pop()

    return nc


_NC_CACHE = {}


def _get_nc():
    if "nc" not in _NC_CACHE:
        _NC_CACHE["nc"] = build_nc()
    return _NC_CACHE["nc"]


def _rope_tables():
    half = 64
    inv_freq = 1.0 / ROPE_BASE ** (np.arange(0, 128, 2, dtype=np.float64) / 128.0)
    t = np.arange(S, dtype=np.float64)
    freqs = np.outer(t, inv_freq)
    cos = np.cos(freqs).astype(np.float32).T
    sin = np.sin(freqs).astype(np.float32).T
    cos_dup = np.concatenate([cos, cos], axis=0).astype(BFD)
    sin_sgn = np.concatenate([-sin, sin], axis=0).astype(BFD)
    return cos_dup, sin_sgn


def _diag_masks():
    k = np.arange(128)[:, None]
    q = np.arange(QM)[None, :]
    ms = [(q >= (128 * j + k)).astype(np.float32) for j in range(4)]
    return np.concatenate(ms, axis=1).astype(BFD)


def kernel(L0_attn_hidden_states, L0_attn_Wq, L0_attn_Wk, L0_attn_Wv, L0_attn_Wo,
           L0_ffn_router_w, L0_ffn_router_b, L0_ffn_expert_w, L0_ffn_expert_b,
           L0_ffn_proj_w, L0_ffn_proj_b, L0_ffn_norm_w):
    x = np.asarray(L0_attn_hidden_states, dtype=np.float32)
    cos_dup, sin_sgn = _rope_tables()
    dmask = _diag_masks()
    ident = np.eye(128, dtype=np.float32)
    sel = np.zeros((8, E * 128), dtype=BFD)
    for e in range(E):
        sel[e, e * 128:(e + 1) * 128] = 1

    wq = np.ascontiguousarray(np.asarray(L0_attn_Wq, np.float32).T)
    wk = np.ascontiguousarray(np.asarray(L0_attn_Wk, np.float32).T)
    wv = np.ascontiguousarray(np.asarray(L0_attn_Wv, np.float32).T)
    wo = np.ascontiguousarray(np.asarray(L0_attn_Wo, np.float32).T)
    rw = np.ascontiguousarray(np.asarray(L0_ffn_router_w, np.float32))
    rb = np.ascontiguousarray(np.asarray(L0_ffn_router_b, np.float32)).reshape(1, E)
    ew = np.ascontiguousarray(np.asarray(L0_ffn_expert_w, np.float32))
    eb = np.ascontiguousarray(
        np.asarray(L0_ffn_expert_b, np.float32).reshape(E, 16, 128).transpose(2, 0, 1).reshape(128, E * 16))
    pw = np.ascontiguousarray(np.asarray(L0_ffn_proj_w, np.float32))
    pb = np.ascontiguousarray(np.asarray(L0_ffn_proj_b, np.float32).reshape(16, 128).T)
    nw = np.ascontiguousarray(np.asarray(L0_ffn_norm_w, np.float32).reshape(16, 128).T)

    in_maps = []
    for c in range(8):
        b = c // 2
        hh = c % 2
        x_fm = np.ascontiguousarray(x[b].T)
        x_ownc = np.ascontiguousarray(x_fm[:, hh * OWN:(hh + 1) * OWN])
        in_maps.append({
            "x_fm": x_fm,
            "x_own": x_ownc,
            "wqT": np.ascontiguousarray(wq[:, hh * 1024:(hh + 1) * 1024]),
            "wkT": np.ascontiguousarray(wk[:, hh * 1024:(hh + 1) * 1024]),
            "wvT": np.ascontiguousarray(wv[:, hh * 1024:(hh + 1) * 1024]),
            "woT": wo,
            "rw_d": rw, "rb_d": rb, "ew_d": ew, "eb_d": eb,
            "pw_d": pw, "pb_d": pb, "nw_d": nw,
            "cos_d": cos_dup, "sin_d": sin_sgn, "msk_d": dmask,
            "ident_d": ident,
            "sel_d": sel,
        })

    nc = _get_nc()
    res = run_bass_kernel_spmd(nc, in_maps, core_ids=list(range(8)))

    out = np.empty((B, S, D), dtype=np.float32)
    for c in range(8):
        b = c // 2
        hh = c % 2
        out[b, hh * OWN:(hh + 1) * OWN, :] = res.results[c]["out_d"].T
    return out


# version 05
# speedup vs baseline: 1.0389x; 1.0389x over previous
import contextlib
import math

import numpy as np
import ml_dtypes

import concourse.bass as bass
import concourse.mybir as mybir
from concourse.bass_utils import run_bass_kernel_spmd

F32 = mybir.dt.float32
BF16 = mybir.dt.bfloat16
AF = mybir.ActivationFunctionType
ALU = mybir.AluOpType
AXX = mybir.AxisListType.X

B, S, D = 4, 4096, 2048
NCH = 16
TCH = 512
NT = S // TCH
G = 2
NGRP = 4
QM = 512
NQM = S // QM
E = 8
OWN = 2048
EBIAS = -12.0
ISQD = 1.0 / math.sqrt(128.0)
ROPE_BASE = 10000.0

BFD = np.dtype(ml_dtypes.bfloat16)


class Ctr:
    def __init__(self, es, nc, name):
        self.h = es.enter_context(nc.semaphore(name))
        self.n = 0

    def inc(self, instr, k=1):
        instr.then_inc(self.h, k)
        self.n += k
        return self.n


class Arena:
    def __init__(self, t, nelem):
        self.t, self.nelem, self.off, self.marks = t, nelem, 0, []

    def push(self):
        self.marks.append(self.off)

    def pop(self):
        self.off = self.marks.pop()

    def bf16(self, n):
        n = (n + 511) & ~511
        ap = self.t[:, self.off:self.off + n]
        self.off += n
        assert self.off <= self.nelem, f"arena overflow {self.off} > {self.nelem}"
        return ap

    def f32(self, n):
        return self.bf16(2 * n).bitcast(F32)


def build_nc():
    nc = bass.Bass("TRN2", target_bir_lowering=False)

    x_fm = nc.dram_tensor("x_fm", [D, S], F32, kind="ExternalInput").ap()
    x_own = nc.dram_tensor("x_own", [D, OWN], F32, kind="ExternalInput").ap()
    wqT = nc.dram_tensor("wqT", [D, 1024], F32, kind="ExternalInput").ap()
    wkT = nc.dram_tensor("wkT", [D, 1024], F32, kind="ExternalInput").ap()
    wvT = nc.dram_tensor("wvT", [D, 1024], F32, kind="ExternalInput").ap()
    woT = nc.dram_tensor("woT", [D, D], F32, kind="ExternalInput").ap()
    rw_d = nc.dram_tensor("rw_d", [D, E], F32, kind="ExternalInput").ap()
    rb_d = nc.dram_tensor("rb_d", [1, E], F32, kind="ExternalInput").ap()
    ew_d = nc.dram_tensor("ew_d", [E, D, D], F32, kind="ExternalInput").ap()
    eb_d = nc.dram_tensor("eb_d", [128, E * NCH], F32, kind="ExternalInput").ap()
    pw_d = nc.dram_tensor("pw_d", [D, D], F32, kind="ExternalInput").ap()
    pb_d = nc.dram_tensor("pb_d", [128, NCH], F32, kind="ExternalInput").ap()
    nw_d = nc.dram_tensor("nw_d", [128, NCH], F32, kind="ExternalInput").ap()
    cos_d = nc.dram_tensor("cos_d", [128, S], BF16, kind="ExternalInput").ap()
    sin_d = nc.dram_tensor("sin_d", [128, S], BF16, kind="ExternalInput").ap()
    msk_d = nc.dram_tensor("msk_d", [128, 2 * 1024], BF16, kind="ExternalInput").ap()
    ident_d = nc.dram_tensor("ident_d", [128, 128], F32, kind="ExternalInput").ap()
    sel_d = nc.dram_tensor("sel_d", [8, E * 128], BF16, kind="ExternalInput").ap()
    out_d = nc.dram_tensor("out_d", [D, OWN], F32, kind="ExternalOutput").ap()

    xn_d = nc.dram_tensor("xn_d", [D, S], BF16).ap()
    xno_d = nc.dram_tensor("xno_d", [D, OWN], BF16).ap()
    rows_d = nc.dram_tensor("rows_d", [2, S], F32).ap()
    rowso_d = nc.dram_tensor("rowso_d", [2, OWN], F32).ap()
    ctxo_d = nc.dram_tensor("ctxo_d", [D, OWN], BF16).ap()
    comb_d = nc.dram_tensor("comb_d", [D, OWN], BF16).ap()
    y1_d = nc.dram_tensor("y1_d", [D, OWN], BF16).ap()
    attn_d = nc.dram_tensor("attn_d", [D, OWN], BF16).ap()
    cin_t = nc.dram_tensor("cin", [8 * 128, S], BF16)
    cout_t = nc.dram_tensor("cout", [16 * 128, S], BF16)
    cin = cin_t.ap()
    cout = cout_t.ap()
    RG = [[0, 1], [2, 3], [4, 5], [6, 7]]

    es = contextlib.ExitStack()
    AR_N = 90 * 1024
    art = es.enter_context(nc.sbuf_tensor("arena", [128, AR_N], BF16))
    ar = Arena(art, AR_N)

    ones_cb = es.enter_context(nc.sbuf_tensor("ones_cb", [128, 1], BF16))
    ones_rf = es.enter_context(nc.sbuf_tensor("ones_rf", [1, 128], F32))
    ident = es.enter_context(nc.sbuf_tensor("ident", [128, 128], F32))
    ebias_c = es.enter_context(nc.sbuf_tensor("ebias_c", [128, 1], F32))
    epsln_c = es.enter_context(nc.sbuf_tensor("epsln_c", [1, 1], F32))
    epsrm_c = es.enter_context(nc.sbuf_tensor("epsrm_c", [1, 1], F32))
    masks = es.enter_context(nc.sbuf_tensor("masks", [128, 2 * 1024], BF16))
    eb_sb = es.enter_context(nc.sbuf_tensor("eb_sb", [128, E * NCH], F32))
    pb_sb = es.enter_context(nc.sbuf_tensor("pb_sb", [128, NCH], F32))
    nw_sb = es.enter_context(nc.sbuf_tensor("nw_sb", [128, NCH], F32))
    rb_sb = es.enter_context(nc.sbuf_tensor("rb_sb", [1, E], F32))
    rw_sb = es.enter_context(nc.sbuf_tensor("rw_sb", [128, NCH * E], BF16))
    wrow_sb = es.enter_context(nc.sbuf_tensor("wrow_sb", [8, OWN], BF16))
    sel_sb = es.enter_context(nc.sbuf_tensor("sel_sb", [8, E * 128], BF16))
    mrow = es.enter_context(nc.sbuf_tensor("mrow", [1, TCH], F32))
    srow = es.enter_context(nc.sbuf_tensor("srow", [1, TCH], F32))
    vrow = es.enter_context(nc.sbuf_tensor("vrow", [1, TCH], F32))
    rrow = es.enter_context(nc.sbuf_tensor("rrow", [1, TCH], F32))
    linv = es.enter_context(nc.sbuf_tensor("linv", [1, QM], F32))
    lbc = es.enter_context(nc.sbuf_tensor("lbc", [128, QM], F32))
    lgt = es.enter_context(nc.sbuf_tensor("lgt", [128, 8], F32))
    rm1 = es.enter_context(nc.sbuf_tensor("rm1", [128, 1], F32))
    rm2 = es.enter_context(nc.sbuf_tensor("rm2", [128, 1], F32))
    req1 = es.enter_context(nc.sbuf_tensor("req1", [128, 8], F32))
    req2 = es.enter_context(nc.sbuf_tensor("req2", [128, 8], F32))
    rmsk = es.enter_context(nc.sbuf_tensor("rmsk", [128, 8], F32))
    rw1 = es.enter_context(nc.sbuf_tensor("rw1", [128, 1], F32))
    rw2 = es.enter_context(nc.sbuf_tensor("rw2", [128, 1], F32))
    rd12 = es.enter_context(nc.sbuf_tensor("rd12", [128, 1], F32))
    rfw = es.enter_context(nc.sbuf_tensor("rfw", [128, 8], F32))
    rfw1 = es.enter_context(nc.sbuf_tensor("rfw1", [128, 8], F32))

    PA = es.enter_context(nc.psum_tensor("PA", [128, 1024], F32))
    PB = es.enter_context(nc.psum_tensor("PB", [128, 1024], F32))
    PC = es.enter_context(nc.psum_tensor("PC", [128, 512], F32))
    PD = es.enter_context(nc.psum_tensor("PD", [128, 512], F32))
    PE_ = es.enter_context(nc.psum_tensor("PE", [128, 512], F32))
    PF = es.enter_context(nc.psum_tensor("PF", [128, 512], F32))

    with es:
        nc.cache_partition_id()
        block = es.enter_context(nc.Block())

        ld = Ctr(es, nc, "ld")
        gd = Ctr(es, nc, "gd")
        tc = Ctr(es, nc, "tc")
        sc = Ctr(es, nc, "sc")
        vc = Ctr(es, nc, "vc")
        cc = Ctr(es, nc, "cc")
        oc = Ctr(es, nc, "oc")

        def toko(eng, key=None):
            return (eng.partition_id() % 2) * OWN

        @block.vector
        def _(v):
            v.memset(ones_cb[:], 1.0)
            v.memset(ones_rf[:], 1.0)
            v.memset(ebias_c[:], EBIAS)
            v.memset(epsln_c[:], 1e-5)
            i = v.memset(epsrm_c[:], 1e-6)
            vc.inc(i)
        v_init = vc.n

        @block.sync
        def _(s):
            ld.inc(s.dma_start(out=masks[:], in_=msk_d[:, :]), 16)
            ld.inc(s.dma_start(out=rb_sb[:], in_=rb_d[:, :]), 16)
            ld.inc(s.dma_start(out=eb_sb[:], in_=eb_d[:, :]), 16)
            ld.inc(s.dma_start(out=ident[:], in_=ident_d[:, :]), 16)
            ld.inc(s.dma_start(out=pb_sb[:], in_=pb_d[:, :]), 16)
            ld.inc(s.dma_start(out=nw_sb[:], in_=nw_d[:, :]), 16)
            ld.inc(s.dma_start(out=sel_sb[:], in_=sel_d[:, :]), 16)
            s.wait_ge(ld.h, ld.n)
        ld_init = ld.n

        @block.gpsimd
        def _(g):
            gd.inc(g.dma_start(out=rw_sb[:].rearrange("p (c e) -> p c e", c=NCH),
                               in_=rw_d.rearrange("(c p) e -> p c e", p=128)), 16)
        gd_init = gd.n

        ar.push()
        cosb = ar.bf16(S)
        sinb = ar.bf16(S)
        ar.push()
        xf32 = [ar.f32(NCH * TCH) for _ in range(2)]
        xbf = [ar.bf16(NCH * TCH) for _ in range(2)]
        xsq = ar.bf16(NCH * TCH)
        xnb = ar.bf16(NCH * TCH)
        mbc = ar.f32(TCH)
        rbc = ar.f32(TCH)

        @block.sync
        def _(s):
            ld.inc(s.dma_start(out=cosb, in_=cos_d[:, :]), 16)
            ld.inc(s.dma_start(out=sinb, in_=sin_d[:, :]), 16)

        ld_x, gd_x, sc_sq, tc_st, vc_rows, sc_rs, tc_bc, vc_cp, vc_xn, ld_sto = \
            {}, {}, {}, {}, {}, {}, {}, {}, {}, {}

        for t in range(NT):
            bi = t % 2
            t0 = t * TCH

            @block.sync
            def _(s, t=t, bi=bi, t0=t0):
                if t >= 2:
                    s.wait_ge(vc.h, vc_xn[t - 2])
                ld.inc(s.dma_start(
                    out=xf32[bi].rearrange("p (c t) -> p c t", c=NCH),
                    in_=x_fm.rearrange("(c p) t -> p c t", p=128)[:, :, t0:t0 + TCH]), 16)
            ld_x[t] = ld.n

            @block.gpsimd
            def _(g, t=t, bi=bi, t0=t0):
                if t >= 2:
                    g.wait_ge(tc.h, tc_st[t - 2])
                gd.inc(g.dma_start(
                    out=xbf[bi].rearrange("p (c t) -> p c t", c=NCH),
                    in_=x_fm.rearrange("(c p) t -> p c t", p=128)[:, :, t0:t0 + TCH]), 16)
            gd_x[t] = gd.n

            @block.scalar
            def _(s, t=t, bi=bi):
                s.wait_ge(gd.h, gd_x[t])
                if t >= 1:
                    s.wait_ge(tc.h, tc_st[t - 1])
                sc.inc(s.activation(out=xsq, in_=xbf[bi], func=AF.Square))
            sc_sq[t] = sc.n

            @block.tensor
            def _(te, t=t, bi=bi):
                te.wait_ge(sc.h, sc_sq[t])
                if t >= 1:
                    te.wait_ge(vc.h, vc_rows[t - 1])
                for c in range(NCH):
                    te.matmul(PE_[0:1, 0:TCH], ones_cb[:, 0:1],
                              xbf[bi][:, c * TCH:(c + 1) * TCH],
                              start=(c == 0), stop=(c == NCH - 1))
                for c in range(NCH):
                    i = te.matmul(PF[0:1, 0:TCH], ones_cb[:, 0:1],
                                  xsq[:, c * TCH:(c + 1) * TCH],
                                  start=(c == 0), stop=(c == NCH - 1))
                tc.inc(i)
            tc_st[t] = tc.n

            @block.vector
            def _(v, t=t):
                v.wait_ge(tc.h, tc_st[t])
                if t >= 1:
                    v.wait_ge(ld.h, ld_sto[t - 1])
                v.tensor_scalar(out=mrow[:], in0=PE_[0:1, 0:TCH], scalar1=1.0 / D,
                                scalar2=None, op0=ALU.mult)
                v.tensor_scalar(out=srow[:], in0=PF[0:1, 0:TCH], scalar1=1.0 / D,
                                scalar2=None, op0=ALU.mult)
                v.drain()
                v.tensor_tensor(out=vrow[:], in0=mrow[:], in1=mrow[:], op=ALU.mult)
                v.drain()
                i = v.tensor_tensor(out=vrow[:], in0=srow[:], in1=vrow[:], op=ALU.subtract)
                vc.inc(i)
            vc_rows[t] = vc.n

            @block.scalar
            def _(s, t=t):
                s.wait_ge(vc.h, vc_rows[t])
                s.activation(out=rrow[:], in_=vrow[:], func=AF.Ln, bias=epsln_c[:])
                s.drain()
                sc.inc(s.activation(out=rrow[:], in_=rrow[:], func=AF.Exp, scale=-0.5))
            sc_rs[t] = sc.n

            @block.tensor
            def _(te, t=t):
                te.wait_ge(sc.h, sc_rs[t])
                if t >= 1:
                    te.wait_ge(vc.h, vc_cp[t - 1])
                te.matmul(PC[:, 0:TCH], ones_rf[0:1, :], mrow[0:1, :], start=True, stop=True)
                i = te.matmul(PD[:, 0:TCH], ones_rf[0:1, :], rrow[0:1, :], start=True, stop=True)
                tc.inc(i)
            tc_bc[t] = tc.n

            @block.vector
            def _(v, t=t, bi=bi):
                v.wait_ge(tc.h, tc_bc[t])
                v.wait_ge(ld.h, ld_x[t])
                if t >= 1:
                    v.wait_ge(ld.h, ld_sto[t - 1])
                v.tensor_copy(out=mbc, in_=PC[:, 0:TCH])
                i = v.tensor_copy(out=rbc, in_=PD[:, 0:TCH])
                vc.inc(i)
                v.drain()
                for c in range(NCH):
                    v.tensor_tensor(out=xf32[bi][:, c * TCH:(c + 1) * TCH],
                                    in0=xf32[bi][:, c * TCH:(c + 1) * TCH], in1=mbc,
                                    op=ALU.subtract)
                v.drain()
                for c in range(NCH):
                    i = v.tensor_tensor(out=xnb[:, c * TCH:(c + 1) * TCH],
                                        in0=xf32[bi][:, c * TCH:(c + 1) * TCH], in1=rbc,
                                        op=ALU.mult)
                vc.inc(i)
            vc_cp[t] = vc.n - 1
            vc_xn[t] = vc.n

            @block.sync
            def _(s, t=t, t0=t0):
                s.wait_ge(vc.h, vc_xn[t])
                ld.inc(s.dma_start(
                    out=xn_d.rearrange("(c p) t -> p c t", p=128)[:, :, t0:t0 + TCH],
                    in_=xnb.rearrange("p (c t) -> p c t", c=NCH)), 16)
                ld.inc(s.dma_start(out=rows_d[0:1, t0:t0 + TCH], in_=mrow[:]), 16)
                ld.inc(s.dma_start(out=rows_d[1:2, t0:t0 + TCH], in_=rrow[:]), 16)
                s.wait_ge(ld.h, ld.n)
            ld_sto[t] = ld.n

        p0_ld = ld.n
        ar.pop()

        ar.push()
        xnp = [ar.bf16(NCH * TCH) for _ in range(2)]
        qraw = ar.bf16(G * S)
        kraw = ar.bf16(G * S)
        qfm = ar.bf16(G * S)
        kfm = ar.bf16(G * S)
        vtm = ar.bf16(G * S)
        xsw = ar.bf16(S)
        t2s = ar.bf16(S)
        wq_sb = ar.bf16(NCH * G * 128)
        wk_sb = ar.bf16(NCH * G * 128)
        wv_sb = ar.bf16(NCH * G * 128)
        pt = [ar.bf16(1024) for _ in range(2)]
        ctxo = [ar.bf16(QM) for _ in range(2)]

        @block.gpsimd
        def _(gg):
            tko = toko(gg)
            gg.wait_ge(ld.h, p0_ld)
            gd.inc(gg.dma_start(out=xno_d[:, :], in_=xn_d[:, bass.ds(tko, OWN)]), 16)
            gd.inc(gg.dma_start(out=rowso_d[:, :], in_=rows_d[:, bass.ds(tko, OWN)]), 16)
            gg.wait_ge(gd.h, gd.n)
        gd_own = gd.n

        gd_w, ld_xn1, tc_proj, vc_evac, vc_rope, ld_sw = {}, {}, {}, {}, {}, {}
        ld_cst = []
        v_ctx_list = []
        grp_end_vc = {}
        fin_idx = 0

        for g in range(NGRP):
            o0 = g * G * 128

            @block.gpsimd
            def _(gg, g=g, o0=o0):
                if g >= 1:
                    gg.wait_ge(tc.h, tc_proj[(g - 1, NT - 1)])
                for w_sb, w_dd in ((wq_sb, wqT), (wk_sb, wkT), (wv_sb, wvT)):
                    gd.inc(gg.dma_start(
                        out=w_sb.rearrange("p (c o) -> p c o", c=NCH),
                        in_=w_dd.rearrange("(c p) o -> p c o", p=128)[:, :, o0:o0 + G * 128]), 16)
            gd_w[g] = gd.n

            for t in range(NT):
                bi = t % 2
                t0 = t * TCH

                @block.sync
                def _(s, g=g, t=t, bi=bi, t0=t0):
                    if g == 0 and t == 0:
                        s.wait_ge(ld.h, p0_ld)
                    gi = g * NT + t
                    if gi >= 2:
                        pg, pt_ = divmod(gi - 2, NT)
                        s.wait_ge(tc.h, tc_proj[(pg, pt_)])
                    ld.inc(s.dma_start(
                        out=xnp[bi].rearrange("p (c t) -> p c t", c=NCH),
                        in_=xn_d.rearrange("(c p) t -> p c t", p=128)[:, :, t0:t0 + TCH]), 16)
                ld_xn1[(g, t)] = ld.n

                @block.tensor
                def _(te, g=g, t=t, bi=bi):
                    te.wait_ge(ld.h, ld_xn1[(g, t)])
                    te.wait_ge(gd.h, gd_w[g])
                    if t > 0:
                        te.wait_ge(vc.h, vc_evac[(g, t - 1)])
                    elif g > 0:
                        te.wait_ge(vc.h, grp_end_vc[g - 1])
                    for oc in range(G):
                        for c in range(NCH):
                            te.matmul(PA[:, oc * 512:oc * 512 + TCH],
                                      wq_sb[:, c * G * 128 + oc * 128:c * G * 128 + (oc + 1) * 128],
                                      xnp[bi][:, c * TCH:(c + 1) * TCH],
                                      start=(c == 0), stop=(c == NCH - 1))
                    for oc in range(G):
                        for c in range(NCH):
                            te.matmul(PB[:, oc * 512:oc * 512 + TCH],
                                      wk_sb[:, c * G * 128 + oc * 128:c * G * 128 + (oc + 1) * 128],
                                      xnp[bi][:, c * TCH:(c + 1) * TCH],
                                      start=(c == 0), stop=(c == NCH - 1))
                    for ts in range(4):
                        pp = (PC, PD, PE_, PF)[ts]
                        for c in range(NCH):
                            i = te.matmul(pp[:, 0:G * 128],
                                          xnp[bi][:, c * TCH + ts * 128:c * TCH + (ts + 1) * 128],
                                          wv_sb[:, c * G * 128:(c + 1) * G * 128],
                                          start=(c == 0), stop=(c == NCH - 1))
                    tc.inc(i)
                tc_proj[(g, t)] = tc.n

                @block.vector
                def _(v, g=g, t=t, t0=t0):
                    v.wait_ge(tc.h, tc_proj[(g, t)])
                    for ocp in range(G):
                        v.tensor_copy(out=qraw[:, ocp * S + t0:ocp * S + t0 + TCH],
                                      in_=PA[:, ocp * 512:ocp * 512 + TCH])
                        v.tensor_copy(out=kraw[:, ocp * S + t0:ocp * S + t0 + TCH],
                                      in_=PB[:, ocp * 512:ocp * 512 + TCH])
                    for ts in range(4):
                        pp = (PC, PD, PE_, PF)[ts]
                        cg = t * 4 + ts
                        for h in range(G):
                            i = v.tensor_copy(
                                out=vtm[:, h * S + cg * 128:h * S + (cg + 1) * 128],
                                in_=pp[:, h * 128:(h + 1) * 128])
                    vc.inc(i)
                vc_evac[(g, t)] = vc.n

            for pi, (raw, fm) in enumerate(((qraw, qfm), (kraw, kfm))):
                for h in range(G):
                    @block.sync
                    def _(s, raw=raw, h=h, g=g, pi=pi):
                        s.wait_ge(vc.h, vc_evac[(g, NT - 1)])
                        if (g, pi, h) != (0, 0, 0):
                            s.wait_ge(vc.h, vc.n)
                        ld.inc(s.dma_start(out=xsw[0:64, :],
                                           in_=raw[64:128, h * S:(h + 1) * S]), 16)
                        ld.inc(s.dma_start(out=xsw[64:128, :],
                                           in_=raw[0:64, h * S:(h + 1) * S]), 16)
                        s.wait_ge(ld.h, ld.n)
                    ld_sw[(g, pi, h)] = ld.n

                    @block.vector
                    def _(v, raw=raw, fm=fm, h=h, g=g, pi=pi):
                        v.wait_ge(ld.h, ld_sw[(g, pi, h)])
                        v.tensor_tensor(out=xsw, in0=xsw, in1=sinb, op=ALU.mult)
                        v.tensor_tensor(out=t2s, in0=raw[:, h * S:(h + 1) * S], in1=cosb,
                                        op=ALU.mult)
                        v.drain()
                        i = v.tensor_tensor(out=fm[:, h * S:(h + 1) * S], in0=xsw, in1=t2s,
                                            op=ALU.add)
                        vc.inc(i)
                    vc_rope[(g, pi, h)] = vc.n

            for hg in range(G):
                for qm in range(NQM):
                    q0 = qm * QM
                    nsup = 2 * (qm + 1)
                    diag0 = 2 * qm

                    sup_sc, sup_pe, sup_pm, sup_pv = {}, {}, {}, {}
                    tq, sq_, vq = tc.n, sc.n, vc.n
                    for ks in range(nsup):
                        tq += 1
                        sup_sc[ks] = tq
                        sq_ += 1
                        sup_pe[ks] = sq_
                        if ks >= diag0:
                            vq += 1
                            sup_pm[ks] = vq
                        tq += 1
                        sup_pv[ks] = tq

                    @block.tensor
                    def _(te, g=g, hg=hg, qm=qm, nsup=nsup, q0=q0, diag0=diag0,
                          sup_pe=sup_pe, sup_pm=sup_pm, fin_idx=fin_idx):
                        te.wait_ge(vc.h, vc_rope[(g, 0, hg)])
                        te.wait_ge(vc.h, vc_rope[(g, 1, hg)])
                        if fin_idx > 0:
                            te.wait_ge(vc.h, v_ctx_list[fin_idx - 1])
                        for ks in range(nsup):
                            b = ks % 2
                            sst = (PA, PB)[b]
                            if ks >= 2:
                                te.wait_ge(sc.h, sup_pe[ks - 2])
                            for c2 in range(2):
                                kc = ks * 2 + c2
                                i = te.matmul(sst[:, c2 * 512:c2 * 512 + QM],
                                              kfm[:, hg * S + kc * 128:hg * S + (kc + 1) * 128],
                                              qfm[:, hg * S + q0:hg * S + q0 + QM],
                                              start=True, stop=True)
                            tc.inc(i)
                            if ks >= diag0:
                                te.wait_ge(vc.h, sup_pm[ks])
                            else:
                                te.wait_ge(sc.h, sup_pe[ks])
                            for c2 in range(2):
                                kc = ks * 2 + c2
                                te.matmul(PC[:, 0:QM],
                                          vtm[:, hg * S + kc * 128:hg * S + (kc + 1) * 128],
                                          pt[b][:, c2 * 512:c2 * 512 + QM],
                                          start=(ks == 0 and c2 == 0),
                                          stop=(ks == nsup - 1 and c2 == 1))
                                i = te.matmul(PD[0:1, 0:QM], ones_cb[:, 0:1],
                                              pt[b][:, c2 * 512:c2 * 512 + QM],
                                              start=(ks == 0 and c2 == 0),
                                              stop=(ks == nsup - 1 and c2 == 1))
                            tc.inc(i)
                    assert tc.n == sup_pv[nsup - 1], (tc.n, sup_pv[nsup - 1])

                    @block.scalar
                    def _(s, nsup=nsup, sup_sc=sup_sc, sup_pv=sup_pv):
                        for ks in range(nsup):
                            b = ks % 2
                            sst = (PA, PB)[b]
                            s.wait_ge(tc.h, sup_sc[ks])
                            if ks >= 2:
                                s.wait_ge(tc.h, sup_pv[ks - 2])
                            sc.inc(s.activation(out=pt[b][:], in_=sst[:, 0:1024],
                                                func=AF.Exp, scale=ISQD, bias=ebias_c[:]))
                    assert sc.n == sup_pe[nsup - 1]

                    if nsup > diag0:
                        @block.vector
                        def _(v, nsup=nsup, diag0=diag0, sup_pe=sup_pe):
                            for ks in range(diag0, nsup):
                                b = ks % 2
                                js = ks - diag0
                                v.wait_ge(sc.h, sup_pe[ks])
                                vc.inc(v.tensor_tensor(
                                    out=pt[b][:], in0=pt[b][:],
                                    in1=masks[:, js * 1024:(js + 1) * 1024], op=ALU.mult))
                        assert vc.n == sup_pm[nsup - 1]

                    @block.vector
                    def _(v, sup_pv=sup_pv, nsup=nsup):
                        v.wait_ge(tc.h, sup_pv[nsup - 1])
                        vc.inc(v.reciprocal(out=linv[:], in_=PD[0:1, 0:QM]))
                    v_linv = vc.n

                    @block.tensor
                    def _(te, v_linv=v_linv):
                        te.wait_ge(vc.h, v_linv)
                        tc.inc(te.matmul(PE_[:, 0:QM], ones_rf[0:1, :], linv[0:1, :],
                                         start=True, stop=True))
                    t_lb = tc.n

                    @block.vector
                    def _(v, t_lb=t_lb, fin_idx=fin_idx):
                        bo = fin_idx % 2
                        v.wait_ge(tc.h, t_lb)
                        if fin_idx >= 2:
                            v.wait_ge(ld.h, ld_cst[fin_idx - 2])
                        v.tensor_copy(out=lbc[:], in_=PE_[:, 0:QM])
                        v.drain()
                        vc.inc(v.tensor_tensor(out=ctxo[bo][:], in0=PC[:, 0:QM],
                                               in1=lbc[:], op=ALU.mult))
                    v_ctx_list.append(vc.n)

                    @block.sync
                    def _(s, g=g, hg=hg, qm=qm, q0=q0, fin_idx=fin_idx):
                        bo = fin_idx % 2
                        s.wait_ge(vc.h, v_ctx_list[fin_idx])
                        hidx = g * G + hg
                        ld.inc(s.dma_start(out=cin[hidx * 128:(hidx + 1) * 128, q0:q0 + QM],
                                           in_=ctxo[bo][:]), 16)
                    ld_cst.append(ld.n)
                    fin_idx += 1

            grp_end_vc[g] = vc.n

        p1_ld = ld.n
        p1_tc = tc.n
        p1_vc = vc.n
        ar.pop()
        ar.pop()

        ar.push()
        xnp3 = [ar.bf16(NCH * TCH) for _ in range(2)]
        comb = ar.bf16(NCH * OWN)
        wep = [ar.bf16(NCH * 1024) for _ in range(2)]
        silu_t = [ar.bf16(TCH) for _ in range(4)]
        tmp_t = [ar.bf16(TCH) for _ in range(4)]


        ld_xn3, tc_lg, tc_tr, vc_fw = {}, {}, {}, {}
        vc_r1, sc_sg = {}, {}
        for t3 in range(4):
            bi = t3 % 2

            @block.sync
            def _(s, t3=t3, bi=bi):
                s.wait_ge(ld.h, p1_ld)
                s.wait_ge(vc.h, p1_vc)
                s.wait_ge(tc.h, p1_tc)
                s.wait_ge(gd.h, gd_own)
                if t3 == 0:
                    s.wait_ge(cc.h, 1)
                if t3 >= 2:
                    s.wait_ge(tc.h, tc_lg[(t3 - 2, 3)])
                ld.inc(s.dma_start(
                    out=xnp3[bi].rearrange("p (c t) -> p c t", c=NCH),
                    in_=xno_d.rearrange("(c p) t -> p c t", p=128)[:, :, t3 * TCH:(t3 + 1) * TCH]), 16)
            ld_xn3[t3] = ld.n

            for ts in range(4):
                ti = t3 * 4 + ts

                @block.tensor
                def _(te, t3=t3, bi=bi, ts=ts, ti=ti):
                    te.wait_ge(ld.h, ld_xn3[t3])
                    te.wait_ge(gd.h, gd_init)
                    if ti == 0:
                        te.wait_ge(ld.h, ld_init)
                    else:
                        te.wait_ge(vc.h, vc_r1[ti - 1])
                    if ti >= 1:
                        te.wait_ge(vc.h, vc_fw[ti - 1])
                    for c in range(NCH):
                        te.matmul(PE_[:, 0:8],
                                  xnp3[bi][:, c * TCH + ts * 128:c * TCH + (ts + 1) * 128],
                                  rw_sb[:, c * E:(c + 1) * E],
                                  start=(c == 0), stop=False)
                    tc.inc(te.matmul(PE_[:, 0:8], ones_rf[0:1, 0:128], rb_sb[0:1, :],
                                     start=False, stop=True))
                tc_lg[(t3, ts)] = tc.n

                @block.vector
                def _(v, t3=t3, ts=ts, ti=ti):
                    v.wait_ge(tc.h, tc_lg[(t3, ts)])
                    v.tensor_copy(out=lgt[:], in_=PE_[:, 0:8])
                    v.drain()
                    v.reduce_max(out=rm1[:], in_=lgt[:], axis=AXX)
                    v.drain()
                    v.tensor_scalar(out=req1[:], in0=lgt[:], scalar1=rm1[:], scalar2=None,
                                    op0=ALU.is_equal)
                    v.drain()
                    v.scalar_tensor_tensor(out=rmsk[:], in0=req1[:], scalar=-1e30, in1=lgt[:],
                                           op0=ALU.mult, op1=ALU.add)
                    v.drain()
                    v.reduce_max(out=rm2[:], in_=rmsk[:], axis=AXX)
                    v.drain()
                    v.tensor_tensor(out=rd12[:], in0=rm1[:], in1=rm2[:], op=ALU.subtract)
                    vc.inc(v.tensor_scalar(out=req2[:], in0=lgt[:], scalar1=rm2[:],
                                           scalar2=None, op0=ALU.is_equal))
                vc_r1[ti] = vc.n

                @block.scalar
                def _(s, ti=ti):
                    s.wait_ge(vc.h, vc_r1[ti])
                    sc.inc(s.activation(out=rw1[:], in_=rd12[:], func=AF.Sigmoid))
                sc_sg[ti] = sc.n

                @block.vector
                def _(v, ti=ti):
                    v.wait_ge(sc.h, sc_sg[ti])
                    v.tensor_scalar(out=rw2[:], in0=rw1[:], scalar1=-1.0, scalar2=1.0,
                                    op0=ALU.mult, op1=ALU.add)
                    v.tensor_scalar(out=rfw1[:], in0=req1[:], scalar1=rw1[:], scalar2=None,
                                    op0=ALU.mult)
                    v.drain()
                    vc.inc(v.scalar_tensor_tensor(out=rfw[:], in0=req2[:], scalar=rw2[:],
                                                  in1=rfw1[:], op0=ALU.mult, op1=ALU.add))
                v_fw2 = vc.n

                @block.tensor
                def _(te, v_fw2=v_fw2):
                    te.wait_ge(vc.h, v_fw2)
                    tc.inc(te.transpose(PF[0:8, 0:128], rfw[:, 0:8], ident[:]))
                tc_tr[ti] = tc.n

                @block.vector
                def _(v, ti=ti):
                    v.wait_ge(tc.h, tc_tr[ti])
                    vc.inc(v.tensor_copy(out=wrow_sb[:, ti * 128:(ti + 1) * 128],
                                         in_=PF[0:8, 0:128]))
                vc_fw[ti] = vc.n

        NEP = 2
        gd_wep = {}
        tc_hh, sc_hh, vc_cmb, vc_cmb_idx = {}, {}, {}, {}

        @block.gpsimd
        def _(gg):
            gg.wait_ge(tc.h, p1_tc)
            gg.wait_ge(vc.h, p1_vc)
            gd.inc(gg.dma_start(
                out=wep[0].rearrange("p (c f) -> p c f", c=NCH),
                in_=ew_d[0].rearrange("(c p) f -> p c f", p=128)[:, :, 0:1024]), 16)
        gd_wep[(0, 0)] = gd.n

        @block.gpsimd
        def _(gg):
            tko = toko(gg)
            gg.wait_ge(ld.h, p1_ld)
            gg.wait_ge(tc.h, p1_tc)
            gg.wait_ge(vc.h, p1_vc)
            gg.wait_ge(gd.h, gd_wep[(0, 0)])
            cc.inc(gg.collective_compute("AllGather", ALU.bypass, replica_groups=RG,
                                         ins=[cin_t.ap().opt()], outs=[cout_t.ap().opt()]))
            gg.wait_ge(cc.h, cc.n)
            gd.inc(gg.dma_start(out=ctxo_d[:, :], in_=cout[:, bass.ds(tko, OWN)]), 16)
            gg.wait_ge(gd.h, gd.n)
        gd_ctxo = gd.n

        tc_bc3 = {}
        for e in range(E):
            @block.tensor
            def _(te, e=e):
                for t3 in range(4):
                    slot = (PC, PD, PE_, PF)[t3]
                    if e == 0:
                        te.wait_ge(vc.h, vc_fw[15])
                    else:
                        te.wait_ge(vc.h, vc_cmb[(e - 1, NEP - 1, t3, 7)])
                    tc.inc(te.matmul(slot[:, 0:TCH], sel_sb[0:8, e * 128:(e + 1) * 128],
                                     wrow_sb[0:8, t3 * TCH:(t3 + 1) * TCH],
                                     start=True, stop=True))
                    tc_bc3[(e, t3)] = tc.n
            for p in range(NEP):
                ep = e * NEP + p
                if ep > 0:
                    @block.gpsimd
                    def _(gg, e=e, p=p, ep=ep):
                        if ep >= 2:
                            e2, p2 = divmod(ep - 2, NEP)
                            gg.wait_ge(tc.h, tc_hh[(e2, p2, 3, 7)])
                        gd.inc(gg.dma_start(
                            out=wep[ep % 2].rearrange("p (c f) -> p c f", c=NCH),
                            in_=ew_d[e].rearrange("(c p) f -> p c f", p=128)[:, :, p * 1024:(p + 1) * 1024]), 16)
                    gd_wep[(e, p)] = gd.n
                else:
                    pass

                for t3 in range(4):
                    bi3 = (ep * 4 + t3) % 2

                    @block.sync
                    def _(s, e=e, p=p, t3=t3, bi3=bi3, ep=ep):
                        gi = ep * 4 + t3
                        if gi == 0:
                            s.wait_ge(tc.h, tc_lg[(3, 3)])
                        if gi >= 2:
                            g2 = gi - 2
                            e2, r2 = divmod(g2, NEP * 4)
                            p2, t2 = divmod(r2, 4)
                            s.wait_ge(tc.h, tc_hh[(e2, p2, t2, 7)])
                        ld.inc(s.dma_start(
                            out=xnp3[bi3].rearrange("p (c t) -> p c t", c=NCH),
                            in_=xno_d.rearrange("(c p) t -> p c t", p=128)[:, :, t3 * TCH:(t3 + 1) * TCH]), 16)
                    ld_xn3[(e, p, t3)] = ld.n

                    for fj in range(8):
                        idx = (ep * 4 + t3) * 8 + fj

                        @block.tensor
                        def _(te, e=e, p=p, t3=t3, fj=fj, bi3=bi3, idx=idx):
                            te.wait_ge(ld.h, ld_xn3[(e, p, t3)])
                            te.wait_ge(gd.h, gd_wep[(e, p)])
                            if idx >= 4:
                                te.wait_ge(sc.h, sc_hh[idx - 4])
                            pp = (PA[:, 0:512], PA[:, 512:1024],
                                  PB[:, 0:512], PB[:, 512:1024])[fj % 4]
                            for c in range(NCH):
                                i = te.matmul(pp,
                                              wep[p % 2][:, c * 1024 + fj * 128:c * 1024 + (fj + 1) * 128],
                                              xnp3[bi3][:, c * TCH:(c + 1) * TCH],
                                              start=(c == 0), stop=(c == NCH - 1))
                            tc.inc(i)
                        tc_hh[(e, p, t3, fj)] = tc.n

                        @block.scalar
                        def _(s, e=e, p=p, t3=t3, fj=fj, idx=idx):
                            s.wait_ge(tc.h, tc_hh[(e, p, t3, fj)])
                            if idx >= 4:
                                s.wait_ge(vc.h, vc_cmb_idx[idx - 4])
                            pp = (PA[:, 0:512], PA[:, 512:1024],
                                  PB[:, 0:512], PB[:, 512:1024])[fj % 4]
                            cf = p * 8 + fj
                            sc.inc(s.activation(out=silu_t[fj % 4][:], in_=pp, func=AF.Silu,
                                                bias=eb_sb[:, e * NCH + cf:e * NCH + cf + 1]))
                        sc_hh[idx] = sc.n

                        if fj % 4 == 3:
                            @block.vector
                            def _(v, e=e, p=p, t3=t3, fj=fj, idx=idx):
                                v.wait_ge(sc.h, sc_hh[idx])
                                v.wait_ge(tc.h, tc_bc3[(e, t3)])
                                wslice = (PC, PD, PE_, PF)[t3][:, 0:TCH]
                                for q in range(4):
                                    fjq = fj - 3 + q
                                    cf = p * 8 + fjq
                                    dst = comb[:, cf * OWN + t3 * TCH:cf * OWN + (t3 + 1) * TCH]
                                    if e == 0:
                                        i = v.tensor_tensor(out=dst, in0=silu_t[q][:],
                                                            in1=wslice, op=ALU.mult)
                                    else:
                                        v.tensor_tensor(out=tmp_t[q][:], in0=silu_t[q][:],
                                                        in1=wslice, op=ALU.mult)
                                if e > 0:
                                    v.drain()
                                    for q in range(4):
                                        fjq = fj - 3 + q
                                        cf = p * 8 + fjq
                                        dst = comb[:, cf * OWN + t3 * TCH:cf * OWN + (t3 + 1) * TCH]
                                        i = v.tensor_tensor(out=dst, in0=dst, in1=tmp_t[q][:],
                                                            op=ALU.add)
                                vc.inc(i)
                            for q in range(4):
                                vc_cmb_idx[idx - 3 + q] = vc.n
                            vc_cmb[(e, p, t3, fj)] = vc.n

        @block.sync
        def _(s):
            s.wait_ge(vc.h, vc_cmb[(E - 1, NEP - 1, 3, 7)])
            for q in range(4):
                ld.inc(s.dma_start(
                    out=comb_d.rearrange("(c p) t -> p c t", p=128)[:, :, q * TCH:(q + 1) * TCH],
                    in_=comb.rearrange("p (c t) -> p c t", c=NCH)[:, :, q * TCH:(q + 1) * TCH]), 16)
        p3_ld = ld.n
        p3_tc = tc.n
        p3_vc = vc.n
        p3_sc = sc.n
        ar.pop()

        ar.push()
        T4 = 256
        wob = [ar.bf16(NCH * 512) for _ in range(2)]
        pwb = [ar.bf16(NCH * 512) for _ in range(2)]
        cmb4 = [ar.bf16(NCH * T4) for _ in range(2)]
        ctx4 = [ar.bf16(NCH * T4) for _ in range(2)]
        y1s = [ar.bf16(4 * T4) for _ in range(2)]
        ats = [ar.bf16(4 * T4) for _ in range(2)]

        gd_pan, ld_c4, sc_y1, vc_at, ld_s4, tc_w4 = {}, {}, {}, {}, {}, {}

        for op in range(4):
            @block.gpsimd
            def _(gg, op=op):
                if op == 0:
                    gg.wait_ge(tc.h, p3_tc)
                    gg.wait_ge(sc.h, p3_sc)
                    gg.wait_ge(vc.h, p3_vc)
                    gg.wait_ge(ld.h, p3_ld)
                if op >= 2:
                    gg.wait_ge(tc.h, tc_w4[(op - 2, 7, 3)])
                gd.inc(gg.dma_start(
                    out=wob[op % 2].rearrange("p (c o) -> p c o", c=NCH),
                    in_=woT.rearrange("(c p) o -> p c o", p=128)[:, :, op * 512:(op + 1) * 512]), 16)
                gd.inc(gg.dma_start(
                    out=pwb[op % 2].rearrange("p (c o) -> p c o", c=NCH),
                    in_=pw_d.rearrange("(c p) o -> p c o", p=128)[:, :, op * 512:(op + 1) * 512]), 16)
            gd_pan[op] = gd.n

            for tq in range(8):
                bi4 = tq % 2
                gi4 = op * 8 + tq

                @block.sync
                def _(s, op=op, tq=tq, bi4=bi4, gi4=gi4):
                    if op == 0 and tq == 0:
                        s.wait_ge(ld.h, p3_ld)
                        s.wait_ge(gd.h, gd_ctxo)
                    if gi4 >= 2:
                        o2, t2 = divmod(gi4 - 2, 8)
                        s.wait_ge(tc.h, tc_w4[(o2, t2, 3)])
                    ld.inc(s.dma_start(
                        out=cmb4[bi4].rearrange("p (c t) -> p c t", c=NCH),
                        in_=comb_d.rearrange("(c p) t -> p c t", p=128)[:, :, tq * T4:(tq + 1) * T4]), 16)
                    ld.inc(s.dma_start(
                        out=ctx4[bi4].rearrange("p (c t) -> p c t", c=NCH),
                        in_=ctxo_d.rearrange("(c p) t -> p c t", p=128)[:, :, tq * T4:(tq + 1) * T4]), 16)
                ld_c4[(op, tq)] = ld.n

                for ocs in range(4):
                    @block.tensor
                    def _(te, op=op, tq=tq, ocs=ocs, bi4=bi4, gi4=gi4):
                        te.wait_ge(ld.h, ld_c4[(op, tq)])
                        te.wait_ge(gd.h, gd_pan[op])
                        idx4 = gi4 * 4 + ocs
                        if idx4 >= 2:
                            te.wait_ge(sc.h, sc_y1[idx4 - 2])
                            te.wait_ge(vc.h, vc_at[idx4 - 2])
                        pj = (PC, PE_)[ocs % 2]
                        pw_ = (PD, PF)[ocs % 2]
                        for c in range(NCH):
                            te.matmul(pj[:, 0:T4],
                                      pwb[op % 2][:, c * 512 + ocs * 128:c * 512 + (ocs + 1) * 128],
                                      cmb4[bi4][:, c * T4:(c + 1) * T4],
                                      start=(c == 0), stop=(c == NCH - 1))
                        for c in range(NCH):
                            i = te.matmul(pw_[:, 0:T4],
                                          wob[op % 2][:, c * 512 + ocs * 128:c * 512 + (ocs + 1) * 128],
                                          ctx4[bi4][:, c * T4:(c + 1) * T4],
                                          start=(c == 0), stop=(c == NCH - 1))
                        tc.inc(i)
                    tc_w4[(op, tq, ocs)] = tc.n

                    @block.scalar
                    def _(s, op=op, tq=tq, ocs=ocs, bi4=bi4):
                        s.wait_ge(tc.h, tc_w4[(op, tq, ocs)])
                        gi4 = op * 8 + tq
                        idx4 = gi4 * 4 + ocs
                        if idx4 >= 8:
                            s.wait_ge(gd.h, ld_s4[idx4 // 4 - 2])
                        pj = (PC, PE_)[ocs % 2]
                        oc_g = op * 4 + ocs
                        sc.inc(s.add(out=y1s[bi4][:, ocs * T4:(ocs + 1) * T4], in_=pj[:, 0:T4],
                                     add=pb_sb[:, oc_g:oc_g + 1]))
                    sc_y1[(op * 8 + tq) * 4 + ocs] = sc.n

                    @block.vector
                    def _(v, op=op, tq=tq, ocs=ocs, bi4=bi4):
                        v.wait_ge(tc.h, tc_w4[(op, tq, ocs)])
                        idx4 = (op * 8 + tq) * 4 + ocs
                        if idx4 >= 8:
                            v.wait_ge(gd.h, ld_s4[idx4 // 4 - 2])
                        pw_ = (PD, PF)[ocs % 2]
                        vc.inc(v.tensor_copy(out=ats[bi4][:, ocs * T4:(ocs + 1) * T4],
                                             in_=pw_[:, 0:T4]))
                    vc_at[(op * 8 + tq) * 4 + ocs] = vc.n

                @block.gpsimd
                def _(gg, op=op, tq=tq, bi4=bi4, gi4=gi4):
                    gg.wait_ge(sc.h, sc_y1[gi4 * 4 + 3])
                    gg.wait_ge(vc.h, vc_at[gi4 * 4 + 3])
                    gd.inc(gg.dma_start(
                        out=y1_d.rearrange("(o p) t -> p o t", p=128)[:, op * 4:(op + 1) * 4, tq * T4:(tq + 1) * T4],
                        in_=y1s[bi4].rearrange("p (o t) -> p o t", o=4)), 16)
                    gd.inc(gg.dma_start(
                        out=attn_d.rearrange("(o p) t -> p o t", p=128)[:, op * 4:(op + 1) * 4, tq * T4:(tq + 1) * T4],
                        in_=ats[bi4].rearrange("p (o t) -> p o t", o=4)), 16)
                ld_s4[gi4] = gd.n

        p4a_ld = ld.n
        p4a_gd = gd.n
        ar.pop()

        ar.push()
        xo4 = ar.f32(NCH * TCH)
        y14 = ar.bf16(NCH * TCH)
        at4 = ar.bf16(NCH * TCH)
        yv = ar.f32(NCH * TCH)
        fin = ar.f32(NCH * TCH)
        sq4 = ar.bf16(TCH)
        m4 = ar.f32(TCH)
        r4 = ar.f32(TCH)
        mr4 = ar.f32(TCH)[0:1, :]
        rr4 = ar.f32(TCH)[0:1, :]
        ss4 = ar.f32(TCH)[0:1, :]
        rb4 = ar.f32(TCH)

        ld_b4, tc_b4, vc_b4, sc_b4, tc_ss, sc_rr, tc_rb, vc_fin = {}, {}, {}, {}, {}, {}, {}, {}

        for tq in range(4):
            @block.sync
            def _(s, tq=tq):
                if tq == 0:
                    s.wait_ge(ld.h, p4a_ld)
                    s.wait_ge(gd.h, p4a_gd)
                if tq >= 1:
                    s.wait_ge(oc.h, 16 * tq)
                ld.inc(s.dma_start(
                    out=xo4.rearrange("p (c t) -> p c t", c=NCH),
                    in_=x_own.rearrange("(c p) t -> p c t", p=128)[:, :, tq * TCH:(tq + 1) * TCH]), 16)
                ld.inc(s.dma_start(
                    out=y14.rearrange("p (c t) -> p c t", c=NCH),
                    in_=y1_d.rearrange("(c p) t -> p c t", p=128)[:, :, tq * TCH:(tq + 1) * TCH]), 16)
                ld.inc(s.dma_start(
                    out=at4.rearrange("p (c t) -> p c t", c=NCH),
                    in_=attn_d.rearrange("(c p) t -> p c t", p=128)[:, :, tq * TCH:(tq + 1) * TCH]), 16)
                ld.inc(s.dma_start(out=mr4[:], in_=rowso_d[0:1, tq * TCH:(tq + 1) * TCH]), 16)
                ld.inc(s.dma_start(out=rr4[:], in_=rowso_d[1:2, tq * TCH:(tq + 1) * TCH]), 16)
                s.wait_ge(ld.h, ld.n)
            ld_b4[tq] = ld.n

            @block.tensor
            def _(te, tq=tq):
                te.wait_ge(ld.h, ld_b4[tq])
                if tq >= 1:
                    te.wait_ge(vc.h, vc_b4[tq - 1])
                te.matmul(PC[:, 0:TCH], ones_rf[0:1, :], mr4[0:1, :], start=True, stop=True)
                tc.inc(te.matmul(PD[:, 0:TCH], ones_rf[0:1, :], rr4[0:1, :], start=True, stop=True))
            tc_b4[tq] = tc.n

            @block.vector
            def _(v, tq=tq):
                v.wait_ge(tc.h, tc_b4[tq])
                v.tensor_copy(out=m4, in_=PC[:, 0:TCH])
                v.tensor_copy(out=r4, in_=PD[:, 0:TCH])
                v.drain()
                for c in range(NCH):
                    v.tensor_tensor(out=yv[:, c * TCH:(c + 1) * TCH],
                                    in0=xo4[:, c * TCH:(c + 1) * TCH], in1=m4, op=ALU.subtract)
                v.drain()
                for c in range(NCH):
                    v.tensor_tensor(out=yv[:, c * TCH:(c + 1) * TCH],
                                    in0=yv[:, c * TCH:(c + 1) * TCH], in1=r4, op=ALU.mult)
                v.drain()
                for c in range(NCH):
                    i = v.tensor_tensor(out=yv[:, c * TCH:(c + 1) * TCH],
                                        in0=yv[:, c * TCH:(c + 1) * TCH],
                                        in1=y14[:, c * TCH:(c + 1) * TCH], op=ALU.add)
                vc.inc(i)
            vc_b4[tq] = vc.n

            for c in range(NCH):
                @block.scalar
                def _(s, tq=tq, c=c):
                    if c == 0:
                        s.wait_ge(vc.h, vc_b4[tq])
                    if c >= 1:
                        s.wait_ge(tc.h, tc_ss.get((tq, c - 1), 0))
                    sc.inc(s.activation(out=sq4, in_=yv[:, c * TCH:(c + 1) * TCH],
                                        func=AF.Square))
                sc_b4[(tq, c)] = sc.n

                @block.tensor
                def _(te, tq=tq, c=c):
                    te.wait_ge(sc.h, sc_b4[(tq, c)])
                    if tq >= 1 and c == 0:
                        te.wait_ge(sc.h, sc_rr[tq - 1])
                    tc.inc(te.matmul(PE_[0:1, 0:TCH], ones_cb[:, 0:1], sq4,
                                     start=(c == 0), stop=(c == NCH - 1)))
                tc_ss[(tq, c)] = tc.n

            @block.scalar
            def _(s, tq=tq):
                s.wait_ge(tc.h, tc_ss[(tq, NCH - 1)])
                s.activation(out=ss4[:], in_=PE_[0:1, 0:TCH], func=AF.Ln, scale=1.0 / D,
                             bias=epsrm_c[:])
                s.drain()
                sc.inc(s.activation(out=ss4[:], in_=ss4[:], func=AF.Exp, scale=-0.5))
            sc_rr[tq] = sc.n

            @block.tensor
            def _(te, tq=tq):
                te.wait_ge(sc.h, sc_rr[tq])
                if tq >= 1:
                    te.wait_ge(vc.h, vc_fin[tq - 1])
                tc.inc(te.matmul(PF[:, 0:TCH], ones_rf[0:1, :], ss4[0:1, :],
                                 start=True, stop=True))
            tc_rb[tq] = tc.n

            @block.vector
            def _(v, tq=tq):
                v.wait_ge(tc.h, tc_rb[tq])
                v.tensor_copy(out=rb4, in_=PF[:, 0:TCH])
                v.drain()
                for c in range(NCH):
                    v.scalar_tensor_tensor(out=yv[:, c * TCH:(c + 1) * TCH],
                                           in0=yv[:, c * TCH:(c + 1) * TCH],
                                           scalar=nw_sb[:, c:c + 1], in1=rb4,
                                           op0=ALU.mult, op1=ALU.mult)
                for c in range(NCH):
                    v.tensor_tensor(out=fin[:, c * TCH:(c + 1) * TCH],
                                    in0=xo4[:, c * TCH:(c + 1) * TCH],
                                    in1=at4[:, c * TCH:(c + 1) * TCH], op=ALU.add)
                v.drain()
                for c in range(NCH):
                    i = v.tensor_tensor(out=fin[:, c * TCH:(c + 1) * TCH],
                                        in0=fin[:, c * TCH:(c + 1) * TCH],
                                        in1=yv[:, c * TCH:(c + 1) * TCH], op=ALU.add)
                vc.inc(i)
            vc_fin[tq] = vc.n

            @block.sync
            def _(s, tq=tq):
                s.wait_ge(vc.h, vc_fin[tq])
                oc.inc(s.dma_start(
                    out=out_d.rearrange("(c p) t -> p c t", p=128)[:, :, tq * TCH:(tq + 1) * TCH],
                    in_=fin.rearrange("p (c t) -> p c t", c=NCH)), 16)

        @block.sync
        def _(s):
            s.wait_ge(oc.h, oc.n)
        ar.pop()

    return nc


_NC_CACHE = {}


def _get_nc():
    if "nc" not in _NC_CACHE:
        _NC_CACHE["nc"] = build_nc()
    return _NC_CACHE["nc"]


def _rope_tables():
    half = 64
    inv_freq = 1.0 / ROPE_BASE ** (np.arange(0, 128, 2, dtype=np.float64) / 128.0)
    t = np.arange(S, dtype=np.float64)
    freqs = np.outer(t, inv_freq)
    cos = np.cos(freqs).astype(np.float32).T
    sin = np.sin(freqs).astype(np.float32).T
    cos_dup = np.concatenate([cos, cos], axis=0).astype(BFD)
    sin_sgn = np.concatenate([-sin, sin], axis=0).astype(BFD)
    return cos_dup, sin_sgn


def _diag_masks():
    k = np.arange(128)[:, None]
    q = np.arange(QM)[None, :]
    ms = [(q >= (128 * j + k)).astype(np.float32) for j in range(4)]
    return np.concatenate(ms, axis=1).astype(BFD)


def kernel(L0_attn_hidden_states, L0_attn_Wq, L0_attn_Wk, L0_attn_Wv, L0_attn_Wo,
           L0_ffn_router_w, L0_ffn_router_b, L0_ffn_expert_w, L0_ffn_expert_b,
           L0_ffn_proj_w, L0_ffn_proj_b, L0_ffn_norm_w):
    x = np.asarray(L0_attn_hidden_states, dtype=np.float32)
    cos_dup, sin_sgn = _rope_tables()
    dmask = _diag_masks()
    ident = np.eye(128, dtype=np.float32)
    sel = np.zeros((8, E * 128), dtype=BFD)
    for e in range(E):
        sel[e, e * 128:(e + 1) * 128] = 1

    wq = np.ascontiguousarray(np.asarray(L0_attn_Wq, np.float32).T)
    wk = np.ascontiguousarray(np.asarray(L0_attn_Wk, np.float32).T)
    wv = np.ascontiguousarray(np.asarray(L0_attn_Wv, np.float32).T)
    wo = np.ascontiguousarray(np.asarray(L0_attn_Wo, np.float32).T)
    rw = np.ascontiguousarray(np.asarray(L0_ffn_router_w, np.float32))
    rb = np.ascontiguousarray(np.asarray(L0_ffn_router_b, np.float32)).reshape(1, E)
    ew = np.ascontiguousarray(np.asarray(L0_ffn_expert_w, np.float32))
    eb = np.ascontiguousarray(
        np.asarray(L0_ffn_expert_b, np.float32).reshape(E, 16, 128).transpose(2, 0, 1).reshape(128, E * 16))
    pw = np.ascontiguousarray(np.asarray(L0_ffn_proj_w, np.float32))
    pb = np.ascontiguousarray(np.asarray(L0_ffn_proj_b, np.float32).reshape(16, 128).T)
    nw = np.ascontiguousarray(np.asarray(L0_ffn_norm_w, np.float32).reshape(16, 128).T)

    in_maps = []
    for c in range(8):
        b = c // 2
        hh = c % 2
        x_fm = np.ascontiguousarray(x[b].T)
        x_ownc = np.ascontiguousarray(x_fm[:, hh * OWN:(hh + 1) * OWN])
        in_maps.append({
            "x_fm": x_fm,
            "x_own": x_ownc,
            "wqT": np.ascontiguousarray(wq[:, hh * 1024:(hh + 1) * 1024]),
            "wkT": np.ascontiguousarray(wk[:, hh * 1024:(hh + 1) * 1024]),
            "wvT": np.ascontiguousarray(wv[:, hh * 1024:(hh + 1) * 1024]),
            "woT": wo,
            "rw_d": rw, "rb_d": rb, "ew_d": ew, "eb_d": eb,
            "pw_d": pw, "pb_d": pb, "nw_d": nw,
            "cos_d": cos_dup, "sin_d": sin_sgn, "msk_d": dmask,
            "ident_d": ident,
            "sel_d": sel,
        })

    nc = _get_nc()
    res = run_bass_kernel_spmd(nc, in_maps, core_ids=list(range(8)))

    out = np.empty((B, S, D), dtype=np.float32)
    for c in range(8):
        b = c // 2
        hh = c % 2
        out[b, hh * OWN:(hh + 1) * OWN, :] = res.results[c]["out_d"].T
    return out


# version 06
# speedup vs baseline: 1.0855x; 1.0449x over previous
import contextlib
import math

import numpy as np
import ml_dtypes

import concourse.bass as bass
import concourse.mybir as mybir
from concourse.bass_utils import run_bass_kernel_spmd

F32 = mybir.dt.float32
BF16 = mybir.dt.bfloat16
AF = mybir.ActivationFunctionType
ALU = mybir.AluOpType
AXX = mybir.AxisListType.X

B, S, D = 4, 4096, 2048
NCH = 16
TCH = 512
NT = S // TCH
G = 2
NGRP = 4
QM = 512
NQM = S // QM
E = 8
OWN = 2048
EBIAS = -12.0
ISQD = 1.0 / math.sqrt(128.0)
ROPE_BASE = 10000.0

BFD = np.dtype(ml_dtypes.bfloat16)


class Ctr:
    def __init__(self, es, nc, name):
        self.h = es.enter_context(nc.semaphore(name))
        self.n = 0

    def inc(self, instr, k=1):
        instr.then_inc(self.h, k)
        self.n += k
        return self.n


class Arena:
    def __init__(self, t, nelem):
        self.t, self.nelem, self.off, self.marks = t, nelem, 0, []

    def push(self):
        self.marks.append(self.off)

    def pop(self):
        self.off = self.marks.pop()

    def bf16(self, n):
        n = (n + 511) & ~511
        ap = self.t[:, self.off:self.off + n]
        self.off += n
        assert self.off <= self.nelem, f"arena overflow {self.off} > {self.nelem}"
        return ap

    def f32(self, n):
        return self.bf16(2 * n).bitcast(F32)


def build_nc():
    nc = bass.Bass("TRN2", target_bir_lowering=False)

    x_fm = nc.dram_tensor("x_fm", [D, S], F32, kind="ExternalInput").ap()
    x_own = nc.dram_tensor("x_own", [D, OWN], F32, kind="ExternalInput").ap()
    wqT = nc.dram_tensor("wqT", [D, 1024], F32, kind="ExternalInput").ap()
    wkT = nc.dram_tensor("wkT", [D, 1024], F32, kind="ExternalInput").ap()
    wvT = nc.dram_tensor("wvT", [D, 1024], F32, kind="ExternalInput").ap()
    woT = nc.dram_tensor("woT", [D, D], F32, kind="ExternalInput").ap()
    rw_d = nc.dram_tensor("rw_d", [D, E], F32, kind="ExternalInput").ap()
    rb_d = nc.dram_tensor("rb_d", [1, E], F32, kind="ExternalInput").ap()
    ew_d = nc.dram_tensor("ew_d", [E, D, D], F32, kind="ExternalInput").ap()
    eb_d = nc.dram_tensor("eb_d", [128, E * NCH], F32, kind="ExternalInput").ap()
    pw_d = nc.dram_tensor("pw_d", [D, D], F32, kind="ExternalInput").ap()
    pb_d = nc.dram_tensor("pb_d", [128, NCH], F32, kind="ExternalInput").ap()
    nw_d = nc.dram_tensor("nw_d", [128, NCH], F32, kind="ExternalInput").ap()
    cos_d = nc.dram_tensor("cos_d", [128, S], BF16, kind="ExternalInput").ap()
    sin_d = nc.dram_tensor("sin_d", [128, S], BF16, kind="ExternalInput").ap()
    msk_d = nc.dram_tensor("msk_d", [128, 2 * 1024], BF16, kind="ExternalInput").ap()
    ident_d = nc.dram_tensor("ident_d", [128, 128], F32, kind="ExternalInput").ap()
    sel_d = nc.dram_tensor("sel_d", [8, E * 128], BF16, kind="ExternalInput").ap()
    out_d = nc.dram_tensor("out_d", [D, OWN], F32, kind="ExternalOutput").ap()

    xn_d = nc.dram_tensor("xn_d", [D, S], BF16).ap()
    xno_d = nc.dram_tensor("xno_d", [D, OWN], BF16).ap()
    rows_d = nc.dram_tensor("rows_d", [2, S], F32).ap()
    rowso_d = nc.dram_tensor("rowso_d", [2, OWN], F32).ap()
    ctxo_d = nc.dram_tensor("ctxo_d", [D, OWN], BF16).ap()
    comb_d = nc.dram_tensor("comb_d", [D, OWN], BF16).ap()
    y1_d = nc.dram_tensor("y1_d", [D, OWN], BF16).ap()
    attn_d = nc.dram_tensor("attn_d", [D, OWN], BF16).ap()
    cin_t = nc.dram_tensor("cin", [8 * 128, S], BF16)
    cout_t = nc.dram_tensor("cout", [16 * 128, S], BF16)
    cin = cin_t.ap()
    cout = cout_t.ap()
    RG = [[0, 1], [2, 3], [4, 5], [6, 7]]

    es = contextlib.ExitStack()
    AR_N = 90 * 1024
    art = es.enter_context(nc.sbuf_tensor("arena", [128, AR_N], BF16))
    ar = Arena(art, AR_N)

    ones_cb = es.enter_context(nc.sbuf_tensor("ones_cb", [128, 1], BF16))
    ones_rf = es.enter_context(nc.sbuf_tensor("ones_rf", [1, 128], F32))
    ident = es.enter_context(nc.sbuf_tensor("ident", [128, 128], F32))
    ebias_c = es.enter_context(nc.sbuf_tensor("ebias_c", [128, 1], F32))
    epsln_c = es.enter_context(nc.sbuf_tensor("epsln_c", [1, 1], F32))
    epsrm_c = es.enter_context(nc.sbuf_tensor("epsrm_c", [1, 1], F32))
    masks = es.enter_context(nc.sbuf_tensor("masks", [128, 2 * 1024], BF16))
    eb_sb = es.enter_context(nc.sbuf_tensor("eb_sb", [128, E * NCH], F32))
    pb_sb = es.enter_context(nc.sbuf_tensor("pb_sb", [128, NCH], F32))
    nw_sb = es.enter_context(nc.sbuf_tensor("nw_sb", [128, NCH], F32))
    rb_sb = es.enter_context(nc.sbuf_tensor("rb_sb", [1, E], F32))
    rw_sb = es.enter_context(nc.sbuf_tensor("rw_sb", [128, NCH * E], BF16))
    wrow_sb = es.enter_context(nc.sbuf_tensor("wrow_sb", [8, OWN], BF16))
    sel_sb = es.enter_context(nc.sbuf_tensor("sel_sb", [8, E * 128], BF16))
    mrow = es.enter_context(nc.sbuf_tensor("mrow", [1, TCH], F32))
    srow = es.enter_context(nc.sbuf_tensor("srow", [1, TCH], F32))
    vrow = es.enter_context(nc.sbuf_tensor("vrow", [1, TCH], F32))
    rrow = es.enter_context(nc.sbuf_tensor("rrow", [1, TCH], F32))
    linv = es.enter_context(nc.sbuf_tensor("linv", [1, QM], F32))
    lbc = es.enter_context(nc.sbuf_tensor("lbc", [128, QM], F32))
    lgt = es.enter_context(nc.sbuf_tensor("lgt", [128, 8], F32))
    rm1 = es.enter_context(nc.sbuf_tensor("rm1", [128, 1], F32))
    rm2 = es.enter_context(nc.sbuf_tensor("rm2", [128, 1], F32))
    req1 = es.enter_context(nc.sbuf_tensor("req1", [128, 8], F32))
    req2 = es.enter_context(nc.sbuf_tensor("req2", [128, 8], F32))
    rmsk = es.enter_context(nc.sbuf_tensor("rmsk", [128, 8], F32))
    rw1 = es.enter_context(nc.sbuf_tensor("rw1", [128, 1], F32))
    rw2 = es.enter_context(nc.sbuf_tensor("rw2", [128, 1], F32))
    rd12 = es.enter_context(nc.sbuf_tensor("rd12", [128, 1], F32))
    rfw = es.enter_context(nc.sbuf_tensor("rfw", [128, 8], F32))
    rfw1 = es.enter_context(nc.sbuf_tensor("rfw1", [128, 8], F32))

    PAB = es.enter_context(nc.psum_tensor("PAB", [128, 2048], F32))
    PA = PAB[:, 0:1024]
    PB = PAB[:, 1024:2048]
    PC = es.enter_context(nc.psum_tensor("PC", [128, 512], F32))
    PD = es.enter_context(nc.psum_tensor("PD", [128, 512], F32))
    PE_ = es.enter_context(nc.psum_tensor("PE", [128, 512], F32))
    PF = es.enter_context(nc.psum_tensor("PF", [128, 512], F32))

    with es:
        nc.cache_partition_id()
        block = es.enter_context(nc.Block())

        ld = Ctr(es, nc, "ld")
        gd = Ctr(es, nc, "gd")
        tc = Ctr(es, nc, "tc")
        sc = Ctr(es, nc, "sc")
        vc = Ctr(es, nc, "vc")
        cc = Ctr(es, nc, "cc")
        oc = Ctr(es, nc, "oc")

        def toko(eng, key=None):
            return (eng.partition_id() % 2) * OWN

        @block.vector
        def _(v):
            v.memset(ones_cb[:], 1.0)
            v.memset(ones_rf[:], 1.0)
            v.memset(ebias_c[:], EBIAS)
            v.memset(epsln_c[:], 1e-5)
            i = v.memset(epsrm_c[:], 1e-6)
            vc.inc(i)
        v_init = vc.n

        @block.sync
        def _(s):
            ld.inc(s.dma_start(out=masks[:], in_=msk_d[:, :]), 16)
            ld.inc(s.dma_start(out=rb_sb[:], in_=rb_d[:, :]), 16)
            ld.inc(s.dma_start(out=eb_sb[:], in_=eb_d[:, :]), 16)
            ld.inc(s.dma_start(out=ident[:], in_=ident_d[:, :]), 16)
            ld.inc(s.dma_start(out=pb_sb[:], in_=pb_d[:, :]), 16)
            ld.inc(s.dma_start(out=nw_sb[:], in_=nw_d[:, :]), 16)
            ld.inc(s.dma_start(out=sel_sb[:], in_=sel_d[:, :]), 16)
            s.wait_ge(ld.h, ld.n)
        ld_init = ld.n

        @block.gpsimd
        def _(g):
            gd.inc(g.dma_start(out=rw_sb[:].rearrange("p (c e) -> p c e", c=NCH),
                               in_=rw_d.rearrange("(c p) e -> p c e", p=128)), 16)
        gd_init = gd.n

        ar.push()
        cosb = ar.bf16(S)
        sinb = ar.bf16(S)
        ar.push()
        xf32 = [ar.f32(NCH * TCH) for _ in range(2)]
        xbf = [ar.bf16(NCH * TCH) for _ in range(2)]
        xsq = ar.bf16(NCH * TCH)
        xnb = ar.bf16(NCH * TCH)
        mbc = ar.f32(TCH)
        rbc = ar.f32(TCH)

        @block.sync
        def _(s):
            ld.inc(s.dma_start(out=cosb, in_=cos_d[:, :]), 16)
            ld.inc(s.dma_start(out=sinb, in_=sin_d[:, :]), 16)

        ld_x, gd_x, sc_sq, tc_st, vc_rows, sc_rs, tc_bc, vc_cp, vc_xn, ld_sto = \
            {}, {}, {}, {}, {}, {}, {}, {}, {}, {}

        for t in range(NT):
            bi = t % 2
            t0 = t * TCH

            @block.sync
            def _(s, t=t, bi=bi, t0=t0):
                if t >= 2:
                    s.wait_ge(vc.h, vc_xn[t - 2])
                ld.inc(s.dma_start(
                    out=xf32[bi].rearrange("p (c t) -> p c t", c=NCH),
                    in_=x_fm.rearrange("(c p) t -> p c t", p=128)[:, :, t0:t0 + TCH]), 16)
            ld_x[t] = ld.n

            @block.gpsimd
            def _(g, t=t, bi=bi, t0=t0):
                if t >= 2:
                    g.wait_ge(tc.h, tc_st[t - 2])
                gd.inc(g.dma_start(
                    out=xbf[bi].rearrange("p (c t) -> p c t", c=NCH),
                    in_=x_fm.rearrange("(c p) t -> p c t", p=128)[:, :, t0:t0 + TCH]), 16)
            gd_x[t] = gd.n

            @block.scalar
            def _(s, t=t, bi=bi):
                s.wait_ge(gd.h, gd_x[t])
                if t >= 1:
                    s.wait_ge(tc.h, tc_st[t - 1])
                sc.inc(s.activation(out=xsq, in_=xbf[bi], func=AF.Square))
            sc_sq[t] = sc.n

            @block.tensor
            def _(te, t=t, bi=bi):
                te.wait_ge(sc.h, sc_sq[t])
                if t >= 1:
                    te.wait_ge(vc.h, vc_rows[t - 1])
                for c in range(NCH):
                    te.matmul(PE_[0:1, 0:TCH], ones_cb[:, 0:1],
                              xbf[bi][:, c * TCH:(c + 1) * TCH],
                              start=(c == 0), stop=(c == NCH - 1))
                for c in range(NCH):
                    i = te.matmul(PF[0:1, 0:TCH], ones_cb[:, 0:1],
                                  xsq[:, c * TCH:(c + 1) * TCH],
                                  start=(c == 0), stop=(c == NCH - 1))
                tc.inc(i)
            tc_st[t] = tc.n

            @block.vector
            def _(v, t=t):
                v.wait_ge(tc.h, tc_st[t])
                if t >= 1:
                    v.wait_ge(ld.h, ld_sto[t - 1])
                v.tensor_scalar(out=mrow[:], in0=PE_[0:1, 0:TCH], scalar1=1.0 / D,
                                scalar2=None, op0=ALU.mult)
                v.tensor_scalar(out=srow[:], in0=PF[0:1, 0:TCH], scalar1=1.0 / D,
                                scalar2=None, op0=ALU.mult)
                v.drain()
                v.tensor_tensor(out=vrow[:], in0=mrow[:], in1=mrow[:], op=ALU.mult)
                v.drain()
                i = v.tensor_tensor(out=vrow[:], in0=srow[:], in1=vrow[:], op=ALU.subtract)
                vc.inc(i)
            vc_rows[t] = vc.n

            @block.scalar
            def _(s, t=t):
                s.wait_ge(vc.h, vc_rows[t])
                s.activation(out=rrow[:], in_=vrow[:], func=AF.Ln, bias=epsln_c[:])
                s.drain()
                sc.inc(s.activation(out=rrow[:], in_=rrow[:], func=AF.Exp, scale=-0.5))
            sc_rs[t] = sc.n

            @block.tensor
            def _(te, t=t):
                te.wait_ge(sc.h, sc_rs[t])
                if t >= 1:
                    te.wait_ge(vc.h, vc_cp[t - 1])
                te.matmul(PC[:, 0:TCH], ones_rf[0:1, :], mrow[0:1, :], start=True, stop=True)
                i = te.matmul(PD[:, 0:TCH], ones_rf[0:1, :], rrow[0:1, :], start=True, stop=True)
                tc.inc(i)
            tc_bc[t] = tc.n

            @block.vector
            def _(v, t=t, bi=bi):
                v.wait_ge(tc.h, tc_bc[t])
                v.wait_ge(ld.h, ld_x[t])
                if t >= 1:
                    v.wait_ge(ld.h, ld_sto[t - 1])
                v.tensor_copy(out=mbc, in_=PC[:, 0:TCH])
                i = v.tensor_copy(out=rbc, in_=PD[:, 0:TCH])
                vc.inc(i)
                v.drain()
                for c in range(NCH):
                    v.tensor_tensor(out=xf32[bi][:, c * TCH:(c + 1) * TCH],
                                    in0=xf32[bi][:, c * TCH:(c + 1) * TCH], in1=mbc,
                                    op=ALU.subtract)
                v.drain()
                for c in range(NCH):
                    i = v.tensor_tensor(out=xnb[:, c * TCH:(c + 1) * TCH],
                                        in0=xf32[bi][:, c * TCH:(c + 1) * TCH], in1=rbc,
                                        op=ALU.mult)
                vc.inc(i)
            vc_cp[t] = vc.n - 1
            vc_xn[t] = vc.n

            @block.sync
            def _(s, t=t, t0=t0):
                s.wait_ge(vc.h, vc_xn[t])
                ld.inc(s.dma_start(
                    out=xn_d.rearrange("(c p) t -> p c t", p=128)[:, :, t0:t0 + TCH],
                    in_=xnb.rearrange("p (c t) -> p c t", c=NCH)), 16)
                ld.inc(s.dma_start(out=rows_d[0:1, t0:t0 + TCH], in_=mrow[:]), 16)
                ld.inc(s.dma_start(out=rows_d[1:2, t0:t0 + TCH], in_=rrow[:]), 16)
                s.wait_ge(ld.h, ld.n)
            ld_sto[t] = ld.n

        p0_ld = ld.n
        ar.pop()

        ar.push()
        xnp = [ar.bf16(NCH * TCH) for _ in range(2)]
        qraw = ar.bf16(G * S)
        kraw = ar.bf16(G * S)
        qfm = ar.bf16(G * S)
        kfm = ar.bf16(G * S)
        vtm = ar.bf16(G * S)
        xsw = ar.bf16(S)
        t2s = ar.bf16(S)
        wq_sb = ar.bf16(NCH * G * 128)
        wk_sb = ar.bf16(NCH * G * 128)
        wv_sb = ar.bf16(NCH * G * 128)
        pt = [ar.bf16(2048) for _ in range(2)]
        ctxo = [ar.bf16(QM) for _ in range(2)]

        @block.gpsimd
        def _(gg):
            tko = toko(gg)
            gg.wait_ge(ld.h, p0_ld)
            gd.inc(gg.dma_start(out=xno_d[:, :], in_=xn_d[:, bass.ds(tko, OWN)]), 16)
            gd.inc(gg.dma_start(out=rowso_d[:, :], in_=rows_d[:, bass.ds(tko, OWN)]), 16)
            gg.wait_ge(gd.h, gd.n)
        gd_own = gd.n

        gd_w, ld_xn1, tc_proj, vc_evac, vc_rope, ld_sw = {}, {}, {}, {}, {}, {}
        ld_cst = []
        v_ctx_list = []
        grp_end_vc = {}
        fin_idx = 0
        last_pe = None

        for g in range(NGRP):
            o0 = g * G * 128

            @block.gpsimd
            def _(gg, g=g, o0=o0):
                if g >= 1:
                    gg.wait_ge(tc.h, tc_proj[(g - 1, NT - 1)])
                for w_sb, w_dd in ((wq_sb, wqT), (wk_sb, wkT), (wv_sb, wvT)):
                    gd.inc(gg.dma_start(
                        out=w_sb.rearrange("p (c o) -> p c o", c=NCH),
                        in_=w_dd.rearrange("(c p) o -> p c o", p=128)[:, :, o0:o0 + G * 128]), 16)
            gd_w[g] = gd.n

            for t in range(NT):
                bi = t % 2
                t0 = t * TCH

                @block.sync
                def _(s, g=g, t=t, bi=bi, t0=t0):
                    if g == 0 and t == 0:
                        s.wait_ge(ld.h, p0_ld)
                    gi = g * NT + t
                    if gi >= 2:
                        pg, pt_ = divmod(gi - 2, NT)
                        s.wait_ge(tc.h, tc_proj[(pg, pt_)])
                    ld.inc(s.dma_start(
                        out=xnp[bi].rearrange("p (c t) -> p c t", c=NCH),
                        in_=xn_d.rearrange("(c p) t -> p c t", p=128)[:, :, t0:t0 + TCH]), 16)
                ld_xn1[(g, t)] = ld.n

                @block.tensor
                def _(te, g=g, t=t, bi=bi):
                    te.wait_ge(ld.h, ld_xn1[(g, t)])
                    te.wait_ge(gd.h, gd_w[g])
                    if t > 0:
                        te.wait_ge(vc.h, vc_evac[(g, t - 1)])
                    elif g > 0:
                        te.wait_ge(vc.h, grp_end_vc[g - 1])
                    for oc in range(G):
                        for c in range(NCH):
                            te.matmul(PA[:, oc * 512:oc * 512 + TCH],
                                      wq_sb[:, c * G * 128 + oc * 128:c * G * 128 + (oc + 1) * 128],
                                      xnp[bi][:, c * TCH:(c + 1) * TCH],
                                      start=(c == 0), stop=(c == NCH - 1))
                    for oc in range(G):
                        for c in range(NCH):
                            te.matmul(PB[:, oc * 512:oc * 512 + TCH],
                                      wk_sb[:, c * G * 128 + oc * 128:c * G * 128 + (oc + 1) * 128],
                                      xnp[bi][:, c * TCH:(c + 1) * TCH],
                                      start=(c == 0), stop=(c == NCH - 1))
                    for ts in range(4):
                        pp = (PC, PD, PE_, PF)[ts]
                        for c in range(NCH):
                            i = te.matmul(pp[:, 0:G * 128],
                                          xnp[bi][:, c * TCH + ts * 128:c * TCH + (ts + 1) * 128],
                                          wv_sb[:, c * G * 128:(c + 1) * G * 128],
                                          start=(c == 0), stop=(c == NCH - 1))
                    tc.inc(i)
                tc_proj[(g, t)] = tc.n

                @block.vector
                def _(v, g=g, t=t, t0=t0):
                    v.wait_ge(tc.h, tc_proj[(g, t)])
                    for ocp in range(G):
                        v.tensor_copy(out=qraw[:, ocp * S + t0:ocp * S + t0 + TCH],
                                      in_=PA[:, ocp * 512:ocp * 512 + TCH])
                        v.tensor_copy(out=kraw[:, ocp * S + t0:ocp * S + t0 + TCH],
                                      in_=PB[:, ocp * 512:ocp * 512 + TCH])
                    for ts in range(4):
                        pp = (PC, PD, PE_, PF)[ts]
                        cg = t * 4 + ts
                        for h in range(G):
                            i = v.tensor_copy(
                                out=vtm[:, h * S + cg * 128:h * S + (cg + 1) * 128],
                                in_=pp[:, h * 128:(h + 1) * 128])
                    vc.inc(i)
                vc_evac[(g, t)] = vc.n

            for pi, (raw, fm) in enumerate(((qraw, qfm), (kraw, kfm))):
                for h in range(G):
                    @block.sync
                    def _(s, raw=raw, h=h, g=g, pi=pi):
                        s.wait_ge(vc.h, vc_evac[(g, NT - 1)])
                        if (g, pi, h) != (0, 0, 0):
                            s.wait_ge(vc.h, vc.n)
                        ld.inc(s.dma_start(out=xsw[0:64, :],
                                           in_=raw[64:128, h * S:(h + 1) * S]), 16)
                        ld.inc(s.dma_start(out=xsw[64:128, :],
                                           in_=raw[0:64, h * S:(h + 1) * S]), 16)
                        s.wait_ge(ld.h, ld.n)
                    ld_sw[(g, pi, h)] = ld.n

                    @block.vector
                    def _(v, raw=raw, fm=fm, h=h, g=g, pi=pi):
                        v.wait_ge(ld.h, ld_sw[(g, pi, h)])
                        v.tensor_tensor(out=xsw, in0=xsw, in1=sinb, op=ALU.mult)
                        v.tensor_tensor(out=t2s, in0=raw[:, h * S:(h + 1) * S], in1=cosb,
                                        op=ALU.mult)
                        v.drain()
                        i = v.tensor_tensor(out=fm[:, h * S:(h + 1) * S], in0=xsw, in1=t2s,
                                            op=ALU.add)
                        vc.inc(i)
                    vc_rope[(g, pi, h)] = vc.n

            for hg in range(G):
                for qm in range(NQM):
                    q0 = qm * QM
                    npair = qm + 1

                    pr_sc, pr_pe, pr_pm, pr_pv = {}, {}, {}, {}
                    tq, sq_, vq = tc.n, sc.n, vc.n
                    for j in range(npair):
                        tq += 1
                        pr_sc[j] = tq
                        sq_ += 1
                        pr_pe[j] = sq_
                        if j == npair - 1:
                            vq += 1
                            pr_pm[j] = vq
                        tq += 1
                        pr_pv[j] = tq

                    @block.tensor
                    def _(te, g=g, hg=hg, qm=qm, npair=npair, q0=q0,
                          pr_pe=pr_pe, pr_pm=pr_pm, fin_idx=fin_idx, last_pe=last_pe):
                        te.wait_ge(vc.h, vc_rope[(g, 0, hg)])
                        te.wait_ge(vc.h, vc_rope[(g, 1, hg)])
                        if fin_idx > 0:
                            te.wait_ge(vc.h, v_ctx_list[fin_idx - 1])
                        for j in range(npair):
                            b = j % 2
                            if j >= 1:
                                te.wait_ge(sc.h, pr_pe[j - 1])
                            elif last_pe is not None:
                                te.wait_ge(sc.h, last_pe)
                            for c2 in range(4):
                                kc = j * 4 + c2
                                i = te.matmul(PAB[:, c2 * 512:c2 * 512 + QM],
                                              kfm[:, hg * S + kc * 128:hg * S + (kc + 1) * 128],
                                              qfm[:, hg * S + q0:hg * S + q0 + QM],
                                              start=True, stop=True)
                            tc.inc(i)
                            if j == npair - 1:
                                te.wait_ge(vc.h, pr_pm[j])
                            else:
                                te.wait_ge(sc.h, pr_pe[j])
                            for c2 in range(4):
                                kc = j * 4 + c2
                                te.matmul(PC[:, 0:QM],
                                          vtm[:, hg * S + kc * 128:hg * S + (kc + 1) * 128],
                                          pt[b][:, c2 * 512:c2 * 512 + QM],
                                          start=(j == 0 and c2 == 0),
                                          stop=(j == npair - 1 and c2 == 3))
                                i = te.matmul(PD[0:1, 0:QM], ones_cb[:, 0:1],
                                              pt[b][:, c2 * 512:c2 * 512 + QM],
                                              start=(j == 0 and c2 == 0),
                                              stop=(j == npair - 1 and c2 == 3))
                            tc.inc(i)
                    assert tc.n == pr_pv[npair - 1], (tc.n, pr_pv[npair - 1])

                    @block.scalar
                    def _(s, npair=npair, pr_sc=pr_sc, pr_pv=pr_pv):
                        for j in range(npair):
                            b = j % 2
                            s.wait_ge(tc.h, pr_sc[j])
                            if j >= 2:
                                s.wait_ge(tc.h, pr_pv[j - 2])
                            sc.inc(s.activation(out=pt[b][:], in_=PAB[:, 0:2048],
                                                func=AF.Exp, scale=ISQD, bias=ebias_c[:]))
                    assert sc.n == pr_pe[npair - 1]
                    last_pe = pr_pe[npair - 1]

                    @block.vector
                    def _(v, npair=npair, pr_pe=pr_pe):
                        j = npair - 1
                        b = j % 2
                        v.wait_ge(sc.h, pr_pe[j])
                        vc.inc(v.tensor_tensor(out=pt[b][:], in0=pt[b][:],
                                               in1=masks[:, 0:2048], op=ALU.mult))
                    assert vc.n == pr_pm[npair - 1]

                    @block.vector
                    def _(v, pr_pv=pr_pv, npair=npair):
                        v.wait_ge(tc.h, pr_pv[npair - 1])
                        vc.inc(v.reciprocal(out=linv[:], in_=PD[0:1, 0:QM]))
                    v_linv = vc.n

                    @block.tensor
                    def _(te, v_linv=v_linv):
                        te.wait_ge(vc.h, v_linv)
                        tc.inc(te.matmul(PE_[:, 0:QM], ones_rf[0:1, :], linv[0:1, :],
                                         start=True, stop=True))
                    t_lb = tc.n

                    @block.vector
                    def _(v, t_lb=t_lb, fin_idx=fin_idx):
                        bo = fin_idx % 2
                        v.wait_ge(tc.h, t_lb)
                        if fin_idx >= 2:
                            v.wait_ge(ld.h, ld_cst[fin_idx - 2])
                        v.tensor_copy(out=lbc[:], in_=PE_[:, 0:QM])
                        v.drain()
                        vc.inc(v.tensor_tensor(out=ctxo[bo][:], in0=PC[:, 0:QM],
                                               in1=lbc[:], op=ALU.mult))
                    v_ctx_list.append(vc.n)

                    @block.sync
                    def _(s, g=g, hg=hg, qm=qm, q0=q0, fin_idx=fin_idx):
                        bo = fin_idx % 2
                        s.wait_ge(vc.h, v_ctx_list[fin_idx])
                        hidx = g * G + hg
                        ld.inc(s.dma_start(out=cin[hidx * 128:(hidx + 1) * 128, q0:q0 + QM],
                                           in_=ctxo[bo][:]), 16)
                    ld_cst.append(ld.n)
                    fin_idx += 1

            grp_end_vc[g] = vc.n

        p1_ld = ld.n
        p1_tc = tc.n
        p1_vc = vc.n
        ar.pop()
        ar.pop()

        ar.push()
        xnp3 = [ar.bf16(NCH * TCH) for _ in range(2)]
        comb = ar.bf16(NCH * OWN)
        wep = [ar.bf16(NCH * 1024) for _ in range(2)]
        silu_t = [ar.bf16(TCH) for _ in range(4)]
        tmp_t = [ar.bf16(TCH) for _ in range(4)]


        ld_xn3, tc_lg, tc_tr, vc_fw = {}, {}, {}, {}
        vc_r1, sc_sg = {}, {}
        for t3 in range(4):
            bi = t3 % 2

            @block.sync
            def _(s, t3=t3, bi=bi):
                s.wait_ge(ld.h, p1_ld)
                s.wait_ge(vc.h, p1_vc)
                s.wait_ge(tc.h, p1_tc)
                s.wait_ge(gd.h, gd_own)
                if t3 == 0:
                    s.wait_ge(cc.h, 1)
                if t3 >= 2:
                    s.wait_ge(tc.h, tc_lg[(t3 - 2, 3)])
                ld.inc(s.dma_start(
                    out=xnp3[bi].rearrange("p (c t) -> p c t", c=NCH),
                    in_=xno_d.rearrange("(c p) t -> p c t", p=128)[:, :, t3 * TCH:(t3 + 1) * TCH]), 16)
            ld_xn3[t3] = ld.n

            for ts in range(4):
                ti = t3 * 4 + ts

                @block.tensor
                def _(te, t3=t3, bi=bi, ts=ts, ti=ti):
                    te.wait_ge(ld.h, ld_xn3[t3])
                    te.wait_ge(gd.h, gd_init)
                    if ti == 0:
                        te.wait_ge(ld.h, ld_init)
                    else:
                        te.wait_ge(vc.h, vc_r1[ti - 1])
                    if ti >= 1:
                        te.wait_ge(vc.h, vc_fw[ti - 1])
                    for c in range(NCH):
                        te.matmul(PE_[:, 0:8],
                                  xnp3[bi][:, c * TCH + ts * 128:c * TCH + (ts + 1) * 128],
                                  rw_sb[:, c * E:(c + 1) * E],
                                  start=(c == 0), stop=False)
                    tc.inc(te.matmul(PE_[:, 0:8], ones_rf[0:1, 0:128], rb_sb[0:1, :],
                                     start=False, stop=True))
                tc_lg[(t3, ts)] = tc.n

                @block.vector
                def _(v, t3=t3, ts=ts, ti=ti):
                    v.wait_ge(tc.h, tc_lg[(t3, ts)])
                    v.tensor_copy(out=lgt[:], in_=PE_[:, 0:8])
                    v.drain()
                    v.reduce_max(out=rm1[:], in_=lgt[:], axis=AXX)
                    v.drain()
                    v.tensor_scalar(out=req1[:], in0=lgt[:], scalar1=rm1[:], scalar2=None,
                                    op0=ALU.is_equal)
                    v.drain()
                    v.scalar_tensor_tensor(out=rmsk[:], in0=req1[:], scalar=-1e30, in1=lgt[:],
                                           op0=ALU.mult, op1=ALU.add)
                    v.drain()
                    v.reduce_max(out=rm2[:], in_=rmsk[:], axis=AXX)
                    v.drain()
                    v.tensor_tensor(out=rd12[:], in0=rm1[:], in1=rm2[:], op=ALU.subtract)
                    vc.inc(v.tensor_scalar(out=req2[:], in0=lgt[:], scalar1=rm2[:],
                                           scalar2=None, op0=ALU.is_equal))
                vc_r1[ti] = vc.n

                @block.scalar
                def _(s, ti=ti):
                    s.wait_ge(vc.h, vc_r1[ti])
                    sc.inc(s.activation(out=rw1[:], in_=rd12[:], func=AF.Sigmoid))
                sc_sg[ti] = sc.n

                @block.vector
                def _(v, ti=ti):
                    v.wait_ge(sc.h, sc_sg[ti])
                    v.tensor_scalar(out=rw2[:], in0=rw1[:], scalar1=-1.0, scalar2=1.0,
                                    op0=ALU.mult, op1=ALU.add)
                    v.tensor_scalar(out=rfw1[:], in0=req1[:], scalar1=rw1[:], scalar2=None,
                                    op0=ALU.mult)
                    v.drain()
                    vc.inc(v.scalar_tensor_tensor(out=rfw[:], in0=req2[:], scalar=rw2[:],
                                                  in1=rfw1[:], op0=ALU.mult, op1=ALU.add))
                v_fw2 = vc.n

                @block.tensor
                def _(te, v_fw2=v_fw2):
                    te.wait_ge(vc.h, v_fw2)
                    tc.inc(te.transpose(PF[0:8, 0:128], rfw[:, 0:8], ident[:]))
                tc_tr[ti] = tc.n

                @block.vector
                def _(v, ti=ti):
                    v.wait_ge(tc.h, tc_tr[ti])
                    vc.inc(v.tensor_copy(out=wrow_sb[:, ti * 128:(ti + 1) * 128],
                                         in_=PF[0:8, 0:128]))
                vc_fw[ti] = vc.n

        NEP = 2
        gd_wep = {}
        tc_hh, sc_hh, vc_cmb, vc_cmb_idx = {}, {}, {}, {}

        @block.gpsimd
        def _(gg):
            gg.wait_ge(tc.h, p1_tc)
            gg.wait_ge(vc.h, p1_vc)
            gd.inc(gg.dma_start(
                out=wep[0].rearrange("p (c f) -> p c f", c=NCH),
                in_=ew_d[0].rearrange("(c p) f -> p c f", p=128)[:, :, 0:1024]), 16)
        gd_wep[(0, 0)] = gd.n

        @block.gpsimd
        def _(gg):
            tko = toko(gg)
            gg.wait_ge(ld.h, p1_ld)
            gg.wait_ge(tc.h, p1_tc)
            gg.wait_ge(vc.h, p1_vc)
            gg.wait_ge(gd.h, gd_wep[(0, 0)])
            cc.inc(gg.collective_compute("AllGather", ALU.bypass, replica_groups=RG,
                                         ins=[cin_t.ap().opt()], outs=[cout_t.ap().opt()]))
            gg.wait_ge(cc.h, cc.n)
            gd.inc(gg.dma_start(out=ctxo_d[:, :], in_=cout[:, bass.ds(tko, OWN)]), 16)
            gg.wait_ge(gd.h, gd.n)
        gd_ctxo = gd.n

        tc_bc3 = {}
        for e in range(E):
            @block.tensor
            def _(te, e=e):
                for t3 in range(4):
                    slot = (PC, PD, PE_, PF)[t3]
                    if e == 0:
                        te.wait_ge(vc.h, vc_fw[15])
                    else:
                        te.wait_ge(vc.h, vc_cmb[(e - 1, NEP - 1, t3, 7)])
                    tc.inc(te.matmul(slot[:, 0:TCH], sel_sb[0:8, e * 128:(e + 1) * 128],
                                     wrow_sb[0:8, t3 * TCH:(t3 + 1) * TCH],
                                     start=True, stop=True))
                    tc_bc3[(e, t3)] = tc.n
            for p in range(NEP):
                ep = e * NEP + p
                if ep > 0:
                    @block.gpsimd
                    def _(gg, e=e, p=p, ep=ep):
                        if ep >= 2:
                            e2, p2 = divmod(ep - 2, NEP)
                            gg.wait_ge(tc.h, tc_hh[(e2, p2, 3, 7)])
                        gd.inc(gg.dma_start(
                            out=wep[ep % 2].rearrange("p (c f) -> p c f", c=NCH),
                            in_=ew_d[e].rearrange("(c p) f -> p c f", p=128)[:, :, p * 1024:(p + 1) * 1024]), 16)
                    gd_wep[(e, p)] = gd.n
                else:
                    pass

                for t3 in range(4):
                    bi3 = (ep * 4 + t3) % 2

                    @block.sync
                    def _(s, e=e, p=p, t3=t3, bi3=bi3, ep=ep):
                        gi = ep * 4 + t3
                        if gi == 0:
                            s.wait_ge(tc.h, tc_lg[(3, 3)])
                        if gi >= 2:
                            g2 = gi - 2
                            e2, r2 = divmod(g2, NEP * 4)
                            p2, t2 = divmod(r2, 4)
                            s.wait_ge(tc.h, tc_hh[(e2, p2, t2, 7)])
                        ld.inc(s.dma_start(
                            out=xnp3[bi3].rearrange("p (c t) -> p c t", c=NCH),
                            in_=xno_d.rearrange("(c p) t -> p c t", p=128)[:, :, t3 * TCH:(t3 + 1) * TCH]), 16)
                    ld_xn3[(e, p, t3)] = ld.n

                    for fj in range(8):
                        idx = (ep * 4 + t3) * 8 + fj

                        @block.tensor
                        def _(te, e=e, p=p, t3=t3, fj=fj, bi3=bi3, idx=idx):
                            te.wait_ge(ld.h, ld_xn3[(e, p, t3)])
                            te.wait_ge(gd.h, gd_wep[(e, p)])
                            if idx >= 4:
                                te.wait_ge(sc.h, sc_hh[idx - 4])
                            pp = (PA[:, 0:512], PA[:, 512:1024],
                                  PB[:, 0:512], PB[:, 512:1024])[fj % 4]
                            for c in range(NCH):
                                i = te.matmul(pp,
                                              wep[p % 2][:, c * 1024 + fj * 128:c * 1024 + (fj + 1) * 128],
                                              xnp3[bi3][:, c * TCH:(c + 1) * TCH],
                                              start=(c == 0), stop=(c == NCH - 1))
                            tc.inc(i)
                        tc_hh[(e, p, t3, fj)] = tc.n

                        @block.scalar
                        def _(s, e=e, p=p, t3=t3, fj=fj, idx=idx):
                            s.wait_ge(tc.h, tc_hh[(e, p, t3, fj)])
                            if idx >= 4:
                                s.wait_ge(vc.h, vc_cmb_idx[idx - 4])
                            pp = (PA[:, 0:512], PA[:, 512:1024],
                                  PB[:, 0:512], PB[:, 512:1024])[fj % 4]
                            cf = p * 8 + fj
                            sc.inc(s.activation(out=silu_t[fj % 4][:], in_=pp, func=AF.Silu,
                                                bias=eb_sb[:, e * NCH + cf:e * NCH + cf + 1]))
                        sc_hh[idx] = sc.n

                        if fj % 4 == 3:
                            @block.vector
                            def _(v, e=e, p=p, t3=t3, fj=fj, idx=idx):
                                v.wait_ge(sc.h, sc_hh[idx])
                                v.wait_ge(tc.h, tc_bc3[(e, t3)])
                                wslice = (PC, PD, PE_, PF)[t3][:, 0:TCH]
                                for q in range(4):
                                    fjq = fj - 3 + q
                                    cf = p * 8 + fjq
                                    dst = comb[:, cf * OWN + t3 * TCH:cf * OWN + (t3 + 1) * TCH]
                                    if e == 0:
                                        i = v.tensor_tensor(out=dst, in0=silu_t[q][:],
                                                            in1=wslice, op=ALU.mult)
                                    else:
                                        v.tensor_tensor(out=tmp_t[q][:], in0=silu_t[q][:],
                                                        in1=wslice, op=ALU.mult)
                                if e > 0:
                                    v.drain()
                                    for q in range(4):
                                        fjq = fj - 3 + q
                                        cf = p * 8 + fjq
                                        dst = comb[:, cf * OWN + t3 * TCH:cf * OWN + (t3 + 1) * TCH]
                                        i = v.tensor_tensor(out=dst, in0=dst, in1=tmp_t[q][:],
                                                            op=ALU.add)
                                vc.inc(i)
                            for q in range(4):
                                vc_cmb_idx[idx - 3 + q] = vc.n
                            vc_cmb[(e, p, t3, fj)] = vc.n

        @block.sync
        def _(s):
            s.wait_ge(vc.h, vc_cmb[(E - 1, NEP - 1, 3, 7)])
            for q in range(4):
                ld.inc(s.dma_start(
                    out=comb_d.rearrange("(c p) t -> p c t", p=128)[:, :, q * TCH:(q + 1) * TCH],
                    in_=comb.rearrange("p (c t) -> p c t", c=NCH)[:, :, q * TCH:(q + 1) * TCH]), 16)
        p3_ld = ld.n
        p3_tc = tc.n
        p3_vc = vc.n
        p3_sc = sc.n
        ar.pop()

        ar.push()
        T4 = 256
        wob = [ar.bf16(NCH * 512) for _ in range(2)]
        pwb = [ar.bf16(NCH * 512) for _ in range(2)]
        cmb4 = [ar.bf16(NCH * T4) for _ in range(2)]
        ctx4 = [ar.bf16(NCH * T4) for _ in range(2)]
        y1s = [ar.bf16(4 * T4) for _ in range(2)]
        ats = [ar.bf16(4 * T4) for _ in range(2)]

        gd_pan, ld_c4, sc_y1, vc_at, ld_s4, tc_w4 = {}, {}, {}, {}, {}, {}

        for op in range(4):
            @block.gpsimd
            def _(gg, op=op):
                if op == 0:
                    gg.wait_ge(tc.h, p3_tc)
                    gg.wait_ge(sc.h, p3_sc)
                    gg.wait_ge(vc.h, p3_vc)
                    gg.wait_ge(ld.h, p3_ld)
                if op >= 2:
                    gg.wait_ge(tc.h, tc_w4[(op - 2, 7, 3)])
                gd.inc(gg.dma_start(
                    out=wob[op % 2].rearrange("p (c o) -> p c o", c=NCH),
                    in_=woT.rearrange("(c p) o -> p c o", p=128)[:, :, op * 512:(op + 1) * 512]), 16)
                gd.inc(gg.dma_start(
                    out=pwb[op % 2].rearrange("p (c o) -> p c o", c=NCH),
                    in_=pw_d.rearrange("(c p) o -> p c o", p=128)[:, :, op * 512:(op + 1) * 512]), 16)
            gd_pan[op] = gd.n

            for tq in range(8):
                bi4 = tq % 2
                gi4 = op * 8 + tq

                @block.sync
                def _(s, op=op, tq=tq, bi4=bi4, gi4=gi4):
                    if op == 0 and tq == 0:
                        s.wait_ge(ld.h, p3_ld)
                        s.wait_ge(gd.h, gd_ctxo)
                    if gi4 >= 2:
                        o2, t2 = divmod(gi4 - 2, 8)
                        s.wait_ge(tc.h, tc_w4[(o2, t2, 3)])
                    ld.inc(s.dma_start(
                        out=cmb4[bi4].rearrange("p (c t) -> p c t", c=NCH),
                        in_=comb_d.rearrange("(c p) t -> p c t", p=128)[:, :, tq * T4:(tq + 1) * T4]), 16)
                    ld.inc(s.dma_start(
                        out=ctx4[bi4].rearrange("p (c t) -> p c t", c=NCH),
                        in_=ctxo_d.rearrange("(c p) t -> p c t", p=128)[:, :, tq * T4:(tq + 1) * T4]), 16)
                ld_c4[(op, tq)] = ld.n

                for ocs in range(4):
                    @block.tensor
                    def _(te, op=op, tq=tq, ocs=ocs, bi4=bi4, gi4=gi4):
                        te.wait_ge(ld.h, ld_c4[(op, tq)])
                        te.wait_ge(gd.h, gd_pan[op])
                        idx4 = gi4 * 4 + ocs
                        if idx4 >= 2:
                            te.wait_ge(sc.h, sc_y1[idx4 - 2])
                            te.wait_ge(vc.h, vc_at[idx4 - 2])
                        pj = (PC, PE_)[ocs % 2]
                        pw_ = (PD, PF)[ocs % 2]
                        for c in range(NCH):
                            te.matmul(pj[:, 0:T4],
                                      pwb[op % 2][:, c * 512 + ocs * 128:c * 512 + (ocs + 1) * 128],
                                      cmb4[bi4][:, c * T4:(c + 1) * T4],
                                      start=(c == 0), stop=(c == NCH - 1))
                        for c in range(NCH):
                            i = te.matmul(pw_[:, 0:T4],
                                          wob[op % 2][:, c * 512 + ocs * 128:c * 512 + (ocs + 1) * 128],
                                          ctx4[bi4][:, c * T4:(c + 1) * T4],
                                          start=(c == 0), stop=(c == NCH - 1))
                        tc.inc(i)
                    tc_w4[(op, tq, ocs)] = tc.n

                    @block.scalar
                    def _(s, op=op, tq=tq, ocs=ocs, bi4=bi4):
                        s.wait_ge(tc.h, tc_w4[(op, tq, ocs)])
                        gi4 = op * 8 + tq
                        idx4 = gi4 * 4 + ocs
                        if idx4 >= 8:
                            s.wait_ge(gd.h, ld_s4[idx4 // 4 - 2])
                        pj = (PC, PE_)[ocs % 2]
                        oc_g = op * 4 + ocs
                        sc.inc(s.add(out=y1s[bi4][:, ocs * T4:(ocs + 1) * T4], in_=pj[:, 0:T4],
                                     add=pb_sb[:, oc_g:oc_g + 1]))
                    sc_y1[(op * 8 + tq) * 4 + ocs] = sc.n

                    @block.vector
                    def _(v, op=op, tq=tq, ocs=ocs, bi4=bi4):
                        v.wait_ge(tc.h, tc_w4[(op, tq, ocs)])
                        idx4 = (op * 8 + tq) * 4 + ocs
                        if idx4 >= 8:
                            v.wait_ge(gd.h, ld_s4[idx4 // 4 - 2])
                        pw_ = (PD, PF)[ocs % 2]
                        vc.inc(v.tensor_copy(out=ats[bi4][:, ocs * T4:(ocs + 1) * T4],
                                             in_=pw_[:, 0:T4]))
                    vc_at[(op * 8 + tq) * 4 + ocs] = vc.n

                @block.gpsimd
                def _(gg, op=op, tq=tq, bi4=bi4, gi4=gi4):
                    gg.wait_ge(sc.h, sc_y1[gi4 * 4 + 3])
                    gg.wait_ge(vc.h, vc_at[gi4 * 4 + 3])
                    gd.inc(gg.dma_start(
                        out=y1_d.rearrange("(o p) t -> p o t", p=128)[:, op * 4:(op + 1) * 4, tq * T4:(tq + 1) * T4],
                        in_=y1s[bi4].rearrange("p (o t) -> p o t", o=4)), 16)
                    gd.inc(gg.dma_start(
                        out=attn_d.rearrange("(o p) t -> p o t", p=128)[:, op * 4:(op + 1) * 4, tq * T4:(tq + 1) * T4],
                        in_=ats[bi4].rearrange("p (o t) -> p o t", o=4)), 16)
                ld_s4[gi4] = gd.n

        p4a_ld = ld.n
        p4a_gd = gd.n
        ar.pop()

        ar.push()
        xo4 = ar.f32(NCH * TCH)
        y14 = ar.bf16(NCH * TCH)
        at4 = ar.bf16(NCH * TCH)
        yv = ar.f32(NCH * TCH)
        fin = ar.f32(NCH * TCH)
        sq4 = ar.bf16(TCH)
        m4 = ar.f32(TCH)
        r4 = ar.f32(TCH)
        mr4 = ar.f32(TCH)[0:1, :]
        rr4 = ar.f32(TCH)[0:1, :]
        ss4 = ar.f32(TCH)[0:1, :]
        rb4 = ar.f32(TCH)

        ld_b4, tc_b4, vc_b4, sc_b4, tc_ss, sc_rr, tc_rb, vc_fin = {}, {}, {}, {}, {}, {}, {}, {}

        for tq in range(4):
            @block.sync
            def _(s, tq=tq):
                if tq == 0:
                    s.wait_ge(ld.h, p4a_ld)
                    s.wait_ge(gd.h, p4a_gd)
                if tq >= 1:
                    s.wait_ge(oc.h, 16 * tq)
                ld.inc(s.dma_start(
                    out=xo4.rearrange("p (c t) -> p c t", c=NCH),
                    in_=x_own.rearrange("(c p) t -> p c t", p=128)[:, :, tq * TCH:(tq + 1) * TCH]), 16)
                ld.inc(s.dma_start(
                    out=y14.rearrange("p (c t) -> p c t", c=NCH),
                    in_=y1_d.rearrange("(c p) t -> p c t", p=128)[:, :, tq * TCH:(tq + 1) * TCH]), 16)
                ld.inc(s.dma_start(
                    out=at4.rearrange("p (c t) -> p c t", c=NCH),
                    in_=attn_d.rearrange("(c p) t -> p c t", p=128)[:, :, tq * TCH:(tq + 1) * TCH]), 16)
                ld.inc(s.dma_start(out=mr4[:], in_=rowso_d[0:1, tq * TCH:(tq + 1) * TCH]), 16)
                ld.inc(s.dma_start(out=rr4[:], in_=rowso_d[1:2, tq * TCH:(tq + 1) * TCH]), 16)
                s.wait_ge(ld.h, ld.n)
            ld_b4[tq] = ld.n

            @block.tensor
            def _(te, tq=tq):
                te.wait_ge(ld.h, ld_b4[tq])
                if tq >= 1:
                    te.wait_ge(vc.h, vc_b4[tq - 1])
                te.matmul(PC[:, 0:TCH], ones_rf[0:1, :], mr4[0:1, :], start=True, stop=True)
                tc.inc(te.matmul(PD[:, 0:TCH], ones_rf[0:1, :], rr4[0:1, :], start=True, stop=True))
            tc_b4[tq] = tc.n

            @block.vector
            def _(v, tq=tq):
                v.wait_ge(tc.h, tc_b4[tq])
                v.tensor_copy(out=m4, in_=PC[:, 0:TCH])
                v.tensor_copy(out=r4, in_=PD[:, 0:TCH])
                v.drain()
                for c in range(NCH):
                    v.tensor_tensor(out=yv[:, c * TCH:(c + 1) * TCH],
                                    in0=xo4[:, c * TCH:(c + 1) * TCH], in1=m4, op=ALU.subtract)
                v.drain()
                for c in range(NCH):
                    v.tensor_tensor(out=yv[:, c * TCH:(c + 1) * TCH],
                                    in0=yv[:, c * TCH:(c + 1) * TCH], in1=r4, op=ALU.mult)
                v.drain()
                for c in range(NCH):
                    i = v.tensor_tensor(out=yv[:, c * TCH:(c + 1) * TCH],
                                        in0=yv[:, c * TCH:(c + 1) * TCH],
                                        in1=y14[:, c * TCH:(c + 1) * TCH], op=ALU.add)
                vc.inc(i)
            vc_b4[tq] = vc.n

            for c in range(NCH):
                @block.scalar
                def _(s, tq=tq, c=c):
                    if c == 0:
                        s.wait_ge(vc.h, vc_b4[tq])
                    if c >= 1:
                        s.wait_ge(tc.h, tc_ss.get((tq, c - 1), 0))
                    sc.inc(s.activation(out=sq4, in_=yv[:, c * TCH:(c + 1) * TCH],
                                        func=AF.Square))
                sc_b4[(tq, c)] = sc.n

                @block.tensor
                def _(te, tq=tq, c=c):
                    te.wait_ge(sc.h, sc_b4[(tq, c)])
                    if tq >= 1 and c == 0:
                        te.wait_ge(sc.h, sc_rr[tq - 1])
                    tc.inc(te.matmul(PE_[0:1, 0:TCH], ones_cb[:, 0:1], sq4,
                                     start=(c == 0), stop=(c == NCH - 1)))
                tc_ss[(tq, c)] = tc.n

            @block.scalar
            def _(s, tq=tq):
                s.wait_ge(tc.h, tc_ss[(tq, NCH - 1)])
                s.activation(out=ss4[:], in_=PE_[0:1, 0:TCH], func=AF.Ln, scale=1.0 / D,
                             bias=epsrm_c[:])
                s.drain()
                sc.inc(s.activation(out=ss4[:], in_=ss4[:], func=AF.Exp, scale=-0.5))
            sc_rr[tq] = sc.n

            @block.tensor
            def _(te, tq=tq):
                te.wait_ge(sc.h, sc_rr[tq])
                if tq >= 1:
                    te.wait_ge(vc.h, vc_fin[tq - 1])
                tc.inc(te.matmul(PF[:, 0:TCH], ones_rf[0:1, :], ss4[0:1, :],
                                 start=True, stop=True))
            tc_rb[tq] = tc.n

            @block.vector
            def _(v, tq=tq):
                v.wait_ge(tc.h, tc_rb[tq])
                v.tensor_copy(out=rb4, in_=PF[:, 0:TCH])
                v.drain()
                for c in range(NCH):
                    v.scalar_tensor_tensor(out=yv[:, c * TCH:(c + 1) * TCH],
                                           in0=yv[:, c * TCH:(c + 1) * TCH],
                                           scalar=nw_sb[:, c:c + 1], in1=rb4,
                                           op0=ALU.mult, op1=ALU.mult)
                for c in range(NCH):
                    v.tensor_tensor(out=fin[:, c * TCH:(c + 1) * TCH],
                                    in0=xo4[:, c * TCH:(c + 1) * TCH],
                                    in1=at4[:, c * TCH:(c + 1) * TCH], op=ALU.add)
                v.drain()
                for c in range(NCH):
                    i = v.tensor_tensor(out=fin[:, c * TCH:(c + 1) * TCH],
                                        in0=fin[:, c * TCH:(c + 1) * TCH],
                                        in1=yv[:, c * TCH:(c + 1) * TCH], op=ALU.add)
                vc.inc(i)
            vc_fin[tq] = vc.n

            @block.sync
            def _(s, tq=tq):
                s.wait_ge(vc.h, vc_fin[tq])
                oc.inc(s.dma_start(
                    out=out_d.rearrange("(c p) t -> p c t", p=128)[:, :, tq * TCH:(tq + 1) * TCH],
                    in_=fin.rearrange("p (c t) -> p c t", c=NCH)), 16)

        @block.sync
        def _(s):
            s.wait_ge(oc.h, oc.n)
        ar.pop()

    return nc


_NC_CACHE = {}


def _get_nc():
    if "nc" not in _NC_CACHE:
        _NC_CACHE["nc"] = build_nc()
    return _NC_CACHE["nc"]


def _rope_tables():
    half = 64
    inv_freq = 1.0 / ROPE_BASE ** (np.arange(0, 128, 2, dtype=np.float64) / 128.0)
    t = np.arange(S, dtype=np.float64)
    freqs = np.outer(t, inv_freq)
    cos = np.cos(freqs).astype(np.float32).T
    sin = np.sin(freqs).astype(np.float32).T
    cos_dup = np.concatenate([cos, cos], axis=0).astype(BFD)
    sin_sgn = np.concatenate([-sin, sin], axis=0).astype(BFD)
    return cos_dup, sin_sgn


def _diag_masks():
    k = np.arange(128)[:, None]
    q = np.arange(QM)[None, :]
    ms = [(q >= (128 * j + k)).astype(np.float32) for j in range(4)]
    return np.concatenate(ms, axis=1).astype(BFD)


def kernel(L0_attn_hidden_states, L0_attn_Wq, L0_attn_Wk, L0_attn_Wv, L0_attn_Wo,
           L0_ffn_router_w, L0_ffn_router_b, L0_ffn_expert_w, L0_ffn_expert_b,
           L0_ffn_proj_w, L0_ffn_proj_b, L0_ffn_norm_w):
    x = np.asarray(L0_attn_hidden_states, dtype=np.float32)
    cos_dup, sin_sgn = _rope_tables()
    dmask = _diag_masks()
    ident = np.eye(128, dtype=np.float32)
    sel = np.zeros((8, E * 128), dtype=BFD)
    for e in range(E):
        sel[e, e * 128:(e + 1) * 128] = 1

    wq = np.ascontiguousarray(np.asarray(L0_attn_Wq, np.float32).T)
    wk = np.ascontiguousarray(np.asarray(L0_attn_Wk, np.float32).T)
    wv = np.ascontiguousarray(np.asarray(L0_attn_Wv, np.float32).T)
    wo = np.ascontiguousarray(np.asarray(L0_attn_Wo, np.float32).T)
    rw = np.ascontiguousarray(np.asarray(L0_ffn_router_w, np.float32))
    rb = np.ascontiguousarray(np.asarray(L0_ffn_router_b, np.float32)).reshape(1, E)
    ew = np.ascontiguousarray(np.asarray(L0_ffn_expert_w, np.float32))
    eb = np.ascontiguousarray(
        np.asarray(L0_ffn_expert_b, np.float32).reshape(E, 16, 128).transpose(2, 0, 1).reshape(128, E * 16))
    pw = np.ascontiguousarray(np.asarray(L0_ffn_proj_w, np.float32))
    pb = np.ascontiguousarray(np.asarray(L0_ffn_proj_b, np.float32).reshape(16, 128).T)
    nw = np.ascontiguousarray(np.asarray(L0_ffn_norm_w, np.float32).reshape(16, 128).T)

    in_maps = []
    for c in range(8):
        b = c // 2
        hh = c % 2
        x_fm = np.ascontiguousarray(x[b].T)
        x_ownc = np.ascontiguousarray(x_fm[:, hh * OWN:(hh + 1) * OWN])
        in_maps.append({
            "x_fm": x_fm,
            "x_own": x_ownc,
            "wqT": np.ascontiguousarray(wq[:, hh * 1024:(hh + 1) * 1024]),
            "wkT": np.ascontiguousarray(wk[:, hh * 1024:(hh + 1) * 1024]),
            "wvT": np.ascontiguousarray(wv[:, hh * 1024:(hh + 1) * 1024]),
            "woT": wo,
            "rw_d": rw, "rb_d": rb, "ew_d": ew, "eb_d": eb,
            "pw_d": pw, "pb_d": pb, "nw_d": nw,
            "cos_d": cos_dup, "sin_d": sin_sgn, "msk_d": dmask,
            "ident_d": ident,
            "sel_d": sel,
        })

    nc = _get_nc()
    res = run_bass_kernel_spmd(nc, in_maps, core_ids=list(range(8)))

    out = np.empty((B, S, D), dtype=np.float32)
    for c in range(8):
        b = c // 2
        hh = c % 2
        out[b, hh * OWN:(hh + 1) * OWN, :] = res.results[c]["out_d"].T
    return out


# version 07 (final)
# speedup vs baseline: 1.1642x; 1.0725x over previous
import contextlib
import math

import numpy as np
import ml_dtypes

import concourse.bass as bass
import concourse.mybir as mybir
from concourse.bass_utils import run_bass_kernel_spmd

F32 = mybir.dt.float32
BF16 = mybir.dt.bfloat16
AF = mybir.ActivationFunctionType
ALU = mybir.AluOpType
AXX = mybir.AxisListType.X

B, S, D = 4, 4096, 2048
NCH = 16
TCH = 512
NT = S // TCH
G = 2
NGRP = 4
QM = 512
NQM = S // QM
E = 8
OWN = 2048
EBIAS = -12.0
ISQD = 1.0 / math.sqrt(128.0)
ROPE_BASE = 10000.0

BFD = np.dtype(ml_dtypes.bfloat16)


class Ctr:
    def __init__(self, es, nc, name):
        self.h = es.enter_context(nc.semaphore(name))
        self.n = 0

    def inc(self, instr, k=1):
        instr.then_inc(self.h, k)
        self.n += k
        return self.n


class Arena:
    def __init__(self, t, nelem):
        self.t, self.nelem, self.off, self.marks = t, nelem, 0, []

    def push(self):
        self.marks.append(self.off)

    def pop(self):
        self.off = self.marks.pop()

    def bf16(self, n):
        n = (n + 511) & ~511
        ap = self.t[:, self.off:self.off + n]
        self.off += n
        assert self.off <= self.nelem, f"arena overflow {self.off} > {self.nelem}"
        return ap

    def f32(self, n):
        return self.bf16(2 * n).bitcast(F32)


def build_nc():
    nc = bass.Bass("TRN2", target_bir_lowering=False)

    x_fm = nc.dram_tensor("x_fm", [D, S], F32, kind="ExternalInput").ap()
    x_own = nc.dram_tensor("x_own", [D, OWN], F32, kind="ExternalInput").ap()
    wqT = nc.dram_tensor("wqT", [D, 1024], F32, kind="ExternalInput").ap()
    wkT = nc.dram_tensor("wkT", [D, 1024], F32, kind="ExternalInput").ap()
    wvT = nc.dram_tensor("wvT", [D, 1024], F32, kind="ExternalInput").ap()
    woT = nc.dram_tensor("woT", [D, D], F32, kind="ExternalInput").ap()
    rw_d = nc.dram_tensor("rw_d", [D, E], F32, kind="ExternalInput").ap()
    rb_d = nc.dram_tensor("rb_d", [1, E], F32, kind="ExternalInput").ap()
    ew_d = nc.dram_tensor("ew_d", [E, D, D], F32, kind="ExternalInput").ap()
    eb_d = nc.dram_tensor("eb_d", [128, E * NCH], F32, kind="ExternalInput").ap()
    pw_d = nc.dram_tensor("pw_d", [D, D], F32, kind="ExternalInput").ap()
    pb_d = nc.dram_tensor("pb_d", [128, NCH], F32, kind="ExternalInput").ap()
    nw_d = nc.dram_tensor("nw_d", [128, NCH], F32, kind="ExternalInput").ap()
    cos_d = nc.dram_tensor("cos_d", [128, S], BF16, kind="ExternalInput").ap()
    sin_d = nc.dram_tensor("sin_d", [128, S], BF16, kind="ExternalInput").ap()
    msk_d = nc.dram_tensor("msk_d", [128, 2 * 1024], BF16, kind="ExternalInput").ap()
    ident_d = nc.dram_tensor("ident_d", [128, 128], F32, kind="ExternalInput").ap()
    sel_d = nc.dram_tensor("sel_d", [8, E * 128], BF16, kind="ExternalInput").ap()
    out_d = nc.dram_tensor("out_d", [D, OWN], F32, kind="ExternalOutput").ap()

    xn_d = nc.dram_tensor("xn_d", [D, S], BF16).ap()
    xno_d = nc.dram_tensor("xno_d", [D, OWN], BF16).ap()
    rows_d = nc.dram_tensor("rows_d", [2, S], F32).ap()
    rowso_d = nc.dram_tensor("rowso_d", [2, OWN], F32).ap()
    ctxo_d = nc.dram_tensor("ctxo_d", [D, OWN], BF16).ap()
    comb_d = nc.dram_tensor("comb_d", [D, OWN], BF16).ap()
    y1_d = nc.dram_tensor("y1_d", [D, OWN], BF16).ap()
    attn_d = nc.dram_tensor("attn_d", [D, OWN], BF16).ap()
    cin_t = nc.dram_tensor("cin", [8 * 128, S], BF16)
    cout_t = nc.dram_tensor("cout", [16 * 128, S], BF16)
    cin = cin_t.ap()
    cout = cout_t.ap()
    RG = [[0, 1], [2, 3], [4, 5], [6, 7]]

    es = contextlib.ExitStack()
    AR_N = 90 * 1024
    art = es.enter_context(nc.sbuf_tensor("arena", [128, AR_N], BF16))
    ar = Arena(art, AR_N)

    ones_cb = es.enter_context(nc.sbuf_tensor("ones_cb", [128, 1], BF16))
    ones_rf = es.enter_context(nc.sbuf_tensor("ones_rf", [1, 128], F32))
    ident = es.enter_context(nc.sbuf_tensor("ident", [128, 128], F32))
    ebias_c = es.enter_context(nc.sbuf_tensor("ebias_c", [128, 1], F32))
    epsln_c = es.enter_context(nc.sbuf_tensor("epsln_c", [1, 1], F32))
    epsrm_c = es.enter_context(nc.sbuf_tensor("epsrm_c", [1, 1], F32))
    masks = es.enter_context(nc.sbuf_tensor("masks", [128, 2 * 1024], BF16))
    eb_sb = es.enter_context(nc.sbuf_tensor("eb_sb", [128, E * NCH], F32))
    pb_sb = es.enter_context(nc.sbuf_tensor("pb_sb", [128, NCH], F32))
    nw_sb = es.enter_context(nc.sbuf_tensor("nw_sb", [128, NCH], F32))
    rb_sb = es.enter_context(nc.sbuf_tensor("rb_sb", [1, E], F32))
    rw_sb = es.enter_context(nc.sbuf_tensor("rw_sb", [128, NCH * E], BF16))
    wrow_sb = es.enter_context(nc.sbuf_tensor("wrow_sb", [8, OWN], BF16))
    sel_sb = es.enter_context(nc.sbuf_tensor("sel_sb", [8, E * 128], BF16))
    mrow = es.enter_context(nc.sbuf_tensor("mrow", [1, TCH], F32))
    srow = es.enter_context(nc.sbuf_tensor("srow", [1, TCH], F32))
    vrow = es.enter_context(nc.sbuf_tensor("vrow", [1, TCH], F32))
    rrow = es.enter_context(nc.sbuf_tensor("rrow", [1, TCH], F32))
    linv = es.enter_context(nc.sbuf_tensor("linv", [1, QM], F32))
    lbc = es.enter_context(nc.sbuf_tensor("lbc", [128, QM], F32))
    lgt = es.enter_context(nc.sbuf_tensor("lgt", [128, 8], F32))
    rm1 = es.enter_context(nc.sbuf_tensor("rm1", [128, 1], F32))
    rm2 = es.enter_context(nc.sbuf_tensor("rm2", [128, 1], F32))
    req1 = es.enter_context(nc.sbuf_tensor("req1", [128, 8], F32))
    req2 = es.enter_context(nc.sbuf_tensor("req2", [128, 8], F32))
    rmsk = es.enter_context(nc.sbuf_tensor("rmsk", [128, 8], F32))
    rw1 = es.enter_context(nc.sbuf_tensor("rw1", [128, 1], F32))
    rw2 = es.enter_context(nc.sbuf_tensor("rw2", [128, 1], F32))
    rd12 = es.enter_context(nc.sbuf_tensor("rd12", [128, 1], F32))
    rfw = es.enter_context(nc.sbuf_tensor("rfw", [128, 8], F32))
    rfw1 = es.enter_context(nc.sbuf_tensor("rfw1", [128, 8], F32))

    PAB = es.enter_context(nc.psum_tensor("PAB", [128, 2048], F32))
    PA = PAB[:, 0:1024]
    PB = PAB[:, 1024:2048]
    PC = es.enter_context(nc.psum_tensor("PC", [128, 512], F32))
    PD = es.enter_context(nc.psum_tensor("PD", [128, 512], F32))
    PE_ = es.enter_context(nc.psum_tensor("PE", [128, 512], F32))
    PF = es.enter_context(nc.psum_tensor("PF", [128, 512], F32))

    with es:
        nc.cache_partition_id()
        block = es.enter_context(nc.Block())

        ld = Ctr(es, nc, "ld")
        gd = Ctr(es, nc, "gd")
        tc = Ctr(es, nc, "tc")
        sc = Ctr(es, nc, "sc")
        vc = Ctr(es, nc, "vc")
        cc = Ctr(es, nc, "cc")
        oc = Ctr(es, nc, "oc")

        def toko(eng, key=None):
            return (eng.partition_id() % 2) * OWN

        @block.vector
        def _(v):
            v.memset(ones_cb[:], 1.0)
            v.memset(ones_rf[:], 1.0)
            v.memset(ebias_c[:], EBIAS)
            v.memset(epsln_c[:], 1e-5)
            i = v.memset(epsrm_c[:], 1e-6)
            vc.inc(i)
        v_init = vc.n

        @block.sync
        def _(s):
            ld.inc(s.dma_start(out=masks[:], in_=msk_d[:, :]), 16)
            ld.inc(s.dma_start(out=rb_sb[:], in_=rb_d[:, :]), 16)
            ld.inc(s.dma_start(out=eb_sb[:], in_=eb_d[:, :]), 16)
            ld.inc(s.dma_start(out=ident[:], in_=ident_d[:, :]), 16)
            ld.inc(s.dma_start(out=pb_sb[:], in_=pb_d[:, :]), 16)
            ld.inc(s.dma_start(out=nw_sb[:], in_=nw_d[:, :]), 16)
            ld.inc(s.dma_start(out=sel_sb[:], in_=sel_d[:, :]), 16)
            s.wait_ge(ld.h, ld.n)
        ld_init = ld.n

        @block.gpsimd
        def _(g):
            gd.inc(g.dma_start(out=rw_sb[:].rearrange("p (c e) -> p c e", c=NCH),
                               in_=rw_d.rearrange("(c p) e -> p c e", p=128)), 16)
        gd_init = gd.n

        ar.push()
        cosb = ar.bf16(S)
        sinb = ar.bf16(S)
        ar.push()
        xf32 = [ar.f32(NCH * TCH) for _ in range(2)]
        xbf = [ar.bf16(NCH * TCH) for _ in range(2)]
        xsq = ar.bf16(NCH * TCH)
        xnb = ar.bf16(NCH * TCH)
        mbc = ar.f32(TCH)
        rbc = ar.f32(TCH)

        @block.sync
        def _(s):
            ld.inc(s.dma_start(out=cosb, in_=cos_d[:, :]), 16)
            ld.inc(s.dma_start(out=sinb, in_=sin_d[:, :]), 16)

        ld_x, gd_x, sc_sq, tc_st, vc_rows, sc_rs, tc_bc, vc_cp, vc_xn, ld_sto = \
            {}, {}, {}, {}, {}, {}, {}, {}, {}, {}

        for t in range(NT):
            bi = t % 2
            t0 = t * TCH

            @block.sync
            def _(s, t=t, bi=bi, t0=t0):
                if t >= 2:
                    s.wait_ge(vc.h, vc_xn[t - 2])
                ld.inc(s.dma_start(
                    out=xf32[bi].rearrange("p (c t) -> p c t", c=NCH),
                    in_=x_fm.rearrange("(c p) t -> p c t", p=128)[:, :, t0:t0 + TCH]), 16)
            ld_x[t] = ld.n

            @block.gpsimd
            def _(g, t=t, bi=bi, t0=t0):
                if t >= 2:
                    g.wait_ge(tc.h, tc_st[t - 2])
                gd.inc(g.dma_start(
                    out=xbf[bi].rearrange("p (c t) -> p c t", c=NCH),
                    in_=x_fm.rearrange("(c p) t -> p c t", p=128)[:, :, t0:t0 + TCH]), 16)
            gd_x[t] = gd.n

            @block.scalar
            def _(s, t=t, bi=bi):
                s.wait_ge(gd.h, gd_x[t])
                if t >= 1:
                    s.wait_ge(tc.h, tc_st[t - 1])
                sc.inc(s.activation(out=xsq, in_=xbf[bi], func=AF.Square))
            sc_sq[t] = sc.n

            @block.tensor
            def _(te, t=t, bi=bi):
                te.wait_ge(sc.h, sc_sq[t])
                if t >= 1:
                    te.wait_ge(vc.h, vc_rows[t - 1])
                for c in range(NCH):
                    te.matmul(PE_[0:1, 0:TCH], ones_cb[:, 0:1],
                              xbf[bi][:, c * TCH:(c + 1) * TCH],
                              start=(c == 0), stop=(c == NCH - 1))
                for c in range(NCH):
                    i = te.matmul(PF[0:1, 0:TCH], ones_cb[:, 0:1],
                                  xsq[:, c * TCH:(c + 1) * TCH],
                                  start=(c == 0), stop=(c == NCH - 1))
                tc.inc(i)
            tc_st[t] = tc.n

            @block.vector
            def _(v, t=t):
                v.wait_ge(tc.h, tc_st[t])
                if t >= 1:
                    v.wait_ge(ld.h, ld_sto[t - 1])
                v.tensor_scalar(out=mrow[:], in0=PE_[0:1, 0:TCH], scalar1=1.0 / D,
                                scalar2=None, op0=ALU.mult)
                v.tensor_scalar(out=srow[:], in0=PF[0:1, 0:TCH], scalar1=1.0 / D,
                                scalar2=None, op0=ALU.mult)
                v.drain()
                v.tensor_tensor(out=vrow[:], in0=mrow[:], in1=mrow[:], op=ALU.mult)
                v.drain()
                i = v.tensor_tensor(out=vrow[:], in0=srow[:], in1=vrow[:], op=ALU.subtract)
                vc.inc(i)
            vc_rows[t] = vc.n

            @block.scalar
            def _(s, t=t):
                s.wait_ge(vc.h, vc_rows[t])
                s.activation(out=rrow[:], in_=vrow[:], func=AF.Ln, bias=epsln_c[:])
                s.drain()
                sc.inc(s.activation(out=rrow[:], in_=rrow[:], func=AF.Exp, scale=-0.5))
            sc_rs[t] = sc.n

            @block.tensor
            def _(te, t=t):
                te.wait_ge(sc.h, sc_rs[t])
                if t >= 1:
                    te.wait_ge(vc.h, vc_cp[t - 1])
                te.matmul(PC[:, 0:TCH], ones_rf[0:1, :], mrow[0:1, :], start=True, stop=True)
                i = te.matmul(PD[:, 0:TCH], ones_rf[0:1, :], rrow[0:1, :], start=True, stop=True)
                tc.inc(i)
            tc_bc[t] = tc.n

            @block.vector
            def _(v, t=t, bi=bi):
                v.wait_ge(tc.h, tc_bc[t])
                v.wait_ge(ld.h, ld_x[t])
                if t >= 1:
                    v.wait_ge(ld.h, ld_sto[t - 1])
                v.tensor_copy(out=mbc, in_=PC[:, 0:TCH])
                i = v.tensor_copy(out=rbc, in_=PD[:, 0:TCH])
                vc.inc(i)
                v.drain()
                for c in range(NCH):
                    v.tensor_tensor(out=xf32[bi][:, c * TCH:(c + 1) * TCH],
                                    in0=xf32[bi][:, c * TCH:(c + 1) * TCH], in1=mbc,
                                    op=ALU.subtract)
                v.drain()
                for c in range(NCH):
                    i = v.tensor_tensor(out=xnb[:, c * TCH:(c + 1) * TCH],
                                        in0=xf32[bi][:, c * TCH:(c + 1) * TCH], in1=rbc,
                                        op=ALU.mult)
                vc.inc(i)
            vc_cp[t] = vc.n - 1
            vc_xn[t] = vc.n

            @block.sync
            def _(s, t=t, t0=t0):
                s.wait_ge(vc.h, vc_xn[t])
                ld.inc(s.dma_start(
                    out=xn_d.rearrange("(c p) t -> p c t", p=128)[:, :, t0:t0 + TCH],
                    in_=xnb.rearrange("p (c t) -> p c t", c=NCH)), 16)
                ld.inc(s.dma_start(out=rows_d[0:1, t0:t0 + TCH], in_=mrow[:]), 16)
                ld.inc(s.dma_start(out=rows_d[1:2, t0:t0 + TCH], in_=rrow[:]), 16)
                s.wait_ge(ld.h, ld.n)
            ld_sto[t] = ld.n

        p0_ld = ld.n
        ar.pop()

        ar.push()
        xnp = [ar.bf16(NCH * TCH) for _ in range(2)]
        qraw = ar.bf16(G * S)
        kraw = ar.bf16(G * S)
        qfm = ar.bf16(G * S)
        kfm = ar.bf16(G * S)
        vtm = ar.bf16(G * S)
        xsw = ar.bf16(S)
        t2s = ar.bf16(S)
        wq_sb = ar.bf16(NCH * G * 128)
        wk_sb = ar.bf16(NCH * G * 128)
        wv_sb = ar.bf16(NCH * G * 128)
        pt = [ar.bf16(1024) for _ in range(4)]
        ctxo = [ar.bf16(QM) for _ in range(2)]

        @block.gpsimd
        def _(gg):
            tko = toko(gg)
            gg.wait_ge(ld.h, p0_ld)
            gd.inc(gg.dma_start(out=xno_d[:, :], in_=xn_d[:, bass.ds(tko, OWN)]), 16)
            gd.inc(gg.dma_start(out=rowso_d[:, :], in_=rows_d[:, bass.ds(tko, OWN)]), 16)
            gg.wait_ge(gd.h, gd.n)
        gd_own = gd.n

        gd_w, ld_xn1, tc_proj, vc_evac, vc_rope, ld_sw = {}, {}, {}, {}, {}, {}
        ld_cst = []
        v_ctx_list = []
        grp_end_vc = {}
        fin_idx = 0
        last_pe = None

        for g in range(NGRP):
            o0 = g * G * 128

            @block.gpsimd
            def _(gg, g=g, o0=o0):
                if g >= 1:
                    gg.wait_ge(tc.h, tc_proj[(g - 1, NT - 1)])
                for w_sb, w_dd in ((wq_sb, wqT), (wk_sb, wkT), (wv_sb, wvT)):
                    gd.inc(gg.dma_start(
                        out=w_sb.rearrange("p (c o) -> p c o", c=NCH),
                        in_=w_dd.rearrange("(c p) o -> p c o", p=128)[:, :, o0:o0 + G * 128]), 16)
            gd_w[g] = gd.n

            for t in range(NT):
                bi = t % 2
                t0 = t * TCH

                @block.sync
                def _(s, g=g, t=t, bi=bi, t0=t0):
                    if g == 0 and t == 0:
                        s.wait_ge(ld.h, p0_ld)
                    gi = g * NT + t
                    if gi >= 2:
                        pg, pt_ = divmod(gi - 2, NT)
                        s.wait_ge(tc.h, tc_proj[(pg, pt_)])
                    ld.inc(s.dma_start(
                        out=xnp[bi].rearrange("p (c t) -> p c t", c=NCH),
                        in_=xn_d.rearrange("(c p) t -> p c t", p=128)[:, :, t0:t0 + TCH]), 16)
                ld_xn1[(g, t)] = ld.n

                @block.tensor
                def _(te, g=g, t=t, bi=bi):
                    te.wait_ge(ld.h, ld_xn1[(g, t)])
                    te.wait_ge(gd.h, gd_w[g])
                    if t > 0:
                        te.wait_ge(vc.h, vc_evac[(g, t - 1)])
                    elif g > 0:
                        te.wait_ge(vc.h, grp_end_vc[g - 1])
                    for oc in range(G):
                        for c in range(NCH):
                            te.matmul(PA[:, oc * 512:oc * 512 + TCH],
                                      wq_sb[:, c * G * 128 + oc * 128:c * G * 128 + (oc + 1) * 128],
                                      xnp[bi][:, c * TCH:(c + 1) * TCH],
                                      start=(c == 0), stop=(c == NCH - 1))
                    for oc in range(G):
                        for c in range(NCH):
                            te.matmul(PB[:, oc * 512:oc * 512 + TCH],
                                      wk_sb[:, c * G * 128 + oc * 128:c * G * 128 + (oc + 1) * 128],
                                      xnp[bi][:, c * TCH:(c + 1) * TCH],
                                      start=(c == 0), stop=(c == NCH - 1))
                    for ts in range(4):
                        pp = (PC, PD, PE_, PF)[ts]
                        for c in range(NCH):
                            i = te.matmul(pp[:, 0:G * 128],
                                          xnp[bi][:, c * TCH + ts * 128:c * TCH + (ts + 1) * 128],
                                          wv_sb[:, c * G * 128:(c + 1) * G * 128],
                                          start=(c == 0), stop=(c == NCH - 1))
                    tc.inc(i)
                tc_proj[(g, t)] = tc.n

                @block.vector
                def _(v, g=g, t=t, t0=t0):
                    v.wait_ge(tc.h, tc_proj[(g, t)])
                    for ocp in range(G):
                        v.tensor_copy(out=qraw[:, ocp * S + t0:ocp * S + t0 + TCH],
                                      in_=PA[:, ocp * 512:ocp * 512 + TCH])
                        v.tensor_copy(out=kraw[:, ocp * S + t0:ocp * S + t0 + TCH],
                                      in_=PB[:, ocp * 512:ocp * 512 + TCH])
                    for ts in range(4):
                        pp = (PC, PD, PE_, PF)[ts]
                        cg = t * 4 + ts
                        for h in range(G):
                            i = v.tensor_copy(
                                out=vtm[:, h * S + cg * 128:h * S + (cg + 1) * 128],
                                in_=pp[:, h * 128:(h + 1) * 128])
                    vc.inc(i)
                vc_evac[(g, t)] = vc.n

            for pi, (raw, fm) in enumerate(((qraw, qfm), (kraw, kfm))):
                for h in range(G):
                    @block.sync
                    def _(s, raw=raw, h=h, g=g, pi=pi):
                        s.wait_ge(vc.h, vc_evac[(g, NT - 1)])
                        if (g, pi, h) != (0, 0, 0):
                            s.wait_ge(vc.h, vc.n)
                        ld.inc(s.dma_start(out=xsw[0:64, :],
                                           in_=raw[64:128, h * S:(h + 1) * S]), 16)
                        ld.inc(s.dma_start(out=xsw[64:128, :],
                                           in_=raw[0:64, h * S:(h + 1) * S]), 16)
                        s.wait_ge(ld.h, ld.n)
                    ld_sw[(g, pi, h)] = ld.n

                    @block.vector
                    def _(v, raw=raw, fm=fm, h=h, g=g, pi=pi):
                        v.wait_ge(ld.h, ld_sw[(g, pi, h)])
                        v.tensor_tensor(out=xsw, in0=xsw, in1=sinb, op=ALU.mult)
                        v.tensor_tensor(out=t2s, in0=raw[:, h * S:(h + 1) * S], in1=cosb,
                                        op=ALU.mult)
                        v.drain()
                        i = v.tensor_tensor(out=fm[:, h * S:(h + 1) * S], in0=xsw, in1=t2s,
                                            op=ALU.add)
                        vc.inc(i)
                    vc_rope[(g, pi, h)] = vc.n

            for hg in range(G):
                for qm in range(NQM):
                    q0 = qm * QM
                    nsup = 2 * (qm + 1)
                    diag0 = nsup - 2

                    t_sc, t_pv, s_pe, v_pm = {}, {}, {}, {}
                    tb, sb, vb = tc.n, sc.n, vc.n
                    for ks in range(nsup):
                        t_sc[ks] = tb + (ks + 1 if ks < 2 else 2 * ks)
                        t_pv[ks] = tb + (2 * ks + 3 if ks < nsup - 1 else 2 * nsup)
                        s_pe[ks] = sb + ks + 1
                    v_pm[diag0] = vb + 1
                    v_pm[diag0 + 1] = vb + 2

                    def _scores(te, ks, hg=hg, q0=q0):
                        sst = (PA, PB)[ks % 2]
                        for c2 in range(2):
                            kc = ks * 2 + c2
                            i = te.matmul(sst[:, c2 * 512:c2 * 512 + QM],
                                          kfm[:, hg * S + kc * 128:hg * S + (kc + 1) * 128],
                                          qfm[:, hg * S + q0:hg * S + q0 + QM],
                                          start=True, stop=True)
                        tc.inc(i)

                    def _ctx(te, ks, hg=hg, nsup=nsup):
                        b = ks % 4
                        for c2 in range(2):
                            kc = ks * 2 + c2
                            te.matmul(PC[:, 0:QM],
                                      vtm[:, hg * S + kc * 128:hg * S + (kc + 1) * 128],
                                      pt[b][:, c2 * 512:c2 * 512 + QM],
                                      start=(ks == 0 and c2 == 0),
                                      stop=(ks == nsup - 1 and c2 == 1))
                            i = te.matmul(PD[0:1, 0:QM], ones_cb[:, 0:1],
                                          pt[b][:, c2 * 512:c2 * 512 + QM],
                                          start=(ks == 0 and c2 == 0),
                                          stop=(ks == nsup - 1 and c2 == 1))
                        tc.inc(i)

                    @block.tensor
                    def _(te, g=g, hg=hg, qm=qm, nsup=nsup, q0=q0, diag0=diag0,
                          s_pe=s_pe, v_pm=v_pm, fin_idx=fin_idx, last_pe=last_pe,
                          _scores=_scores, _ctx=_ctx):
                        te.wait_ge(vc.h, vc_rope[(g, 0, hg)])
                        te.wait_ge(vc.h, vc_rope[(g, 1, hg)])
                        if fin_idx > 0:
                            te.wait_ge(vc.h, v_ctx_list[fin_idx - 1])
                        for ks in range(nsup):
                            if ks >= 2:
                                te.wait_ge(sc.h, s_pe[ks - 2])
                            elif last_pe is not None:
                                te.wait_ge(sc.h, last_pe)
                            _scores(te, ks)
                            if ks >= 1:
                                kp = ks - 1
                                if kp >= diag0:
                                    te.wait_ge(vc.h, v_pm[kp])
                                else:
                                    te.wait_ge(sc.h, s_pe[kp])
                                _ctx(te, kp)
                        kp = nsup - 1
                        te.wait_ge(vc.h, v_pm[kp])
                        _ctx(te, kp)
                    assert tc.n == t_pv[nsup - 1], (tc.n, t_pv[nsup - 1])

                    @block.scalar
                    def _(s, nsup=nsup, t_sc=t_sc, t_pv=t_pv):
                        for ks in range(nsup):
                            b = ks % 4
                            sst = (PA, PB)[ks % 2]
                            s.wait_ge(tc.h, t_sc[ks])
                            if ks >= 4:
                                s.wait_ge(tc.h, t_pv[ks - 4])
                            sc.inc(s.activation(out=pt[b][:], in_=sst[:, 0:1024],
                                                func=AF.Exp, scale=ISQD, bias=ebias_c[:]))
                    assert sc.n == s_pe[nsup - 1]
                    last_pe = s_pe[nsup - 1]

                    @block.vector
                    def _(v, nsup=nsup, diag0=diag0, s_pe=s_pe):
                        for ks in (diag0, diag0 + 1):
                            b = ks % 4
                            js = ks - diag0
                            v.wait_ge(sc.h, s_pe[ks])
                            vc.inc(v.tensor_tensor(
                                out=pt[b][:], in0=pt[b][:],
                                in1=masks[:, js * 1024:(js + 1) * 1024], op=ALU.mult))
                    assert vc.n == v_pm[diag0 + 1]

                    @block.vector
                    def _(v, t_pv=t_pv, nsup=nsup):
                        v.wait_ge(tc.h, t_pv[nsup - 1])
                        vc.inc(v.reciprocal(out=linv[:], in_=PD[0:1, 0:QM]))
                    v_linv = vc.n

                    @block.tensor
                    def _(te, v_linv=v_linv):
                        te.wait_ge(vc.h, v_linv)
                        tc.inc(te.matmul(PE_[:, 0:QM], ones_rf[0:1, :], linv[0:1, :],
                                         start=True, stop=True))
                    t_lb = tc.n

                    @block.vector
                    def _(v, t_lb=t_lb, fin_idx=fin_idx):
                        bo = fin_idx % 2
                        v.wait_ge(tc.h, t_lb)
                        if fin_idx >= 2:
                            v.wait_ge(ld.h, ld_cst[fin_idx - 2])
                        v.tensor_copy(out=lbc[:], in_=PE_[:, 0:QM])
                        v.drain()
                        vc.inc(v.tensor_tensor(out=ctxo[bo][:], in0=PC[:, 0:QM],
                                               in1=lbc[:], op=ALU.mult))
                    v_ctx_list.append(vc.n)

                    @block.sync
                    def _(s, g=g, hg=hg, qm=qm, q0=q0, fin_idx=fin_idx):
                        bo = fin_idx % 2
                        s.wait_ge(vc.h, v_ctx_list[fin_idx])
                        hidx = g * G + hg
                        ld.inc(s.dma_start(out=cin[hidx * 128:(hidx + 1) * 128, q0:q0 + QM],
                                           in_=ctxo[bo][:]), 16)
                    ld_cst.append(ld.n)
                    fin_idx += 1

            grp_end_vc[g] = vc.n

        p1_ld = ld.n
        p1_tc = tc.n
        p1_vc = vc.n
        ar.pop()
        ar.pop()

        ar.push()
        xnp3 = [ar.bf16(NCH * TCH) for _ in range(2)]
        comb = ar.bf16(NCH * OWN)
        wep = [ar.bf16(NCH * 1024) for _ in range(2)]
        silu_t = [ar.bf16(TCH) for _ in range(4)]
        tmp_t = [ar.bf16(TCH) for _ in range(4)]


        ld_xn3, tc_lg, tc_tr, vc_fw = {}, {}, {}, {}
        vc_r1, sc_sg = {}, {}
        for t3 in range(4):
            bi = t3 % 2

            @block.sync
            def _(s, t3=t3, bi=bi):
                s.wait_ge(ld.h, p1_ld)
                s.wait_ge(vc.h, p1_vc)
                s.wait_ge(tc.h, p1_tc)
                s.wait_ge(gd.h, gd_own)
                if t3 == 0:
                    s.wait_ge(cc.h, 1)
                if t3 >= 2:
                    s.wait_ge(tc.h, tc_lg[(t3 - 2, 3)])
                ld.inc(s.dma_start(
                    out=xnp3[bi].rearrange("p (c t) -> p c t", c=NCH),
                    in_=xno_d.rearrange("(c p) t -> p c t", p=128)[:, :, t3 * TCH:(t3 + 1) * TCH]), 16)
            ld_xn3[t3] = ld.n

            for ts in range(4):
                ti = t3 * 4 + ts

                @block.tensor
                def _(te, t3=t3, bi=bi, ts=ts, ti=ti):
                    te.wait_ge(ld.h, ld_xn3[t3])
                    te.wait_ge(gd.h, gd_init)
                    if ti == 0:
                        te.wait_ge(ld.h, ld_init)
                    else:
                        te.wait_ge(vc.h, vc_r1[ti - 1])
                    if ti >= 1:
                        te.wait_ge(vc.h, vc_fw[ti - 1])
                    for c in range(NCH):
                        te.matmul(PE_[:, 0:8],
                                  xnp3[bi][:, c * TCH + ts * 128:c * TCH + (ts + 1) * 128],
                                  rw_sb[:, c * E:(c + 1) * E],
                                  start=(c == 0), stop=False)
                    tc.inc(te.matmul(PE_[:, 0:8], ones_rf[0:1, 0:128], rb_sb[0:1, :],
                                     start=False, stop=True))
                tc_lg[(t3, ts)] = tc.n

                @block.vector
                def _(v, t3=t3, ts=ts, ti=ti):
                    v.wait_ge(tc.h, tc_lg[(t3, ts)])
                    v.tensor_copy(out=lgt[:], in_=PE_[:, 0:8])
                    v.drain()
                    v.reduce_max(out=rm1[:], in_=lgt[:], axis=AXX)
                    v.drain()
                    v.tensor_scalar(out=req1[:], in0=lgt[:], scalar1=rm1[:], scalar2=None,
                                    op0=ALU.is_equal)
                    v.drain()
                    v.scalar_tensor_tensor(out=rmsk[:], in0=req1[:], scalar=-1e30, in1=lgt[:],
                                           op0=ALU.mult, op1=ALU.add)
                    v.drain()
                    v.reduce_max(out=rm2[:], in_=rmsk[:], axis=AXX)
                    v.drain()
                    v.tensor_tensor(out=rd12[:], in0=rm1[:], in1=rm2[:], op=ALU.subtract)
                    vc.inc(v.tensor_scalar(out=req2[:], in0=lgt[:], scalar1=rm2[:],
                                           scalar2=None, op0=ALU.is_equal))
                vc_r1[ti] = vc.n

                @block.scalar
                def _(s, ti=ti):
                    s.wait_ge(vc.h, vc_r1[ti])
                    sc.inc(s.activation(out=rw1[:], in_=rd12[:], func=AF.Sigmoid))
                sc_sg[ti] = sc.n

                @block.vector
                def _(v, ti=ti):
                    v.wait_ge(sc.h, sc_sg[ti])
                    v.tensor_scalar(out=rw2[:], in0=rw1[:], scalar1=-1.0, scalar2=1.0,
                                    op0=ALU.mult, op1=ALU.add)
                    v.tensor_scalar(out=rfw1[:], in0=req1[:], scalar1=rw1[:], scalar2=None,
                                    op0=ALU.mult)
                    v.drain()
                    vc.inc(v.scalar_tensor_tensor(out=rfw[:], in0=req2[:], scalar=rw2[:],
                                                  in1=rfw1[:], op0=ALU.mult, op1=ALU.add))
                v_fw2 = vc.n

                @block.tensor
                def _(te, v_fw2=v_fw2):
                    te.wait_ge(vc.h, v_fw2)
                    tc.inc(te.transpose(PF[0:8, 0:128], rfw[:, 0:8], ident[:]))
                tc_tr[ti] = tc.n

                @block.vector
                def _(v, ti=ti):
                    v.wait_ge(tc.h, tc_tr[ti])
                    vc.inc(v.tensor_copy(out=wrow_sb[:, ti * 128:(ti + 1) * 128],
                                         in_=PF[0:8, 0:128]))
                vc_fw[ti] = vc.n

        NEP = 2
        gd_wep = {}
        tc_hh, sc_hh, vc_cmb, vc_cmb_idx = {}, {}, {}, {}

        @block.gpsimd
        def _(gg):
            gg.wait_ge(tc.h, p1_tc)
            gg.wait_ge(vc.h, p1_vc)
            gd.inc(gg.dma_start(
                out=wep[0].rearrange("p (c f) -> p c f", c=NCH),
                in_=ew_d[0].rearrange("(c p) f -> p c f", p=128)[:, :, 0:1024]), 16)
        gd_wep[(0, 0)] = gd.n

        @block.gpsimd
        def _(gg):
            tko = toko(gg)
            gg.wait_ge(ld.h, p1_ld)
            gg.wait_ge(tc.h, p1_tc)
            gg.wait_ge(vc.h, p1_vc)
            gg.wait_ge(gd.h, gd_wep[(0, 0)])
            cc.inc(gg.collective_compute("AllGather", ALU.bypass, replica_groups=RG,
                                         ins=[cin_t.ap().opt()], outs=[cout_t.ap().opt()]))
            gg.wait_ge(cc.h, cc.n)
            gd.inc(gg.dma_start(out=ctxo_d[:, :], in_=cout[:, bass.ds(tko, OWN)]), 16)
            gg.wait_ge(gd.h, gd.n)
        gd_ctxo = gd.n

        tc_bc3 = {}
        for e in range(E):
            @block.tensor
            def _(te, e=e):
                for t3 in range(4):
                    slot = (PC, PD, PE_, PF)[t3]
                    if e == 0:
                        te.wait_ge(vc.h, vc_fw[15])
                    else:
                        te.wait_ge(vc.h, vc_cmb[(e - 1, NEP - 1, t3, 7)])
                    tc.inc(te.matmul(slot[:, 0:TCH], sel_sb[0:8, e * 128:(e + 1) * 128],
                                     wrow_sb[0:8, t3 * TCH:(t3 + 1) * TCH],
                                     start=True, stop=True))
                    tc_bc3[(e, t3)] = tc.n
            for p in range(NEP):
                ep = e * NEP + p
                if ep > 0:
                    @block.gpsimd
                    def _(gg, e=e, p=p, ep=ep):
                        if ep >= 2:
                            e2, p2 = divmod(ep - 2, NEP)
                            gg.wait_ge(tc.h, tc_hh[(e2, p2, 3, 7)])
                        gd.inc(gg.dma_start(
                            out=wep[ep % 2].rearrange("p (c f) -> p c f", c=NCH),
                            in_=ew_d[e].rearrange("(c p) f -> p c f", p=128)[:, :, p * 1024:(p + 1) * 1024]), 16)
                    gd_wep[(e, p)] = gd.n
                else:
                    pass

                for t3 in range(4):
                    bi3 = (ep * 4 + t3) % 2

                    @block.sync
                    def _(s, e=e, p=p, t3=t3, bi3=bi3, ep=ep):
                        gi = ep * 4 + t3
                        if gi == 0:
                            s.wait_ge(tc.h, tc_lg[(3, 3)])
                        if gi >= 2:
                            g2 = gi - 2
                            e2, r2 = divmod(g2, NEP * 4)
                            p2, t2 = divmod(r2, 4)
                            s.wait_ge(tc.h, tc_hh[(e2, p2, t2, 7)])
                        ld.inc(s.dma_start(
                            out=xnp3[bi3].rearrange("p (c t) -> p c t", c=NCH),
                            in_=xno_d.rearrange("(c p) t -> p c t", p=128)[:, :, t3 * TCH:(t3 + 1) * TCH]), 16)
                    ld_xn3[(e, p, t3)] = ld.n

                    for fj in range(8):
                        idx = (ep * 4 + t3) * 8 + fj

                        @block.tensor
                        def _(te, e=e, p=p, t3=t3, fj=fj, bi3=bi3, idx=idx):
                            te.wait_ge(ld.h, ld_xn3[(e, p, t3)])
                            te.wait_ge(gd.h, gd_wep[(e, p)])
                            if idx >= 4:
                                te.wait_ge(sc.h, sc_hh[idx - 4])
                            pp = (PA[:, 0:512], PA[:, 512:1024],
                                  PB[:, 0:512], PB[:, 512:1024])[fj % 4]
                            for c in range(NCH):
                                i = te.matmul(pp,
                                              wep[p % 2][:, c * 1024 + fj * 128:c * 1024 + (fj + 1) * 128],
                                              xnp3[bi3][:, c * TCH:(c + 1) * TCH],
                                              start=(c == 0), stop=(c == NCH - 1))
                            tc.inc(i)
                        tc_hh[(e, p, t3, fj)] = tc.n

                        @block.scalar
                        def _(s, e=e, p=p, t3=t3, fj=fj, idx=idx):
                            s.wait_ge(tc.h, tc_hh[(e, p, t3, fj)])
                            if idx >= 4:
                                s.wait_ge(vc.h, vc_cmb_idx[idx - 4])
                            pp = (PA[:, 0:512], PA[:, 512:1024],
                                  PB[:, 0:512], PB[:, 512:1024])[fj % 4]
                            cf = p * 8 + fj
                            sc.inc(s.activation(out=silu_t[fj % 4][:], in_=pp, func=AF.Silu,
                                                bias=eb_sb[:, e * NCH + cf:e * NCH + cf + 1]))
                        sc_hh[idx] = sc.n

                        if fj % 4 == 3:
                            @block.vector
                            def _(v, e=e, p=p, t3=t3, fj=fj, idx=idx):
                                v.wait_ge(sc.h, sc_hh[idx])
                                v.wait_ge(tc.h, tc_bc3[(e, t3)])
                                wslice = (PC, PD, PE_, PF)[t3][:, 0:TCH]
                                for q in range(4):
                                    fjq = fj - 3 + q
                                    cf = p * 8 + fjq
                                    dst = comb[:, cf * OWN + t3 * TCH:cf * OWN + (t3 + 1) * TCH]
                                    if e == 0:
                                        i = v.tensor_tensor(out=dst, in0=silu_t[q][:],
                                                            in1=wslice, op=ALU.mult)
                                    else:
                                        v.tensor_tensor(out=tmp_t[q][:], in0=silu_t[q][:],
                                                        in1=wslice, op=ALU.mult)
                                if e > 0:
                                    v.drain()
                                    for q in range(4):
                                        fjq = fj - 3 + q
                                        cf = p * 8 + fjq
                                        dst = comb[:, cf * OWN + t3 * TCH:cf * OWN + (t3 + 1) * TCH]
                                        i = v.tensor_tensor(out=dst, in0=dst, in1=tmp_t[q][:],
                                                            op=ALU.add)
                                vc.inc(i)
                            for q in range(4):
                                vc_cmb_idx[idx - 3 + q] = vc.n
                            vc_cmb[(e, p, t3, fj)] = vc.n

        @block.sync
        def _(s):
            s.wait_ge(vc.h, vc_cmb[(E - 1, NEP - 1, 3, 7)])
            for q in range(4):
                ld.inc(s.dma_start(
                    out=comb_d.rearrange("(c p) t -> p c t", p=128)[:, :, q * TCH:(q + 1) * TCH],
                    in_=comb.rearrange("p (c t) -> p c t", c=NCH)[:, :, q * TCH:(q + 1) * TCH]), 16)
        p3_ld = ld.n
        p3_tc = tc.n
        p3_vc = vc.n
        p3_sc = sc.n
        ar.pop()

        ar.push()
        T4 = 256
        wob = [ar.bf16(NCH * 512) for _ in range(2)]
        pwb = [ar.bf16(NCH * 512) for _ in range(2)]
        cmb4 = [ar.bf16(NCH * T4) for _ in range(2)]
        ctx4 = [ar.bf16(NCH * T4) for _ in range(2)]
        y1s = [ar.bf16(4 * T4) for _ in range(2)]
        ats = [ar.bf16(4 * T4) for _ in range(2)]

        gd_pan, ld_c4, sc_y1, vc_at, ld_s4, tc_w4 = {}, {}, {}, {}, {}, {}

        for op in range(4):
            @block.gpsimd
            def _(gg, op=op):
                if op == 0:
                    gg.wait_ge(tc.h, p3_tc)
                    gg.wait_ge(sc.h, p3_sc)
                    gg.wait_ge(vc.h, p3_vc)
                    gg.wait_ge(ld.h, p3_ld)
                if op >= 2:
                    gg.wait_ge(tc.h, tc_w4[(op - 2, 7, 3)])
                gd.inc(gg.dma_start(
                    out=wob[op % 2].rearrange("p (c o) -> p c o", c=NCH),
                    in_=woT.rearrange("(c p) o -> p c o", p=128)[:, :, op * 512:(op + 1) * 512]), 16)
                gd.inc(gg.dma_start(
                    out=pwb[op % 2].rearrange("p (c o) -> p c o", c=NCH),
                    in_=pw_d.rearrange("(c p) o -> p c o", p=128)[:, :, op * 512:(op + 1) * 512]), 16)
            gd_pan[op] = gd.n

            for tq in range(8):
                bi4 = tq % 2
                gi4 = op * 8 + tq

                @block.sync
                def _(s, op=op, tq=tq, bi4=bi4, gi4=gi4):
                    if op == 0 and tq == 0:
                        s.wait_ge(ld.h, p3_ld)
                        s.wait_ge(gd.h, gd_ctxo)
                    if gi4 >= 2:
                        o2, t2 = divmod(gi4 - 2, 8)
                        s.wait_ge(tc.h, tc_w4[(o2, t2, 3)])
                    ld.inc(s.dma_start(
                        out=cmb4[bi4].rearrange("p (c t) -> p c t", c=NCH),
                        in_=comb_d.rearrange("(c p) t -> p c t", p=128)[:, :, tq * T4:(tq + 1) * T4]), 16)
                    ld.inc(s.dma_start(
                        out=ctx4[bi4].rearrange("p (c t) -> p c t", c=NCH),
                        in_=ctxo_d.rearrange("(c p) t -> p c t", p=128)[:, :, tq * T4:(tq + 1) * T4]), 16)
                ld_c4[(op, tq)] = ld.n

                for ocs in range(4):
                    @block.tensor
                    def _(te, op=op, tq=tq, ocs=ocs, bi4=bi4, gi4=gi4):
                        te.wait_ge(ld.h, ld_c4[(op, tq)])
                        te.wait_ge(gd.h, gd_pan[op])
                        idx4 = gi4 * 4 + ocs
                        if idx4 >= 2:
                            te.wait_ge(sc.h, sc_y1[idx4 - 2])
                            te.wait_ge(vc.h, vc_at[idx4 - 2])
                        pj = (PC, PE_)[ocs % 2]
                        pw_ = (PD, PF)[ocs % 2]
                        for c in range(NCH):
                            te.matmul(pj[:, 0:T4],
                                      pwb[op % 2][:, c * 512 + ocs * 128:c * 512 + (ocs + 1) * 128],
                                      cmb4[bi4][:, c * T4:(c + 1) * T4],
                                      start=(c == 0), stop=(c == NCH - 1))
                        for c in range(NCH):
                            i = te.matmul(pw_[:, 0:T4],
                                          wob[op % 2][:, c * 512 + ocs * 128:c * 512 + (ocs + 1) * 128],
                                          ctx4[bi4][:, c * T4:(c + 1) * T4],
                                          start=(c == 0), stop=(c == NCH - 1))
                        tc.inc(i)
                    tc_w4[(op, tq, ocs)] = tc.n

                    @block.scalar
                    def _(s, op=op, tq=tq, ocs=ocs, bi4=bi4):
                        s.wait_ge(tc.h, tc_w4[(op, tq, ocs)])
                        gi4 = op * 8 + tq
                        idx4 = gi4 * 4 + ocs
                        if idx4 >= 8:
                            s.wait_ge(gd.h, ld_s4[idx4 // 4 - 2])
                        pj = (PC, PE_)[ocs % 2]
                        oc_g = op * 4 + ocs
                        sc.inc(s.add(out=y1s[bi4][:, ocs * T4:(ocs + 1) * T4], in_=pj[:, 0:T4],
                                     add=pb_sb[:, oc_g:oc_g + 1]))
                    sc_y1[(op * 8 + tq) * 4 + ocs] = sc.n

                    @block.vector
                    def _(v, op=op, tq=tq, ocs=ocs, bi4=bi4):
                        v.wait_ge(tc.h, tc_w4[(op, tq, ocs)])
                        idx4 = (op * 8 + tq) * 4 + ocs
                        if idx4 >= 8:
                            v.wait_ge(gd.h, ld_s4[idx4 // 4 - 2])
                        pw_ = (PD, PF)[ocs % 2]
                        vc.inc(v.tensor_copy(out=ats[bi4][:, ocs * T4:(ocs + 1) * T4],
                                             in_=pw_[:, 0:T4]))
                    vc_at[(op * 8 + tq) * 4 + ocs] = vc.n

                @block.gpsimd
                def _(gg, op=op, tq=tq, bi4=bi4, gi4=gi4):
                    gg.wait_ge(sc.h, sc_y1[gi4 * 4 + 3])
                    gg.wait_ge(vc.h, vc_at[gi4 * 4 + 3])
                    gd.inc(gg.dma_start(
                        out=y1_d.rearrange("(o p) t -> p o t", p=128)[:, op * 4:(op + 1) * 4, tq * T4:(tq + 1) * T4],
                        in_=y1s[bi4].rearrange("p (o t) -> p o t", o=4)), 16)
                    gd.inc(gg.dma_start(
                        out=attn_d.rearrange("(o p) t -> p o t", p=128)[:, op * 4:(op + 1) * 4, tq * T4:(tq + 1) * T4],
                        in_=ats[bi4].rearrange("p (o t) -> p o t", o=4)), 16)
                ld_s4[gi4] = gd.n

        p4a_ld = ld.n
        p4a_gd = gd.n
        ar.pop()

        ar.push()
        xo4 = ar.f32(NCH * TCH)
        y14 = ar.bf16(NCH * TCH)
        at4 = ar.bf16(NCH * TCH)
        yv = ar.f32(NCH * TCH)
        fin = ar.f32(NCH * TCH)
        sq4 = ar.bf16(TCH)
        m4 = ar.f32(TCH)
        r4 = ar.f32(TCH)
        mr4 = ar.f32(TCH)[0:1, :]
        rr4 = ar.f32(TCH)[0:1, :]
        ss4 = ar.f32(TCH)[0:1, :]
        rb4 = ar.f32(TCH)

        ld_b4, tc_b4, vc_b4, sc_b4, tc_ss, sc_rr, tc_rb, vc_fin = {}, {}, {}, {}, {}, {}, {}, {}

        for tq in range(4):
            @block.sync
            def _(s, tq=tq):
                if tq == 0:
                    s.wait_ge(ld.h, p4a_ld)
                    s.wait_ge(gd.h, p4a_gd)
                if tq >= 1:
                    s.wait_ge(oc.h, 16 * tq)
                ld.inc(s.dma_start(
                    out=xo4.rearrange("p (c t) -> p c t", c=NCH),
                    in_=x_own.rearrange("(c p) t -> p c t", p=128)[:, :, tq * TCH:(tq + 1) * TCH]), 16)
                ld.inc(s.dma_start(
                    out=y14.rearrange("p (c t) -> p c t", c=NCH),
                    in_=y1_d.rearrange("(c p) t -> p c t", p=128)[:, :, tq * TCH:(tq + 1) * TCH]), 16)
                ld.inc(s.dma_start(
                    out=at4.rearrange("p (c t) -> p c t", c=NCH),
                    in_=attn_d.rearrange("(c p) t -> p c t", p=128)[:, :, tq * TCH:(tq + 1) * TCH]), 16)
                ld.inc(s.dma_start(out=mr4[:], in_=rowso_d[0:1, tq * TCH:(tq + 1) * TCH]), 16)
                ld.inc(s.dma_start(out=rr4[:], in_=rowso_d[1:2, tq * TCH:(tq + 1) * TCH]), 16)
                s.wait_ge(ld.h, ld.n)
            ld_b4[tq] = ld.n

            @block.tensor
            def _(te, tq=tq):
                te.wait_ge(ld.h, ld_b4[tq])
                if tq >= 1:
                    te.wait_ge(vc.h, vc_b4[tq - 1])
                te.matmul(PC[:, 0:TCH], ones_rf[0:1, :], mr4[0:1, :], start=True, stop=True)
                tc.inc(te.matmul(PD[:, 0:TCH], ones_rf[0:1, :], rr4[0:1, :], start=True, stop=True))
            tc_b4[tq] = tc.n

            @block.vector
            def _(v, tq=tq):
                v.wait_ge(tc.h, tc_b4[tq])
                v.tensor_copy(out=m4, in_=PC[:, 0:TCH])
                v.tensor_copy(out=r4, in_=PD[:, 0:TCH])
                v.drain()
                for c in range(NCH):
                    v.tensor_tensor(out=yv[:, c * TCH:(c + 1) * TCH],
                                    in0=xo4[:, c * TCH:(c + 1) * TCH], in1=m4, op=ALU.subtract)
                v.drain()
                for c in range(NCH):
                    v.tensor_tensor(out=yv[:, c * TCH:(c + 1) * TCH],
                                    in0=yv[:, c * TCH:(c + 1) * TCH], in1=r4, op=ALU.mult)
                v.drain()
                for c in range(NCH):
                    i = v.tensor_tensor(out=yv[:, c * TCH:(c + 1) * TCH],
                                        in0=yv[:, c * TCH:(c + 1) * TCH],
                                        in1=y14[:, c * TCH:(c + 1) * TCH], op=ALU.add)
                vc.inc(i)
            vc_b4[tq] = vc.n

            for c in range(NCH):
                @block.scalar
                def _(s, tq=tq, c=c):
                    if c == 0:
                        s.wait_ge(vc.h, vc_b4[tq])
                    if c >= 1:
                        s.wait_ge(tc.h, tc_ss.get((tq, c - 1), 0))
                    sc.inc(s.activation(out=sq4, in_=yv[:, c * TCH:(c + 1) * TCH],
                                        func=AF.Square))
                sc_b4[(tq, c)] = sc.n

                @block.tensor
                def _(te, tq=tq, c=c):
                    te.wait_ge(sc.h, sc_b4[(tq, c)])
                    if tq >= 1 and c == 0:
                        te.wait_ge(sc.h, sc_rr[tq - 1])
                    tc.inc(te.matmul(PE_[0:1, 0:TCH], ones_cb[:, 0:1], sq4,
                                     start=(c == 0), stop=(c == NCH - 1)))
                tc_ss[(tq, c)] = tc.n

            @block.scalar
            def _(s, tq=tq):
                s.wait_ge(tc.h, tc_ss[(tq, NCH - 1)])
                s.activation(out=ss4[:], in_=PE_[0:1, 0:TCH], func=AF.Ln, scale=1.0 / D,
                             bias=epsrm_c[:])
                s.drain()
                sc.inc(s.activation(out=ss4[:], in_=ss4[:], func=AF.Exp, scale=-0.5))
            sc_rr[tq] = sc.n

            @block.tensor
            def _(te, tq=tq):
                te.wait_ge(sc.h, sc_rr[tq])
                if tq >= 1:
                    te.wait_ge(vc.h, vc_fin[tq - 1])
                tc.inc(te.matmul(PF[:, 0:TCH], ones_rf[0:1, :], ss4[0:1, :],
                                 start=True, stop=True))
            tc_rb[tq] = tc.n

            @block.vector
            def _(v, tq=tq):
                v.wait_ge(tc.h, tc_rb[tq])
                v.tensor_copy(out=rb4, in_=PF[:, 0:TCH])
                v.drain()
                for c in range(NCH):
                    v.scalar_tensor_tensor(out=yv[:, c * TCH:(c + 1) * TCH],
                                           in0=yv[:, c * TCH:(c + 1) * TCH],
                                           scalar=nw_sb[:, c:c + 1], in1=rb4,
                                           op0=ALU.mult, op1=ALU.mult)
                for c in range(NCH):
                    v.tensor_tensor(out=fin[:, c * TCH:(c + 1) * TCH],
                                    in0=xo4[:, c * TCH:(c + 1) * TCH],
                                    in1=at4[:, c * TCH:(c + 1) * TCH], op=ALU.add)
                v.drain()
                for c in range(NCH):
                    i = v.tensor_tensor(out=fin[:, c * TCH:(c + 1) * TCH],
                                        in0=fin[:, c * TCH:(c + 1) * TCH],
                                        in1=yv[:, c * TCH:(c + 1) * TCH], op=ALU.add)
                vc.inc(i)
            vc_fin[tq] = vc.n

            @block.sync
            def _(s, tq=tq):
                s.wait_ge(vc.h, vc_fin[tq])
                oc.inc(s.dma_start(
                    out=out_d.rearrange("(c p) t -> p c t", p=128)[:, :, tq * TCH:(tq + 1) * TCH],
                    in_=fin.rearrange("p (c t) -> p c t", c=NCH)), 16)

        @block.sync
        def _(s):
            s.wait_ge(oc.h, oc.n)
        ar.pop()

    return nc


_NC_CACHE = {}


def _get_nc():
    if "nc" not in _NC_CACHE:
        _NC_CACHE["nc"] = build_nc()
    return _NC_CACHE["nc"]


def _rope_tables():
    half = 64
    inv_freq = 1.0 / ROPE_BASE ** (np.arange(0, 128, 2, dtype=np.float64) / 128.0)
    t = np.arange(S, dtype=np.float64)
    freqs = np.outer(t, inv_freq)
    cos = np.cos(freqs).astype(np.float32).T
    sin = np.sin(freqs).astype(np.float32).T
    cos_dup = np.concatenate([cos, cos], axis=0).astype(BFD)
    sin_sgn = np.concatenate([-sin, sin], axis=0).astype(BFD)
    return cos_dup, sin_sgn


def _diag_masks():
    k = np.arange(128)[:, None]
    q = np.arange(QM)[None, :]
    ms = [(q >= (128 * j + k)).astype(np.float32) for j in range(4)]
    return np.concatenate(ms, axis=1).astype(BFD)


def kernel(L0_attn_hidden_states, L0_attn_Wq, L0_attn_Wk, L0_attn_Wv, L0_attn_Wo,
           L0_ffn_router_w, L0_ffn_router_b, L0_ffn_expert_w, L0_ffn_expert_b,
           L0_ffn_proj_w, L0_ffn_proj_b, L0_ffn_norm_w):
    x = np.asarray(L0_attn_hidden_states, dtype=np.float32)
    cos_dup, sin_sgn = _rope_tables()
    dmask = _diag_masks()
    ident = np.eye(128, dtype=np.float32)
    sel = np.zeros((8, E * 128), dtype=BFD)
    for e in range(E):
        sel[e, e * 128:(e + 1) * 128] = 1

    wq = np.ascontiguousarray(np.asarray(L0_attn_Wq, np.float32).T)
    wk = np.ascontiguousarray(np.asarray(L0_attn_Wk, np.float32).T)
    wv = np.ascontiguousarray(np.asarray(L0_attn_Wv, np.float32).T)
    wo = np.ascontiguousarray(np.asarray(L0_attn_Wo, np.float32).T)
    rw = np.ascontiguousarray(np.asarray(L0_ffn_router_w, np.float32))
    rb = np.ascontiguousarray(np.asarray(L0_ffn_router_b, np.float32)).reshape(1, E)
    ew = np.ascontiguousarray(np.asarray(L0_ffn_expert_w, np.float32))
    eb = np.ascontiguousarray(
        np.asarray(L0_ffn_expert_b, np.float32).reshape(E, 16, 128).transpose(2, 0, 1).reshape(128, E * 16))
    pw = np.ascontiguousarray(np.asarray(L0_ffn_proj_w, np.float32))
    pb = np.ascontiguousarray(np.asarray(L0_ffn_proj_b, np.float32).reshape(16, 128).T)
    nw = np.ascontiguousarray(np.asarray(L0_ffn_norm_w, np.float32).reshape(16, 128).T)

    in_maps = []
    for c in range(8):
        b = c // 2
        hh = c % 2
        x_fm = np.ascontiguousarray(x[b].T)
        x_ownc = np.ascontiguousarray(x_fm[:, hh * OWN:(hh + 1) * OWN])
        in_maps.append({
            "x_fm": x_fm,
            "x_own": x_ownc,
            "wqT": np.ascontiguousarray(wq[:, hh * 1024:(hh + 1) * 1024]),
            "wkT": np.ascontiguousarray(wk[:, hh * 1024:(hh + 1) * 1024]),
            "wvT": np.ascontiguousarray(wv[:, hh * 1024:(hh + 1) * 1024]),
            "woT": wo,
            "rw_d": rw, "rb_d": rb, "ew_d": ew, "eb_d": eb,
            "pw_d": pw, "pb_d": pb, "nw_d": nw,
            "cos_d": cos_dup, "sin_d": sin_sgn, "msk_d": dmask,
            "ident_d": ident,
            "sel_d": sel,
        })

    nc = _get_nc()
    res = run_bass_kernel_spmd(nc, in_maps, core_ids=list(range(8)))

    out = np.empty((B, S, D), dtype=np.float32)
    for c in range(8):
        b = c // 2
        hh = c % 2
        out[b, hh * OWN:(hh + 1) * OWN, :] = res.results[c]["out_d"].T
    return out
